# version 11
# baseline (speedup 1.0000x reference)
import contextlib
import numpy as np
import ml_dtypes
import concourse.bass as bass
import concourse.mybir as mybir
from concourse.bass_utils import run_bass_kernel_spmd
from concourse.alu_op_type import AluOpType as OP

F32 = mybir.dt.float32
F32R = mybir.dt.float32r
BF16 = mybir.dt.bfloat16
U8 = mybir.dt.uint8
AF = mybir.ActivationFunctionType
AX = mybir.AxisListType

DEBUG_L1 = False
D = 1024
DC = 8
F1 = 4096
FC = 32
H = 16
NP = 8
TT = 512
NCORES = 8

ENGINES = ("sync", "gpsimd", "tensor", "vector", "scalar")
LANES = {"da": "sync", "db": "sync", "dc": "sync", "dd": "sync",
         "ga": "gpsimd", "gb": "gpsimd", "gc": "gpsimd", "aq": "scalar"}


class E:
    def __init__(self, nc, cur, handle, sems, cnt):
        self.nc = nc
        self.cur = cur
        self.h = handle
        self.sems = sems
        self.c = cnt

    def wait(self, who, sem, val):
        if self.cur == who and val > 0:
            self.h.wait_ge(self.sems[sem], val)

    def _emit(self, who, sem, meth, *a, **kw):
        self.c[sem] += 1
        if self.cur == who:
            getattr(self.h, meth)(*a, **kw).then_inc(self.sems[sem], 1)
        return self.c[sem]

    def mm(self, out, lhsT, rhs, **kw):
        return self._emit("tensor", "t", "matmul", out, lhsT, rhs, **kw)

    def v(self, meth, *a, **kw):
        return self._emit("vector", "v", meth, *a, **kw)

    def s(self, meth, *a, **kw):
        return self._emit("scalar", "s", meth, *a, **kw)

    def dma(self, lane, out, in_, nowait=False, **kw):
        prev = self.c[lane]
        self.c[lane] += 16
        if self.cur == LANES[lane]:
            if prev > 0 and not nowait:
                self.h.wait_ge(self.sems[lane], prev)
            self.h.dma_start(out, in_, **kw).then_inc(self.sems[lane], 16)
        return self.c[lane]


def drain_lanes(E_):
    for lane, eng in LANES.items():
        E_.wait(eng, lane, E_.c[lane])


def run_phase(nc, sems, cnt, fn):
    finals = []
    snap = dict(cnt)
    with nc.Block() as block:
        for name in ENGINES:
            def mk(n):
                def f(h):
                    local = dict(snap)
                    fn(E(nc, n, h, sems, local))
                    drain_lanes(E(nc, n, h, sems, local))
                    finals.append(local)
                return f
            getattr(block, name)(mk(name))
    for f in finals[1:]:
        assert f == finals[0], f"engine passes disagree: {finals[0]} vs {f}"
    cnt.clear()
    cnt.update(finals[0])


def r32(ap):
    return ap.bitcast(F32R)


def x_tile_ap(x_dram, T, tt):
    return bass.AP(x_dram.tensor, x_dram.offset + tt * TT,
                   [[T, 128], [128 * T, DC], [1, TT]])


def bias_ap(b_dram, c):
    return bass.AP(b_dram.tensor, b_dram.offset + c * 128, [[1, 128], [1, 1]])


def chunk_ap(w_dram, c, nfree, w=None):
    return bass.AP(w_dram.tensor, w_dram.offset + c * 128 * nfree,
                   [[nfree, 128], [1, nfree if w is None else w]])


def flat_ap(t_dram, n):
    return bass.AP(t_dram.tensor, t_dram.offset, [[1, n]])


class Res:

    def __init__(self, nc, ctx, tag):
        sb = lambda n, s, d: ctx.enter_context(nc.sbuf_tensor(tag + n, s, d))
        ps = lambda n, w: ctx.enter_context(nc.psum_tensor(tag + n, [128, w], F32))
        self.PM0 = ps("_PM0", 512)
        self.PM1 = ps("_PM1", 512)
        self.PSa = ps("_PSa", 1024)
        self.PSb = ps("_PSb", 1024)
        self.PU4 = ps("_PU4", 512)
        self.PU5 = ps("_PU5", 512)
        self.y = sb("_y", [128, DC * TT], F32)
        self.xsq = [sb("_xsq0", [128, TT], BF16), sb("_xsq1", [128, TT], BF16)]
        self.xb = [sb("_xb0", [128, TT], BF16), sb("_xb1", [128, TT], BF16)]
        self.th = sb("_th", [128, TT], F32)
        self.ones128 = sb("_o128", [128, 128], BF16)
        self.ones128f = sb("_o128f", [128, 128], F32)
        self.ones1 = sb("_o1", [33, 128], F32)
        self.onesc = sb("_oc", [128, 1], BF16)
        self.eps11 = sb("_eps", [1, 1], F32)
        self.rmu = sb("_rmu", [1, TT], F32)
        self.rt1 = sb("_rt1", [1, TT], F32)
        self.rt2 = sb("_rt2", [1, TT], F32)

    def init_consts(self, E_):
        E_.v("memset", self.ones128[:], 1.0)
        E_.v("memset", self.ones128f[:], 1.0)
        E_.v("memset", self.ones1[:], 1.0)
        E_.v("memset", self.onesc[:], 1.0)
        return E_.v("memset", self.eps11[:], 1e-5)

    def ln_rows(self, E_, src_sb, PA, PB, pa_t_free, pa_v_free):
        E_.wait("tensor", "v", pa_v_free)
        E_.wait("tensor", "t", pa_t_free)
        xg = [E_.c["t"], E_.c["t"]]
        for c in range(DC):
            sl = src_sb[:, c * TT:(c + 1) * TT]
            xq = self.xsq[c % 2]
            xb = self.xb[c % 2]
            E_.wait("vector", "t", xg[c % 2])
            vcb = E_.v("tensor_copy", xb[:], sl)
            E_.wait("scalar", "t", xg[c % 2])
            ssq = E_.s("activation", xq[:], sl, AF.Square)
            E_.wait("tensor", "v", vcb)
            E_.mm(PA[:, 0:TT], self.ones128[:], xb[:],
                  start=(c == 0), stop=(c == DC - 1))
            E_.wait("tensor", "s", ssq)
            tmm = E_.mm(PB[:, 0:TT], self.ones128[:], xq[:],
                        start=(c == 0), stop=(c == DC - 1))
            xg[c % 2] = tmm
        nfeat = float(D)
        E_.wait("vector", "t", tmm)
        v1 = E_.v("tensor_scalar", self.rmu[:], PA[0:1, 0:TT], 1.0 / nfeat,
                  None, OP.mult, OP.bypass)
        E_.wait("vector", "v", v1)
        v2 = E_.v("tensor_tensor", self.rt1[:], self.rmu[:], self.rmu[:], OP.mult)
        E_.wait("vector", "v", v2)
        v3 = E_.v("scalar_tensor_tensor", self.rt2[:], PB[0:1, 0:TT],
                  1.0 / nfeat, self.rt1[:], OP.mult, OP.subtract)
        E_.wait("scalar", "v", v3)
        s1 = E_.s("activation", self.rt1[:], self.rt2[:], AF.Ln,
                  bias=self.eps11[:], scale=1.0)
        E_.wait("scalar", "s", s1)
        s2 = E_.s("activation", self.rt2[:], self.rt1[:], AF.Exp,
                  scale=-0.5)
        E_.wait("vector", "s", s2)
        v5 = E_.v("scalar_tensor_tensor", self.rt1[:], self.rmu[:], -1.0,
                  self.rt2[:], OP.mult, OP.mult)
        E_.wait("tensor", "v", v5)
        E_.mm(PA[:, 0:TT], self.ones1[0:1, :], self.rt2[:],
              start=True, stop=True)
        tb = E_.mm(PB[:, 0:TT], self.ones1[0:1, :], self.rt1[:],
                   start=True, stop=True)
        return tb

    def ln_apply(self, E_, src_sb, dst_sb, PA, PB, t_bcast):
        E_.wait("vector", "t", t_bcast)
        for c in range(DC):
            sl = src_sb[:, c * TT:(c + 1) * TT]
            dl = dst_sb[:, c * TT:(c + 1) * TT]
            va = E_.v("tensor_tensor", self.th[:], sl, PA[:, 0:TT], OP.mult)
            E_.wait("vector", "v", va)
            last = E_.v("tensor_tensor", dl, self.th[:], PB[:, 0:TT], OP.add)
            E_.wait("vector", "v", last)
        return last

    def finish_tile(self, E_, xout, T, tt, last_s, PA, PB, t_free, v_free):
        E_.wait("vector", "s", last_s)
        E_.wait("scalar", "s", last_s)
        E_.wait("tensor", "s", last_s)
        tb = self.ln_rows(E_, self.y, PA, PB, max(t_free, E_.c["t"]),
                          max(v_free, E_.c["v"]))
        vl = self.ln_apply(E_, self.y, self.y, PA, PB, tb)
        E_.wait("gpsimd", "v", vl)
        gd = E_.dma("ga", x_tile_ap(xout, T, tt), self.y[:], accum_op=OP.add)
        return vl, gd


def copy_x_phase(nc, sems, cnt, xin, xout, T):
    def cp(E_):
        gd = E_.dma("ga", flat_ap(xout, D * T), flat_ap(xin, D * T))
        E_.wait("gpsimd", "ga", gd)
    run_phase(nc, sems, cnt, cp)



def build_l0ffn(nc, sems, cnt, inp, xin, xout, T):
    NTT = T // TT
    W1T = inp("l0f_w1t", [D, F1], F32)
    W2T = inp("l0f_w2t", [F1, D], F32)
    b1 = inp("l0f_b1", [F1], F32)
    b2 = inp("l0f_b2", [D], F32)
    w2scr = nc.dram_tensor("w2scr", [FC * 128, D], BF16).ap()

    with contextlib.ExitStack() as ctx:
        sb = lambda n, s, d: ctx.enter_context(nc.sbuf_tensor("f0" + n, s, d))
        R = Res(nc, ctx, "f0")
        W1q = sb("_w1q", [128, DC * F1], BF16)
        hbuf = sb("_h", [128, FC * TT], BF16)
        b1t = sb("_b1t", [128, FC], F32)
        b2t = sb("_b2t", [128, DC], F32)

        copy_x_phase(nc, sems, cnt, xin, xout, T)

        with contextlib.ExitStack() as cpp:
            p_ = lambda n, s, d: cpp.enter_context(nc.sbuf_tensor("f0p" + n, s, d))
            wls = [p_("_wl0", [128, F1], F32), p_("_wl1", [128, F1], F32)]
            wsg = p_("_wsg", [128, 1024], F32)
            wab = p_("_wab", [128, 1024], F32)
            w2qs = [p_("_w2q0", [128, 1024], BF16), p_("_w2q1", [128, 1024], BF16)]
            acc = p_("_acc", [128, 1], F32)
            part = p_("_part", [128, 1], F32)
            thr = p_("_thr", [128, 1], F32)

            def prep(E_):
                R.init_consts(E_)
                for c in range(FC):
                    E_.dma("da", b1t[:, c:c + 1], bias_ap(b1, c))
                for c in range(DC):
                    E_.dma("da", b2t[:, c:c + 1], bias_ap(b2, c))
                wg_v = [0, 0]
                wg_s = [0, 0]
                st_d = [0, 0]
                q_guard = 0
                ld = [0, 0]
                lane_of = ("da", "db")

                def wload(WT, c, nfree):
                    buf = c % 2
                    E_.wait("sync", "v", wg_v[buf])
                    E_.wait("sync", "s", wg_s[buf])
                    ld[buf] = E_.dma(lane_of[buf], wls[buf][:, 0:nfree],
                                     chunk_ap(WT, c, nfree))

                for wi, (WT, nch, nfree) in enumerate([(W1T, DC, F1), (W2T, FC, D)]):
                    vacc = E_.v("memset", acc[:], 0.0)
                    wload(WT, 0, nfree)
                    for c in range(nch):
                        buf = c % 2
                        if c + 1 < nch:
                            wload(WT, c + 1, nfree)
                        E_.wait("vector", lane_of[buf], ld[buf])
                        E_.wait("vector", "v", vacc)
                        vp = E_.v("tensor_reduce", part[:], wls[buf][:, 0:nfree],
                                  AX.X, OP.add, apply_absolute_value=True)
                        E_.wait("vector", "v", vp)
                        vacc = E_.v("tensor_tensor", acc[:], acc[:], part[:], OP.add)
                        wg_v[buf] = vp
                    E_.wait("tensor", "v", vacc)
                    tg = E_.mm(R.PM0[:, 0:1], R.ones128f[:], acc[:],
                               start=True, stop=True)
                    E_.wait("vector", "t", tg)
                    nel = float(nch * 128 * nfree)
                    vthr = E_.v("tensor_scalar", thr[:], R.PM0[:, 0:1],
                                0.5 / nel, 0.5e-5, OP.mult, OP.add)
                    wload(WT, 0, nfree)
                    for c in range(nch):
                        buf = c % 2
                        if c + 1 < nch:
                            wload(WT, c + 1, nfree)
                        for q4 in range(nfree // 1024):
                            wlsl = wls[buf][:, q4 * 1024:(q4 + 1) * 1024]
                            E_.wait("scalar", lane_of[buf], ld[buf])
                            E_.wait("scalar", "v", q_guard)
                            ssg = E_.s("sign", wsg[:], wlsl)
                            sab = E_.s("activation", wab[:], wlsl, AF.Abs)
                            E_.wait("vector", "s", sab)
                            E_.wait("vector", "v", vthr)
                            vc = E_.v("tensor_scalar", wab[:], wab[:], thr[:], None,
                                      OP.is_ge, OP.bypass)
                            E_.wait("vector", "v", vc)
                            E_.wait("vector", "s", ssg)
                            if wi == 0:
                                dst = W1q[:, c * F1 + q4 * 1024:
                                          c * F1 + (q4 + 1) * 1024]
                            else:
                                dst = w2qs[buf][:]
                                E_.wait("vector", "dc" if buf == 0 else "dd",
                                        st_d[buf])
                            vq = E_.v("tensor_tensor", dst, wab[:], wsg[:], OP.mult)
                            E_.wait("vector", "v", vq)
                            if wi == 1:
                                E_.wait("sync", "v", vq)
                                st_d[buf] = E_.dma("dc" if buf == 0 else "dd",
                                                   chunk_ap(w2scr, c, D),
                                                   w2qs[buf][:])
                            wg_s[buf] = sab
                            q_guard = vq
            run_phase(nc, sems, cnt, prep)

        with contextlib.ExitStack() as cm:
            m_ = lambda n, s, d: cm.enter_context(nc.sbuf_tensor("f0m" + n, s, d))
            xt = m_("_xt", [128, DC * TT], F32)
            xhat = m_("_xhat", [128, DC * TT], BF16)
            w2sl = [m_("_w2a", [128, FC * 128], BF16),
                    m_("_w2b", [128, FC * 128], BF16)]

            def main(E_):
                PM = (R.PM0, R.PM1)
                pm_s = [0, 0]
                h_read_t = 0
                w2_t = [0, 0]
                w2_d = [0, 0]
                ln_t = 0
                ln_v = 0
                for tt in range(NTT):
                    E_.wait("sync", "v", ln_v)
                    dd = E_.dma("da", xt[:], x_tile_ap(xin, T, tt))
                    for eng in ("tensor", "vector", "scalar"):
                        E_.wait(eng, "da", dd)
                    tb = R.ln_rows(E_, xt, R.PU4, R.PU5, ln_t, ln_v)
                    va = R.ln_apply(E_, xt, xhat, R.PU4, R.PU5, tb)
                    ln_v = va
                    E_.wait("tensor", "v", va)
                    evac_last = 0
                    for fc in range(FC):
                        P = PM[fc % 2]
                        E_.wait("tensor", "s", pm_s[fc % 2])
                        E_.wait("tensor", "t", h_read_t)
                        for dc in range(DC):
                            lhs = W1q[:, dc * F1 + fc * 128: dc * F1 + fc * 128 + 128]
                            tm = E_.mm(P[:, 0:TT], lhs,
                                       xhat[:, dc * TT:(dc + 1) * TT],
                                       start=(dc == 0), stop=(dc == DC - 1))
                        E_.wait("scalar", "t", tm)
                        se = E_.s("activation", hbuf[:, fc * TT:(fc + 1) * TT],
                                  P[:, 0:TT], AF.Relu, bias=b1t[:, fc:fc + 1],
                                  scale=1.0)
                        pm_s[fc % 2] = se
                        evac_last = se
                    for oc in range(DC):
                        buf = oc % 2
                        E_.wait("sync", "t", w2_t[buf])
                        w2ap = bass.AP(w2scr.tensor, w2scr.offset + oc * 128,
                                       [[D, 128], [128 * D, FC], [1, 128]])
                        w2_d[buf] = E_.dma("da", w2sl[buf][:], w2ap)
                        P = PM[oc % 2]
                        E_.wait("tensor", "s", pm_s[oc % 2])
                        E_.wait("tensor", "da", w2_d[buf])
                        E_.wait("tensor", "s", evac_last)
                        for fc in range(FC):
                            tm = E_.mm(P[:, 0:TT],
                                       w2sl[buf][:, fc * 128:(fc + 1) * 128],
                                       hbuf[:, fc * TT:(fc + 1) * TT],
                                       start=(fc == 0), stop=(fc == FC - 1))
                        w2_t[buf] = tm
                        E_.wait("scalar", "t", tm)
                        se = E_.s("activation", R.y[:, oc * TT:(oc + 1) * TT],
                                  P[:, 0:TT], AF.Identity, bias=b2t[:, oc:oc + 1],
                                  scale=1.0)
                        pm_s[oc % 2] = se
                    h_read_t = tm
                    vl, gd = R.finish_tile(E_, xout, T, tt, se, R.PU4, R.PU5,
                                           0, ln_v)
                    ln_v = vl
                    ln_t = E_.c["t"]
                fin = E_.c["ga"]
                E_.wait("gpsimd", "ga", fin)
            run_phase(nc, sems, cnt, main)



def build_attn(nc, sems, cnt, inp, xin, xout, T, layer):
    NTT = T // TT
    NTC = T // 128
    if layer == 0:
        Wq = inp("l0a_wq", [D, D], F32)
        Wk = inp("l0a_wk", [D, D], F32)
        Wv = inp("l0a_wv", [D, D], F32)
        Wo = inp("l0a_wo", [D, D], F32)
        bq = inp("l0a_bq", [D], F32)
        bk = inp("l0a_bk", [D], F32)
        bv = inp("l0a_bv", [D], F32)
        bo = inp("l0a_bo", [D], F32)
    else:
        WqkvT = inp("l1a_wqkvt", [D, 3 * D], F32)
        WprojT = inp("l1a_wprojt", [D, D], F32)
        pmask = inp("pmask", [128, 896], BF16)
    emask2 = inp("emask2", [128, 896], BF16)
    osel16 = inp("osel16", [16, NP * 128], BF16)
    if layer == 1 and DEBUG_L1:
        dbg = nc.dram_tensor("dbg", [4, TT], F32, kind="ExternalOutput").ap()
        dbg2 = nc.dram_tensor("dbg2", [64, TT], BF16, kind="ExternalOutput").ap()
    else:
        dbg = None
    qscr = nc.dram_tensor(f"qscr{layer}", [NP * 128, T], BF16).ap()
    wscr = nc.dram_tensor(f"wscr{layer}", [3 * NP * DC * 128, 128], BF16).ap()
    WS_P = DC * 128 * 128

    def wscr_lhs_ap(which, p):
        off = (which * NP + p) * WS_P
        return bass.AP(wscr.tensor, wscr.offset + off,
                       [[128, 128], [128 * 128, DC], [1, 128]])

    def wscr_v_ap(half, dc):
        off = 2 * NP * WS_P + (half * 4) * WS_P + dc * 128 * 128
        return bass.AP(wscr.tensor, wscr.offset + off,
                       [[128, 128], [WS_P, 4], [1, 128]])

    with contextlib.ExitStack() as ctx:
        sb = lambda n, s, d: ctx.enter_context(nc.sbuf_tensor(f"a{layer}" + n, s, d))
        KT = sb("_kt", [128, NP * T], BF16)
        V = sb("_v", [128, NTC * H * 65], BF16)
        R = Res(nc, ctx, f"a{layer}")
        if layer == 1:
            mscr = sb("_mscr", [128, NTC * 16], F32)

        def vsl(tc, h, w=65):
            return V[:, tc * H * 65 + h * 65: tc * H * 65 + h * 65 + w]

        copy_x_phase(nc, sems, cnt, xin, xout, T)

        with contextlib.ExitStack() as cw:
            w_ = lambda n, s, d: cw.enter_context(nc.sbuf_tensor(f"a{layer}w" + n, s, d))
            wl = w_("_wl", [128, 1024], F32)
            wb = w_("_wb", [128, 1024], BF16)

            def wprep(E_):
                R.init_consts(E_)
                prev_v = 0
                prev_d = 0
                for dc in range(DC):
                    for which in range(3):
                        if layer == 0:
                            W = (Wq, Wk, Wv)[which]
                            src = chunk_ap(W, dc, D)
                        else:
                            src = bass.AP(WqkvT.tensor,
                                          WqkvT.offset + dc * 128 * 3 * D + which * D,
                                          [[3 * D, 128], [1, D]])
                        E_.wait("sync", "v", prev_v)
                        dd = E_.dma("da", wl[:], src)
                        E_.wait("vector", "da", dd)
                        E_.wait("vector", "da", prev_d)
                        vq = E_.v("tensor_copy", wb[:], wl[:])
                        E_.wait("sync", "v", vq)
                        dst = bass.AP(wscr.tensor,
                                      wscr.offset + (which * NP) * WS_P + dc * 128 * 128,
                                      [[128, 128], [WS_P, NP], [1, 128]])
                        prev_d = E_.dma("da", dst, wb[:].rearrange(
                            "q (p c) -> q p c", c=128))
                        prev_v = vq
                E_.wait("sync", "da", prev_d)
            run_phase(nc, sems, cnt, wprep)

        with contextlib.ExitStack() as c2:
            p_ = lambda n, s, d: c2.enter_context(nc.sbuf_tensor(f"a{layer}p" + n, s, d))
            xt = p_("_xt", [128, DC * TT], F32)
            xhat = p_("_xhat", [128, DC * TT], BF16)
            wkq = [p_("_wkq0", [128, DC * 128], BF16),
                   p_("_wkq1", [128, DC * 128], BF16)]
            wv2 = [p_("_wv0", [128, 512], BF16), p_("_wv1", [128, 512], BF16)]
            qtmp = p_("_qtmp", [128, TT], BF16)
            bt = p_("_bt", [128, 2 * DC], F32)
            if layer == 1:
                pmaskT = p_("_pm", [128, 896], BF16)
                qt1 = [p_("_qt1a", [128, TT], BF16),
                       p_("_qt1b", [128, TT], BF16)]
                prt = p_("_p1", [128, 16], F32)
                p1th = p_("_p1t", [128, TT], F32)

            def proj(E_):
                if layer == 0:
                    for c in range(DC):
                        E_.dma("da", bt[:, c:c + 1], bias_ap(bq, c))
                        dd = E_.dma("da", bt[:, DC + c:DC + c + 1], bias_ap(bk, c))
                    E_.wait("scalar", "da", dd)
                PM = (R.PM0, R.PM1)
                bank_s = [0, 0, 0, 0]
                pm_s = bank_s
                w_t = [0, 0]
                w_d = [0, 0]
                q_aq = 0
                ln_t = 0
                ln_v = 0
                kse = 0
                ps_v = [0, 0, 0, 0]
                th_v = 0
                fin_v = [0, 0]
                qd1 = [0, 0]
                quse1 = [0, 0]
                p1q_t = 0
                if layer == 1:
                    dd0 = E_.dma("da", pmaskT[:], pmask)
                    E_.wait("vector", "da", dd0)

                def q1_load(buf, p_2, qt_):
                    qap = bass.AP(qscr.tensor,
                                  qscr.offset + p_2 * 128 * T + qt_ * TT,
                                  [[T, 128], [1, TT]])
                    E_.wait("sync", "t", quse1[buf])
                    E_.wait("sync", "aq", q_aq)
                    qd1[buf] = E_.dma("dd", qt1[buf][:], qap)

                def p1_block(qt):
                    nonlocal th_v, p1q_t
                    PS4 = (R.PSa[:, 0:512], R.PSa[:, 512:1024],
                           R.PSb[:, 0:512], R.PSb[:, 512:1024])
                    for p in range(NP):
                        i = qt * NP + p
                        buf = i % 2
                        if p == 0:
                            q1_load(buf, 0, qt)
                        qtile1 = qt1[buf]
                        E_.wait("tensor", "dd", qd1[buf])
                        if p + 1 < NP:
                            q1_load((buf + 1) % 2, p + 1, qt)
                        for qs in range(4):
                            qsg = qt * 4 + qs
                            vr2 = [0, 0]
                            for kt in range(qt + 1):
                                bi = (2 * kt) % 4
                                PA, PB = PS4[bi], PS4[bi + 1]
                                E_.wait("tensor", "v", ps_v[bi])
                                E_.wait("tensor", "v", ps_v[bi + 1])
                                E_.wait("tensor", "s", kse)
                                E_.wait("tensor", "s", bank_s[2 + bi // 2])
                                ta = E_.mm(PA, qtile1[0:64, qs * 128:(qs + 1) * 128],
                                           KT[0:64, p * T + kt * TT:
                                              p * T + (kt + 1) * TT],
                                           start=True, stop=True,
                                           tile_position=(0, 0))
                                tb_ = E_.mm(PB,
                                            qtile1[64:128, qs * 128:(qs + 1) * 128],
                                            KT[64:128, p * T + kt * TT:
                                               p * T + (kt + 1) * TT],
                                            start=True, stop=True,
                                            tile_position=(64, 0))
                                p1q_t = tb_
                                for hh, (P, td) in enumerate(((PA, ta), (PB, tb_))):
                                    E_.wait("vector", "t", td)
                                    E_.wait("vector", "v", fin_v[hh])
                                    if kt == qt:
                                        E_.wait("vector", "v", th_v)
                                        pms = bass.AP(pmaskT[:].tensor,
                                                      384 - 128 * qs,
                                                      [[896, 128], [1, TT]])
                                        vmk = E_.v("tensor_tensor", p1th[:], P,
                                                   pms, OP.min)
                                        E_.wait("vector", "v", vmk)
                                        vr = E_.v("tensor_reduce",
                                                  prt[:, hh * 8 + kt:
                                                      hh * 8 + kt + 1],
                                                  p1th[:], AX.X, OP.max)
                                        th_v = vr
                                    else:
                                        vr = E_.v("tensor_reduce",
                                                  prt[:, hh * 8 + kt:
                                                      hh * 8 + kt + 1],
                                                  P, AX.X, OP.max)
                                    vr2[hh] = vr
                                    ps_v[bi + hh] = vr
                            for hh in range(2):
                                col = qsg * 16 + 2 * p + hh
                                E_.wait("vector", "v", vr2[hh])
                                vf = E_.v("tensor_reduce",
                                          mscr[:, col:col + 1],
                                          prt[:, hh * 8:hh * 8 + qt + 1],
                                          AX.X, OP.max)
                                fin_v[hh] = vf
                        quse1[buf] = p1q_t

                for tt in range(NTT):
                    E_.wait("sync", "v", ln_v)
                    dd = E_.dma("da", xt[:], x_tile_ap(xin, T, tt))
                    for eng in ("tensor", "vector", "scalar"):
                        E_.wait(eng, "da", dd)
                    tb = R.ln_rows(E_, xt, R.PU4, R.PU5, ln_t, ln_v)
                    ln_v = R.ln_apply(E_, xt, xhat, R.PU4, R.PU5, tb)
                    E_.wait("tensor", "v", ln_v)
                    for wi, which in enumerate((1, 0)):
                        for p in range(NP):
                            buf = p % 2
                            E_.wait("sync", "t", w_t[buf])
                            w_d[buf] = E_.dma("da", wkq[buf][:], wscr_lhs_ap(which, p))
                            P = PM[p % 2]
                            E_.wait("tensor", "s", pm_s[p % 2])
                            E_.wait("tensor", "da", w_d[buf])
                            for dc in range(DC):
                                tm = E_.mm(P[:, 0:TT],
                                           wkq[buf][:, dc * 128:(dc + 1) * 128],
                                           xhat[:, dc * TT:(dc + 1) * TT],
                                           start=(dc == 0), stop=(dc == DC - 1))
                            w_t[buf] = tm
                            E_.wait("scalar", "t", tm)
                            if which == 1:
                                dst = KT[:, p * T + tt * TT: p * T + (tt + 1) * TT]
                                if layer == 0:
                                    se = E_.s("activation", dst, P[:, 0:TT],
                                              AF.Identity,
                                              bias=bt[:, DC + p:DC + p + 1], scale=1.0)
                                else:
                                    se = E_.s("copy", dst, P[:, 0:TT])
                            else:
                                E_.wait("scalar", "aq", q_aq)
                                if layer == 0:
                                    se = E_.s("activation", qtmp[:], P[:, 0:TT],
                                              AF.Identity, bias=bt[:, p:p + 1],
                                              scale=1.0)
                                else:
                                    se = E_.s("copy", qtmp[:], P[:, 0:TT])
                                qap = bass.AP(qscr.tensor,
                                              qscr.offset + p * 128 * T + tt * TT,
                                              [[T, 128], [1, TT]])
                                E_.wait("scalar", "s", se)
                                q_aq = E_.dma("aq", qap, qtmp[:])
                            pm_s[p % 2] = se
                        if which == 1:
                            kse = se
                    P4 = (R.PM0, R.PM1, R.PSa, R.PSb)
                    vm = E_.v("memset",
                              V[:, tt * 4 * H * 65:(tt + 1) * 4 * H * 65], 1.0)
                    for half in range(2):
                        for t4 in range(4):
                            E_.wait("tensor", "s", bank_s[t4])
                        E_.wait("tensor", "v", max(ps_v))
                        E_.wait("tensor", "v", vm)
                        for dc in range(DC):
                            buf = dc % 2
                            lane = "db" if buf == 0 else "dc"
                            E_.wait("sync", "t", w_t[buf])
                            w_d[buf] = E_.dma(lane, wv2[buf][:], wscr_v_ap(half, dc))
                            E_.wait("tensor", lane, w_d[buf])
                            for t4 in range(4):
                                tm = E_.mm(P4[t4][:, 0:512],
                                           xhat[:, dc * TT + t4 * 128:
                                                dc * TT + t4 * 128 + 128],
                                           wv2[buf][:],
                                           start=(dc == 0), stop=(dc == DC - 1))
                            w_t[buf] = tm
                        E_.wait("scalar", "t", tm)
                        for t4 in range(4):
                            tc = tt * 4 + t4
                            dst = V[:, tc * H * 65 + half * 8 * 65:
                                    tc * H * 65 + (half + 1) * 8 * 65]
                            dst = dst.rearrange("p (h d) -> p h d", d=65)[:, :, 0:64]
                            se = E_.s("copy", dst, P4[t4][:, 0:512])
                            bank_s[t4] = se
                    E_.wait("tensor", "s", se)
                    ln_t = E_.c["t"]
                    if layer == 1:
                        p1_block(tt)
            run_phase(nc, sems, cnt, proj)

        with contextlib.ExitStack() as c3:
            p_ = lambda n, s, d: c3.enter_context(nc.sbuf_tensor(f"a{layer}t" + n, s, d))
            WO = p_("_wo", [128, DC * D], BF16)
            stag = p_("_stag", [128, NP * TT], BF16)
            qtl = [p_("_qt0", [128, TT], BF16), p_("_qt1", [128, TT], BF16)]
            EAB = [p_("_e0", [128, 1024], BF16), p_("_e1", [128, 1024], BF16),
                   p_("_e2", [128, 1024], BF16)]
            srowA = R.y[0:1, 0:TT]
            srowB = R.y[0:1, TT:2 * TT]
            Scol = p_("_scl", [16, TT], F32)
            chT = R.y[0:16, 2 * TT:3 * TT]
            c1m = p_("_c1m", [16, TT], BF16)
            osel = p_("_osl", [16, NP * 128], BF16)
            emaskT = p_("_em", [128, 896], BF16)
            if layer == 0:
                boP = p_("_bop", [128, DC], F32)
                bvb = p_("_bvb", [128, DC], BF16)
            else:
                Mcol = p_("_mcl", [16, TT], F32)
                c2m = p_("_c2m", [16, TT], BF16)
                wsel = p_("_wsl", [16, NP * 128], BF16)
                wcolp = p_("_wcp", [1, H * 64], BF16)

            with contextlib.ExitStack() as cw2:
                wl2 = R.y[:, 0:D // 2]

                def wo_prep(E_):
                    src = Wo if layer == 0 else WprojT
                    prev = 0
                    for c2 in range(2 * DC):
                        c, hf = c2 // 2, c2 % 2
                        E_.wait("sync", "v", prev)
                        dd = E_.dma("da", wl2, bass.AP(
                            src.tensor, src.offset + c * 128 * D + hf * (D // 2),
                            [[D, 128], [1, D // 2]]))
                        E_.wait("vector", "da", dd)
                        prev = E_.v("tensor_copy",
                                    WO[:, c * D + hf * (D // 2):
                                       c * D + (hf + 1) * (D // 2)], wl2)
                        E_.wait("vector", "v", prev)
                    E_.dma("da", osel[:], osel16)
                    dd = E_.dma("da", emaskT[:], emask2)
                    for eng in ("tensor", "vector", "scalar"):
                        E_.wait(eng, "da", dd)
                    if layer == 0:
                        vq = prev
                        for c in range(DC):
                            E_.wait("sync", "v", vq)
                            dd = E_.dma("da", R.y[:, 0:1], bias_ap(bv, c))
                            E_.wait("vector", "da", dd)
                            vq = E_.v("tensor_copy", bvb[:, c:c + 1], R.y[:, 0:1])
                            E_.wait("vector", "v", vq)
                            E_.dma("da", boP[:, c:c + 1], bias_ap(bo, c))
                        dd = E_.c["da"]
                        E_.wait("tensor", "da", dd)
                        E_.wait("tensor", "v", vq)
                        for oc in range(DC):
                            for dc in range(DC):
                                tm = E_.mm(R.PM0[:, oc:oc + 1],
                                           WO[:, dc * D + oc * 128:
                                              dc * D + oc * 128 + 128],
                                           bvb[:, dc:dc + 1],
                                           start=(dc == 0), stop=(dc == DC - 1))
                        E_.wait("vector", "t", tm)
                        E_.wait("vector", "da", dd)
                        vb2 = E_.v("tensor_tensor", boP[:, 0:DC], boP[:, 0:DC],
                                   R.PM0[:, 0:DC], OP.add)
                        E_.wait("tensor", "v", vb2)
                        E_.wait("scalar", "v", vb2)
                    else:
                        vz = E_.v("memset", wcolp[:], 0.0)
                        pb_v = vz
                        for h in range(H):
                            for tc in range(NTC):
                                E_.wait("tensor", "v", pb_v)
                                tm = E_.mm(R.PM0[0:1, 0:64], R.onesc[:],
                                           vsl(tc, h, 64),
                                           start=(tc == 0), stop=(tc == NTC - 1))
                            E_.wait("vector", "t", tm)
                            pb_v = E_.v("tensor_copy",
                                        wcolp[:, h * 64:(h + 1) * 64],
                                        R.PM0[0:1, 0:64])
                            E_.wait("vector", "v", pb_v)
                        E_.wait("tensor", "v", pb_v)
                        vws = E_.v("memset", wsel[:], 0.0)
                        E_.wait("scalar", "v", vws)
                        E_.wait("scalar", "v", pb_v)
                        for h in range(H):
                            c0 = (h // 2) * 128 + 64 * (h % 2)
                            E_.dma("aq", wsel[h:h + 1, c0:c0 + 64],
                                   wcolp[:, h * 64:(h + 1) * 64])
                run_phase(nc, sems, cnt, wo_prep)

            def attn(E_):
                PM = (R.PM0, R.PM1)
                PSS = (R.PSa, R.PSb)
                pm_s = [0, 0]
                pm_v = [0, 0]
                pu_s = 0
                pu_vv = 0
                q_t = 0
                fin_g = 0
                esPM = 0
                proj_t = 0
                scol_gate = 0
                cht_gate = 0
                mcol_gate = 0
                scol_d = 0
                mcol_d = 0
                c1_rdy = 0
                c1_rdy_s = 0
                c2_rdy = 0
                qd = [0, 0]
                quse = [0, 0]

                def q_load(i):
                    qt_, p_2 = divmod(i, NP)
                    buf = i % 2
                    lane = "db" if buf == 0 else "dc"
                    qap = bass.AP(qscr.tensor,
                                  qscr.offset + p_2 * 128 * T + qt_ * TT,
                                  [[T, 128], [1, TT]])
                    E_.wait("sync", "t", quse[buf])
                    qd[buf] = E_.dma(lane, qtl[buf][:], qap)

                PVb = ((R.PU4, R.PU5), (R.PU4, R.PU5))
                pu_free_s = [0, 0]
                pu_free_v = [0, 0]
                for qt in range(NTT):
                    pu_free_s[1] = max(pu_free_s[1], pm_s[0], pm_s[1])
                    pu_free_v[1] = max(pu_free_v[1], pm_v[0], pm_v[1])
                    if layer == 1:
                        E_.wait("sync", "s", scol_gate)
                        E_.wait("sync", "v", mcol_gate)
                        first = True
                        for h in range(H):
                            for qs in range(4):
                                col = (qt * 4 + qs) * 16 + h
                                mcol_d = E_.dma(
                                    "da",
                                    Mcol[h:h + 1, qs * 128:(qs + 1) * 128],
                                    mscr[:, col:col + 1], nowait=not first)
                                first = False
                    for p in range(NP):
                        par = 0
                        bkA, bkB = PVb[par]
                        qi = qt * NP + p
                        if qi == 0:
                            q_load(0)
                        qtile = qtl[qi % 2]
                        qlane = "db" if qi % 2 == 0 else "dc"
                        E_.wait("tensor", qlane, qd[qi % 2])
                        if qi + 1 < NTT * NP:
                            q_load(qi + 1)
                        nkc = (qt + 1) * 4
                        es = [0, 0, 0]
                        pv_t = [0, 0, 0]
                        sc_t = [0, 0, 0]

                        def s_mms(kc):
                            sl = kc % 3
                            E_.wait("tensor", "s", es[sl])
                            if sl < 2:
                                PSSc = PSS[sl]
                                d0 = PSSc[:, 0:TT]
                                d1 = PSSc[:, 512:1024]
                            else:
                                E_.wait("tensor", "v", max(pm_v))
                                E_.wait("tensor", "s", max(pm_s))
                                d0 = R.PM0[:, 0:TT]
                                d1 = R.PM1[:, 0:TT]
                            E_.mm(d0,
                                  KT[0:64, p * T + kc * 128:
                                     p * T + (kc + 1) * 128],
                                  qtile[0:64, :], start=True, stop=True,
                                  tile_position=(0, 0))
                            tb_ = E_.mm(d1,
                                        KT[64:128, p * T + kc * 128:
                                           p * T + (kc + 1) * 128],
                                        qtile[64:128, :], start=True, stop=True,
                                        tile_position=(64, 0))
                            sc_t[sl] = tb_
                            return tb_

                        q_t = s_mms(0)
                        if nkc > 1:
                            q_t = s_mms(1)
                        for kc in range(nkc):
                            sl = kc % 3
                            EB = EAB[sl]
                            E_.wait("scalar", "t", sc_t[sl])
                            E_.wait("scalar", "t", pv_t[sl])
                            if sl < 2:
                                se = E_.s("activation", EB[:], PSS[sl][:],
                                          AF.Exp, scale=0.125)
                            else:
                                E_.s("activation", EB[:, 0:512],
                                     R.PM0[:, 0:TT], AF.Exp, scale=0.125)
                                se = E_.s("activation", EB[:, 512:1024],
                                          R.PM1[:, 0:TT], AF.Exp, scale=0.125)
                                esPM = se
                            es[sl] = se
                            diag = (kc >= qt * 4)
                            if diag:
                                vv = kc - qt * 4
                                E_.wait("vector", "s", se)
                                mrep = bass.AP(emaskT[:].tensor, 384 - 128 * vv,
                                               [[896, 128], [0, 2], [1, 512]])
                                ve = E_.v("tensor_tensor",
                                          EB[:].rearrange("p (a b) -> p a b", b=512),
                                          EB[:].rearrange("p (a b) -> p a b", b=512),
                                          mrep, OP.mult)
                            if kc + 2 < nkc:
                                q_t = s_mms(kc + 2)
                            if diag:
                                E_.wait("tensor", "v", ve)
                            else:
                                E_.wait("tensor", "s", se)
                            if kc == 0:
                                E_.wait("tensor", "s", pu_free_s[par])
                                E_.wait("tensor", "v", pu_free_v[par])
                            islast = (kc == nkc - 1)
                            t1 = E_.mm(bkA[0:65, 0:TT], vsl(kc, 2 * p),
                                       EB[:, 0:512], start=(kc == 0), stop=islast)
                            t2 = E_.mm(bkB[0:65, 0:TT], vsl(kc, 2 * p + 1),
                                       EB[:, 512:1024], start=(kc == 0), stop=islast)
                            pv_t[kc % 3] = t2
                        quse[qi % 2] = q_t
                        E_.wait("vector", "t", t2)
                        E_.wait("vector", "ga", fin_g)
                        E_.wait("vector", "aq", scol_d)
                        va = E_.v("tensor_copy", srowA, bkA[64:65, 0:TT])
                        E_.wait("vector", "v", va)
                        vb_ = E_.v("tensor_copy", srowB, bkB[64:65, 0:TT])
                        E_.wait("scalar", "v", va)
                        E_.wait("scalar", "t", t2)
                        E_.wait("scalar", "t", proj_t)
                        pu_free_s[par] = E_.s("copy",
                                              stag[0:64, p * TT:(p + 1) * TT],
                                              bkA[0:64, 0:TT])
                        E_.wait("vector", "v", vb_)
                        E_.wait("vector", "t", proj_t)
                        pu_free_v[par] = E_.v("tensor_copy",
                                              stag[64:128, p * TT:(p + 1) * TT],
                                              bkB[0:64, 0:TT])
                        if p == 0:
                            E_.wait("scalar", "s", scol_gate)
                        E_.wait("scalar", "v", vb_)
                        E_.dma("aq", Scol[2 * p:2 * p + 1, :], srowA)
                        scol_d = E_.dma("aq", Scol[2 * p + 1:2 * p + 2, :],
                                        srowB, nowait=True)
                    if layer == 0:
                        E_.wait("scalar", "aq", scol_d)
                        E_.wait("scalar", "v", cht_gate)
                        E_.wait("scalar", "v", pm_v[0])
                        E_.wait("scalar", "ga", fin_g)
                        cs = E_.s("activation", chT, Scol[:], AF.Ln)
                        E_.wait("scalar", "s", cs)
                        cs = E_.s("activation", c1m[:], chT, AF.Exp,
                                  scale=-1.0)
                        scol_gate = cs
                        c1_rdy_s = cs
                    else:
                        E_.wait("scalar", "da", mcol_d)
                        E_.wait("scalar", "v", cht_gate)
                        E_.wait("scalar", "v", pm_v[0])
                        E_.wait("scalar", "ga", fin_g)
                        cs = E_.s("activation", chT, Mcol[:], AF.Exp,
                                  scale=0.125)
                        E_.wait("vector", "s", cs)
                        cv = E_.v("tensor_scalar", chT, chT, 1e-6,
                                  None, OP.mult, OP.bypass)
                        E_.wait("vector", "aq", scol_d)
                        E_.wait("vector", "v", cv)
                        cv = E_.v("tensor_tensor", chT, Scol[:], chT,
                                  OP.add)
                        E_.wait("scalar", "v", cv)
                        cs = E_.s("activation", Mcol[:], chT, AF.Ln)
                        E_.wait("scalar", "s", cs)
                        cs = E_.s("activation", chT, Mcol[:], AF.Exp,
                                  scale=-1.0)
                        E_.wait("vector", "s", cs)
                        cv = E_.v("tensor_tensor", Mcol[:], Scol[:], chT,
                                  OP.mult)
                        E_.wait("vector", "v", cv)
                        cv = E_.v("tensor_scalar", Mcol[:], Mcol[:], 1.0,
                                  float(T) * 1e-8, OP.mult, OP.add)
                        E_.wait("scalar", "v", cv)
                        cs = E_.s("activation", Scol[:], Mcol[:], AF.Ln)
                        E_.wait("scalar", "s", cs)
                        cs = E_.s("activation", Mcol[:], Scol[:], AF.Exp,
                                  scale=-1.0)
                        scol_gate = cs
                        E_.wait("vector", "s", cs)
                        cv = E_.v("tensor_tensor", c1m[:], chT, Mcol[:],
                                  OP.mult)
                        cht_gate = cv
                        c1_rdy = cv
                        E_.wait("vector", "v", cv)
                        cv = E_.v("tensor_scalar", c2m[:], Mcol[:], 1e-8,
                                  None, OP.mult, OP.bypass)
                        mcol_gate = cv
                        c2_rdy = cv
                    bank_v = [pm_v[0], pm_v[1]]
                    ap_v = 0
                    for p in range(NP):
                        stg = stag[:, p * TT:(p + 1) * TT]
                        E_.wait("tensor", "v", c1_rdy)
                        E_.wait("tensor", "s", c1_rdy_s)
                        E_.wait("tensor", "s", max(pu_free_s))
                        E_.wait("tensor", "v", max(pu_free_v))
                        E_.wait("tensor", "s", esPM)
                        E_.wait("tensor", "v", bank_v[0])
                        E_.wait("tensor", "s", pm_s[0])
                        tb0 = E_.mm(PM[0][:, 0:TT],
                                    osel[:, p * 128:(p + 1) * 128],
                                    c1m[:], start=True, stop=True)
                        if layer == 1:
                            E_.wait("tensor", "v", c2_rdy)
                            E_.wait("tensor", "v", bank_v[1])
                            E_.wait("tensor", "s", pm_s[1])
                            tb1 = E_.mm(PM[1][:, 0:TT],
                                        wsel[:, p * 128:(p + 1) * 128],
                                        c2m[:], start=True, stop=True)
                        E_.wait("vector", "t", tb0)
                        E_.wait("vector", "s", esPM)
                        E_.wait("vector", "s", max(pu_free_s))
                        E_.wait("vector", "v", max(pu_free_v))
                        E_.wait("vector", "v", ap_v)
                        if layer == 0:
                            vap = E_.v("tensor_tensor", stg, stg,
                                       PM[0][:, 0:TT], OP.mult)
                            bank_v[0] = vap
                        else:
                            vth = E_.v("tensor_tensor", R.th[:], stg,
                                       PM[0][:, 0:TT], OP.mult)
                            bank_v[0] = vth
                            E_.wait("vector", "t", tb1)
                            E_.wait("vector", "v", vth)
                            vap = E_.v("tensor_tensor", stg, R.th[:],
                                       PM[1][:, 0:TT], OP.add)
                            bank_v[1] = vap
                        ap_v = vap
                    pm_v = [ap_v, ap_v]
                    se = 0
                    E_.wait("scalar", "ga", fin_g)
                    E_.wait("scalar", "aq", scol_d)
                    E_.wait("scalar", "s", scol_gate)
                    E_.wait("scalar", "v", cht_gate)
                    for oc in range(DC):
                        P = PM[oc % 2]
                        E_.wait("tensor", "s", pm_s[oc % 2])
                        E_.wait("tensor", "v", ap_v)
                        for dc in range(DC):
                            tm = E_.mm(P[:, 0:TT],
                                       WO[:, dc * D + oc * 128:
                                          dc * D + oc * 128 + 128],
                                       stag[:, dc * TT:(dc + 1) * TT],
                                       start=(dc == 0), stop=(dc == DC - 1))
                        E_.wait("scalar", "t", tm)
                        if layer == 0:
                            se = E_.s("activation", R.y[:, oc * TT:(oc + 1) * TT],
                                      P[:, 0:TT], AF.Identity,
                                      bias=boP[:, oc:oc + 1], scale=1.0)
                        else:
                            se = E_.s("copy", R.y[:, oc * TT:(oc + 1) * TT],
                                      P[:, 0:TT])
                        pm_s[oc % 2] = se
                    proj_t = tm
                    vl, gd = R.finish_tile(E_, xout, T, qt, se, PM[0], PM[1],
                                           0, max(pm_v))
                    pm_v = [vl, vl]
                    fin_g = gd
                fin = E_.c["ga"]
                E_.wait("gpsimd", "ga", fin)
            run_phase(nc, sems, cnt, attn)



def build_l1ffn(nc, sems, cnt, inp, xin, xout, T):
    NTT = T // TT
    k0 = inp("l1f_k0", [2 * D, F1], U8)
    k1 = inp("l1f_k1", [F1, F1], U8)
    k2 = inp("l1f_k2", [F1, D], U8)
    s0 = inp("l1f_s0", [1, 1], F32)
    s1 = inp("l1f_s1", [1, 1], F32)
    s2 = inp("l1f_s2", [1, 1], F32)
    k1scr = nc.dram_tensor("k1scr", [FC * 128, FC * 128], BF16).ap()

    with contextlib.ExitStack() as ctx:
        sb = lambda n, s, d: ctx.enter_context(nc.sbuf_tensor("f1" + n, s, d))
        R = Res(nc, ctx, "f1")
        k0e = sb("_k0e", [128, DC * F1], BF16)
        h1 = sb("_h1", [128, FC * TT], BF16)
        g2 = sb("_g2", [128, FC * TT], BF16)
        k1s = [sb("_k1a", [128, FC * 128], BF16), sb("_k1b", [128, FC * 128], BF16)]
        k2s = [sb("_k2a", [128, D], BF16), sb("_k2b", [128, D], BF16)]
        ka, kb = k1s[0], k1s[1]
        xt = sb("_xt", [128, DC * TT], F32)
        xhat = sb("_xhat", [128, DC * TT], BF16)
        scl = sb("_scl", [128, 3], F32)
        rrow = sb("_rrow", [1, TT], F32)

        copy_x_phase(nc, sems, cnt, xin, xout, T)

        def prep(E_):
            ic = R.init_consts(E_)
            prev = 0
            for dc in range(DC):
                E_.wait("gpsimd", "v", prev)
                d1 = E_.dma("ga", ka[:], chunk_ap(k0, dc, F1))
                d2 = E_.dma("ga", kb[:], chunk_ap(k0, dc + DC, F1))
                E_.wait("vector", "ga", d2)
                prev = E_.v("tensor_tensor", k0e[:, dc * F1:(dc + 1) * F1],
                            ka[:], kb[:], OP.subtract)
                E_.wait("vector", "v", prev)
            kd = [0, 0]
            kv = [prev, prev]
            for fc1 in range(FC):
                buf = fc1 % 2
                kb_ = (ka, kb)[buf]
                lane = "gb" if buf == 0 else "gc"
                E_.wait("gpsimd", "dd", kd[buf])
                E_.wait("gpsimd", "v", kv[buf])
                dl = E_.dma(lane, kb_[:], chunk_ap(k1, fc1, F1))
                E_.wait("sync", lane, dl)
                dst = bass.AP(k1scr.tensor, k1scr.offset + fc1 * 128,
                              [[FC * 128, 128], [FC * 16384, FC], [1, 128]])
                kd[buf] = E_.dma("dd", dst,
                                 kb_[:].rearrange("r (a c) -> r a c", c=128))
            E_.wait("sync", "dd", E_.c["dd"])
            eps_guard_v = ic
            eps_guard_t = 0
            for i, (sc, mult) in enumerate(((s0, 1.0), (s1, 4.0), (s2, 4.0))):
                E_.wait("sync", "v", eps_guard_v)
                E_.wait("sync", "t", eps_guard_t)
                dd = E_.dma("da", R.eps11[:], sc)
                E_.wait("tensor", "da", dd)
                tm = E_.mm(R.PM0[:, 0:1], R.ones1[0:1, :], R.eps11[:],
                           start=True, stop=True)
                E_.wait("vector", "t", tm)
                vv = E_.v("tensor_scalar", scl[:, i:i + 1], R.PM0[:, 0:1],
                          mult, None, OP.mult, OP.bypass)
                E_.wait("tensor", "v", vv)
                eps_guard_t = tm
            E_.wait("vector", "t", tm)
            vv = E_.v("memset", R.eps11[:], 1e-5)
            E_.wait("scalar", "v", vv)
        run_phase(nc, sems, cnt, prep)

        def main(E_):
            P4 = (R.PM0[:, 0:512], R.PM1[:, 0:512],
                  R.PSa[:, 0:512], R.PSb[:, 0:512])
            pm_v = [0, 0, 0, 0]
            h1_read_t = 0
            g2_read_t = 0
            k1_t = [0, 0]
            k1_d = [0, 0]
            k2_t = [0, 0]
            k2_d = [0, 0]
            ln_t = 0
            ln_v = 0
            for tt in range(NTT):
                E_.wait("sync", "v", ln_v)
                dd = E_.dma("da", xt[:], x_tile_ap(xin, T, tt))
                for eng in ("tensor", "vector", "scalar"):
                    E_.wait(eng, "da", dd)
                tb = R.ln_rows(E_, xt, R.PU4, R.PU5, ln_t, ln_v)
                ln_v = R.ln_apply(E_, xt, xhat, R.PU4, R.PU5, tb)
                E_.wait("tensor", "v", ln_v)
                for fc in range(FC):
                    bi = fc % 2
                    P = P4[bi]
                    E_.wait("tensor", "v", pm_v[bi])
                    E_.wait("tensor", "t", h1_read_t)
                    for dc in range(DC):
                        lhs = k0e[:, dc * F1 + fc * 128: dc * F1 + fc * 128 + 128]
                        tm = E_.mm(P, lhs, xhat[:, dc * TT:(dc + 1) * TT],
                                   start=(dc == 0), stop=(dc == DC - 1))
                    E_.wait("vector", "t", tm)
                    ve = E_.v("tensor_scalar", h1[:, fc * TT:(fc + 1) * TT],
                              P, scl[:, 0:1], 0.0, OP.mult, OP.max)
                    pm_v[bi] = ve
                    E_.wait("vector", "v", ve)
                h1_v = ve
                E_.wait("tensor", "v", h1_v)
                for fc in range(FC):
                    tm = E_.mm(R.PU4[0:1, 0:TT], R.onesc[:],
                               h1[:, fc * TT:(fc + 1) * TT],
                               start=(fc == 0), stop=(fc == FC - 1))
                E_.wait("vector", "t", tm)
                vr = E_.v("tensor_scalar", rrow[:], R.PU4[0:1, 0:TT], -0.5, None,
                          OP.mult, OP.bypass)
                E_.wait("tensor", "v", vr)
                rr_t = tm
                for fc2 in range(FC):
                    buf = fc2 % 2
                    lane = "db" if buf == 0 else "dc"
                    E_.wait("sync", "t", k1_t[buf])
                    k1ap = bass.AP(k1scr.tensor,
                                   k1scr.offset + fc2 * FC * 16384,
                                   [[FC * 128, 128], [1, FC * 128]])
                    k1_d[buf] = E_.dma(lane, k1s[buf][:], k1ap)
                    bi = 2 + fc2 % 2
                    P = P4[bi]
                    E_.wait("tensor", "v", pm_v[bi])
                    E_.wait("tensor", "t", g2_read_t)
                    tm = E_.mm(P, R.ones1[0:1, :], rrow[:],
                               start=True, stop=False)
                    E_.wait("tensor", lane, k1_d[buf])
                    for fc in range(FC):
                        tm = E_.mm(P, k1s[buf][:, fc * 128:(fc + 1) * 128],
                                   h1[:, fc * TT:(fc + 1) * TT],
                                   start=False, stop=(fc == FC - 1))
                    k1_t[buf] = tm
                    E_.wait("vector", "t", tm)
                    ve = E_.v("tensor_scalar", g2[:, fc2 * TT:(fc2 + 1) * TT],
                              P, scl[:, 1:2], 0.0, OP.mult, OP.max)
                    pm_v[bi] = ve
                    E_.wait("vector", "v", ve)
                g2_v = ve
                h1_read_t = tm
                E_.wait("tensor", "v", g2_v)
                E_.wait("tensor", "v", E_.c["v"])
                for fc in range(FC):
                    tm = E_.mm(R.PU4[0:1, 0:TT], R.onesc[:],
                               g2[:, fc * TT:(fc + 1) * TT],
                               start=(fc == 0), stop=(fc == FC - 1))
                E_.wait("vector", "t", tm)
                vr = E_.v("tensor_scalar", rrow[:], R.PU4[0:1, 0:TT], -0.5, None,
                          OP.mult, OP.bypass)
                E_.wait("tensor", "v", vr)
                ogt = [0, 0, 0, 0]
                for ocg in range(2):
                    for fc2 in range(FC):
                        buf = fc2 % 2
                        E_.wait("gpsimd", "t", k2_t[buf])
                        lane = "gb" if buf == 0 else "gc"
                        k2_d[buf] = E_.dma(lane, k2s[buf][:],
                                           chunk_ap(k2, fc2, D))
                        E_.wait("tensor", lane, k2_d[buf])
                        for oci in range(4):
                            oc = ocg * 4 + oci
                            P = P4[oci]
                            if fc2 == 0:
                                E_.wait("tensor", "v", pm_v[oci])
                                E_.mm(P, R.ones1[0:1, :], rrow[:],
                                      start=True, stop=False)
                            tm = E_.mm(P, k2s[buf][:, oc * 128:(oc + 1) * 128],
                                       g2[:, fc2 * TT:(fc2 + 1) * TT],
                                       start=False, stop=(fc2 == FC - 1))
                            ogt[oci] = tm
                            if oci == 3:
                                k2_t[buf] = tm
                    g2_read_t = tm
                    for oci in range(4):
                        oc = ocg * 4 + oci
                        E_.wait("vector", "t", ogt[oci])
                        ve = E_.v("tensor_scalar", R.y[:, oc * TT:(oc + 1) * TT],
                                  P4[oci], scl[:, 2:3], 0.0, OP.mult, OP.bypass)
                        pm_v[oci] = ve
                        E_.wait("vector", "v", ve)
                E_.wait("scalar", "v", ve)
                E_.wait("tensor", "v", ve)
                vl, gd = R.finish_tile(E_, xout, T, tt, E_.c["s"], R.PU4, R.PU5,
                                       rr_t, ln_v)
                ln_v = vl
                ln_t = E_.c["t"]
            fin = E_.c["ga"]
            E_.wait("gpsimd", "ga", fin)
        run_phase(nc, sems, cnt, main)



PARTS = ("l0attn", "l0ffn", "l1attn", "l1ffn")


def build_program(parts, T):
    nc = bass.Bass("TRN2", target_bir_lowering=False)
    declared = {}

    def inp(name, shape, dt):
        if name not in declared:
            declared[name] = nc.dram_tensor(name, shape, dt,
                                            kind="ExternalInput").ap()
        return declared[name]

    xin = nc.dram_tensor("xT", [D, T], F32, kind="ExternalInput").ap()
    yout = nc.dram_tensor("yT", [D, T], F32, kind="ExternalOutput").ap()
    xs = [nc.dram_tensor("xs0", [D, T], F32).ap(),
          nc.dram_tensor("xs1", [D, T], F32).ap()]
    sems = {}
    cnt = {}
    with contextlib.ExitStack() as ctx:
        for s in tuple(LANES) + ("t", "v", "s"):
            sems[s] = ctx.enter_context(nc.semaphore("sem_" + s))
            cnt[s] = 0
        for i, part in enumerate(parts):
            src = xin if i == 0 else xs[(i - 1) % 2]
            dst = yout if i == len(parts) - 1 else xs[i % 2]
            if part == "l0attn":
                build_attn(nc, sems, cnt, inp, src, dst, T, 0)
            elif part == "l0ffn":
                build_l0ffn(nc, sems, cnt, inp, src, dst, T)
            elif part == "l1attn":
                build_attn(nc, sems, cnt, inp, src, dst, T, 1)
            elif part == "l1ffn":
                build_l1ffn(nc, sems, cnt, inp, src, dst, T)
            else:
                raise ValueError(part)
    return nc, set(declared.keys())


def host_consts():
    j = np.arange(512)
    r = np.arange(128)[:, None]
    c = np.arange(896)
    emask2 = (c[None, :] >= r + 384).astype(ml_dtypes.bfloat16)
    pmask = np.where(c[None, :] <= r + 384, 3.0e38, -3.0e38).astype(
        ml_dtypes.bfloat16)
    osel = np.zeros((16, NP * 128), np.float32)
    for p in range(NP):
        osel[2 * p, p * 128:p * 128 + 64] = 1.0
        osel[2 * p + 1, p * 128 + 64:p * 128 + 128] = 1.0
    return {"emask2": emask2, "pmask": pmask,
            "osel16": osel.astype(ml_dtypes.bfloat16)}


def host_weights(inputs):
    c = np.ascontiguousarray
    w = {
        "l0a_wq": c(np.asarray(inputs["L0_attn_Wq"], np.float32)),
        "l0a_wk": c(np.asarray(inputs["L0_attn_Wk"], np.float32)),
        "l0a_wv": c(np.asarray(inputs["L0_attn_Wv"], np.float32)),
        "l0a_wo": c(np.asarray(inputs["L0_attn_Wo"], np.float32)),
        "l0a_bq": c(np.asarray(inputs["L0_attn_bq"], np.float32)),
        "l0a_bk": c(np.asarray(inputs["L0_attn_bk"], np.float32)),
        "l0a_bv": c(np.asarray(inputs["L0_attn_bv"], np.float32)),
        "l0a_bo": c(np.asarray(inputs["L0_attn_bo"], np.float32)),
        "l0f_w1t": c(np.asarray(inputs["L0_ffn_W1"], np.float32).T),
        "l0f_w2t": c(np.asarray(inputs["L0_ffn_W2"], np.float32).T),
        "l0f_b1": c(np.asarray(inputs["L0_ffn_b1"], np.float32)),
        "l0f_b2": c(np.asarray(inputs["L0_ffn_b2"], np.float32)),
        "l1a_wqkvt": c(np.asarray(inputs["L1_attn_w_qkv"], np.float32).T),
        "l1a_wprojt": c(np.asarray(inputs["L1_attn_w_proj"], np.float32).T),
        "l1f_k0": c(np.asarray(inputs["L1_ffn_k0"])).view(np.uint8),
        "l1f_k1": c(np.asarray(inputs["L1_ffn_k1"])).view(np.uint8),
        "l1f_k2": c(np.asarray(inputs["L1_ffn_k2"])).view(np.uint8),
        "l1f_s0": np.asarray(inputs["L1_ffn_s0"], np.float32).reshape(1, 1),
        "l1f_s1": np.asarray(inputs["L1_ffn_s1"], np.float32).reshape(1, 1),
        "l1f_s2": np.asarray(inputs["L1_ffn_s2"], np.float32).reshape(1, 1),
    }
    w.update(host_consts())
    return w


def kernel(**inputs):
    T = 4096
    x = np.asarray(inputs["L0_attn_x"], np.float32)
    w = host_weights(inputs)
    nc, names = build_program(PARTS, T)
    in_maps = []
    for ci in range(NCORES):
        m = {k: v for k, v in w.items() if k in names}
        m["xT"] = np.ascontiguousarray(x[ci].T)
        in_maps.append(m)
    res = run_bass_kernel_spmd(nc, in_maps, core_ids=list(range(NCORES)))
    out = np.stack([np.asarray(res.results[ci]["yT"]).T for ci in range(NCORES)])
    return out.astype(np.float32)


# version 12
# speedup vs baseline: 1.0319x; 1.0319x over previous
import contextlib
import numpy as np
import ml_dtypes
import concourse.bass as bass
import concourse.mybir as mybir
from concourse.bass_utils import run_bass_kernel_spmd
from concourse.alu_op_type import AluOpType as OP

F32 = mybir.dt.float32
F32R = mybir.dt.float32r
BF16 = mybir.dt.bfloat16
U8 = mybir.dt.uint8
AF = mybir.ActivationFunctionType
AX = mybir.AxisListType

DEBUG_L1 = False
D = 1024
DC = 8
F1 = 4096
FC = 32
H = 16
NP = 8
TT = 512
NCORES = 8

ENGINES = ("sync", "gpsimd", "tensor", "vector", "scalar")
LANES = {"da": "sync", "db": "sync", "dc": "sync", "dd": "sync",
         "ga": "gpsimd", "gb": "gpsimd", "gc": "gpsimd", "aq": "scalar"}


class E:
    def __init__(self, nc, cur, handle, sems, cnt):
        self.nc = nc
        self.cur = cur
        self.h = handle
        self.sems = sems
        self.c = cnt

    def wait(self, who, sem, val):
        if self.cur == who and val > 0:
            self.h.wait_ge(self.sems[sem], val)

    def _emit(self, who, sem, meth, *a, **kw):
        self.c[sem] += 1
        if self.cur == who:
            getattr(self.h, meth)(*a, **kw).then_inc(self.sems[sem], 1)
        return self.c[sem]

    def mm(self, out, lhsT, rhs, **kw):
        return self._emit("tensor", "t", "matmul", out, lhsT, rhs, **kw)

    def v(self, meth, *a, **kw):
        return self._emit("vector", "v", meth, *a, **kw)

    def s(self, meth, *a, **kw):
        return self._emit("scalar", "s", meth, *a, **kw)

    def dma(self, lane, out, in_, nowait=False, **kw):
        prev = self.c[lane]
        self.c[lane] += 16
        if self.cur == LANES[lane]:
            if prev > 0 and not nowait:
                self.h.wait_ge(self.sems[lane], prev)
            self.h.dma_start(out, in_, **kw).then_inc(self.sems[lane], 16)
        return self.c[lane]


def drain_lanes(E_):
    for lane, eng in LANES.items():
        E_.wait(eng, lane, E_.c[lane])


def run_phase(nc, sems, cnt, fn):
    finals = []
    snap = dict(cnt)
    with nc.Block() as block:
        for name in ENGINES:
            def mk(n):
                def f(h):
                    local = dict(snap)
                    fn(E(nc, n, h, sems, local))
                    drain_lanes(E(nc, n, h, sems, local))
                    finals.append(local)
                return f
            getattr(block, name)(mk(name))
    for f in finals[1:]:
        assert f == finals[0], f"engine passes disagree: {finals[0]} vs {f}"
    cnt.clear()
    cnt.update(finals[0])


def r32(ap):
    return ap.bitcast(F32R)


def x_tile_ap(x_dram, T, tt):
    return bass.AP(x_dram.tensor, x_dram.offset + tt * TT,
                   [[T, 128], [128 * T, DC], [1, TT]])


def bias_ap(b_dram, c):
    return bass.AP(b_dram.tensor, b_dram.offset + c * 128, [[1, 128], [1, 1]])


def chunk_ap(w_dram, c, nfree, w=None):
    return bass.AP(w_dram.tensor, w_dram.offset + c * 128 * nfree,
                   [[nfree, 128], [1, nfree if w is None else w]])


def flat_ap(t_dram, n):
    return bass.AP(t_dram.tensor, t_dram.offset, [[1, n]])


class Res:

    def __init__(self, nc, ctx, tag):
        sb = lambda n, s, d: ctx.enter_context(nc.sbuf_tensor(tag + n, s, d))
        ps = lambda n, w: ctx.enter_context(nc.psum_tensor(tag + n, [128, w], F32))
        self.PM0 = ps("_PM0", 512)
        self.PM1 = ps("_PM1", 512)
        self.PSa = ps("_PSa", 1024)
        self.PSb = ps("_PSb", 1024)
        self.PU4 = ps("_PU4", 512)
        self.PU5 = ps("_PU5", 512)
        self.y = sb("_y", [128, DC * TT], F32)
        self.xsq = [sb("_xsq0", [128, TT], BF16), sb("_xsq1", [128, TT], BF16)]
        self.xb = [sb("_xb0", [128, TT], BF16), sb("_xb1", [128, TT], BF16)]
        self.th = sb("_th", [128, TT], F32)
        self.ones128 = sb("_o128", [128, 128], BF16)
        self.ones128f = sb("_o128f", [128, 128], F32)
        self.ones1 = sb("_o1", [33, 128], F32)
        self.onesc = sb("_oc", [128, 1], BF16)
        self.eps11 = sb("_eps", [1, 1], F32)
        self.rmu = sb("_rmu", [1, TT], F32)
        self.rt1 = sb("_rt1", [1, TT], F32)
        self.rt2 = sb("_rt2", [1, TT], F32)

    def init_consts(self, E_):
        E_.v("memset", self.ones128[:], 1.0)
        E_.v("memset", self.ones128f[:], 1.0)
        E_.v("memset", self.ones1[:], 1.0)
        E_.v("memset", self.onesc[:], 1.0)
        return E_.v("memset", self.eps11[:], 1e-5)

    def ln_rows(self, E_, src_sb, PA, PB, pa_t_free, pa_v_free):
        E_.wait("tensor", "v", pa_v_free)
        E_.wait("tensor", "t", pa_t_free)
        xg = [E_.c["t"], E_.c["t"]]
        for c in range(DC):
            sl = src_sb[:, c * TT:(c + 1) * TT]
            xq = self.xsq[c % 2]
            xb = self.xb[c % 2]
            E_.wait("vector", "t", xg[c % 2])
            vcb = E_.v("tensor_copy", xb[:], sl)
            E_.wait("scalar", "t", xg[c % 2])
            ssq = E_.s("activation", xq[:], sl, AF.Square)
            E_.wait("tensor", "v", vcb)
            E_.mm(PA[:, 0:TT], self.ones128[:], xb[:],
                  start=(c == 0), stop=(c == DC - 1))
            E_.wait("tensor", "s", ssq)
            tmm = E_.mm(PB[:, 0:TT], self.ones128[:], xq[:],
                        start=(c == 0), stop=(c == DC - 1))
            xg[c % 2] = tmm
        nfeat = float(D)
        E_.wait("vector", "t", tmm)
        v1 = E_.v("tensor_scalar", self.rmu[:], PA[0:1, 0:TT], 1.0 / nfeat,
                  None, OP.mult, OP.bypass)
        E_.wait("vector", "v", v1)
        v2 = E_.v("tensor_tensor", self.rt1[:], self.rmu[:], self.rmu[:], OP.mult)
        E_.wait("vector", "v", v2)
        v3 = E_.v("scalar_tensor_tensor", self.rt2[:], PB[0:1, 0:TT],
                  1.0 / nfeat, self.rt1[:], OP.mult, OP.subtract)
        E_.wait("scalar", "v", v3)
        s1 = E_.s("activation", self.rt1[:], self.rt2[:], AF.Ln,
                  bias=self.eps11[:], scale=1.0)
        E_.wait("scalar", "s", s1)
        s2 = E_.s("activation", self.rt2[:], self.rt1[:], AF.Exp,
                  scale=-0.5)
        E_.wait("vector", "s", s2)
        v5 = E_.v("scalar_tensor_tensor", self.rt1[:], self.rmu[:], -1.0,
                  self.rt2[:], OP.mult, OP.mult)
        E_.wait("tensor", "v", v5)
        E_.mm(PA[:, 0:TT], self.ones1[0:1, :], self.rt2[:],
              start=True, stop=True)
        tb = E_.mm(PB[:, 0:TT], self.ones1[0:1, :], self.rt1[:],
                   start=True, stop=True)
        return tb

    def ln_apply(self, E_, src_sb, dst_sb, PA, PB, t_bcast):
        E_.wait("vector", "t", t_bcast)
        for c in range(DC):
            sl = src_sb[:, c * TT:(c + 1) * TT]
            dl = dst_sb[:, c * TT:(c + 1) * TT]
            va = E_.v("tensor_tensor", self.th[:], sl, PA[:, 0:TT], OP.mult)
            E_.wait("vector", "v", va)
            last = E_.v("tensor_tensor", dl, self.th[:], PB[:, 0:TT], OP.add)
            E_.wait("vector", "v", last)
        return last

    def finish_tile(self, E_, xout, T, tt, last_s, PA, PB, t_free, v_free):
        E_.wait("vector", "s", last_s)
        E_.wait("scalar", "s", last_s)
        E_.wait("tensor", "s", last_s)
        tb = self.ln_rows(E_, self.y, PA, PB, max(t_free, E_.c["t"]),
                          max(v_free, E_.c["v"]))
        vl = self.ln_apply(E_, self.y, self.y, PA, PB, tb)
        E_.wait("gpsimd", "v", vl)
        gd = E_.dma("ga", x_tile_ap(xout, T, tt), self.y[:], accum_op=OP.add)
        return vl, gd


def copy_x_phase(nc, sems, cnt, xin, xout, T):
    def cp(E_):
        gd = E_.dma("ga", flat_ap(xout, D * T), flat_ap(xin, D * T))
        E_.wait("gpsimd", "ga", gd)
    run_phase(nc, sems, cnt, cp)



def build_l0ffn(nc, sems, cnt, inp, xin, xout, T):
    NTT = T // TT
    W1T = inp("l0f_w1t", [D, F1], F32)
    W2T = inp("l0f_w2t", [F1, D], F32)
    b1 = inp("l0f_b1", [F1], F32)
    b2 = inp("l0f_b2", [D], F32)
    w2scr = nc.dram_tensor("w2scr", [FC * 128, D], BF16).ap()

    with contextlib.ExitStack() as ctx:
        sb = lambda n, s, d: ctx.enter_context(nc.sbuf_tensor("f0" + n, s, d))
        R = Res(nc, ctx, "f0")
        W1q = sb("_w1q", [128, DC * F1], BF16)
        hbuf = sb("_h", [128, FC * TT], BF16)
        b1t = sb("_b1t", [128, FC], F32)
        b2t = sb("_b2t", [128, DC], F32)

        copy_x_phase(nc, sems, cnt, xin, xout, T)

        with contextlib.ExitStack() as cpp:
            p_ = lambda n, s, d: cpp.enter_context(nc.sbuf_tensor("f0p" + n, s, d))
            wls = [p_("_wl0", [128, F1], F32), p_("_wl1", [128, F1], F32)]
            wsg = p_("_wsg", [128, 1024], F32)
            wab = p_("_wab", [128, 1024], F32)
            w2qs = [p_("_w2q0", [128, 1024], BF16), p_("_w2q1", [128, 1024], BF16)]
            acc = p_("_acc", [128, 1], F32)
            part = p_("_part", [128, 1], F32)
            thr = p_("_thr", [128, 1], F32)

            def prep(E_):
                R.init_consts(E_)
                for c in range(FC):
                    E_.dma("da", b1t[:, c:c + 1], bias_ap(b1, c))
                for c in range(DC):
                    E_.dma("da", b2t[:, c:c + 1], bias_ap(b2, c))
                wg_v = [0, 0]
                wg_s = [0, 0]
                st_d = [0, 0]
                q_guard = 0
                ld = [0, 0]
                lane_of = ("da", "db")

                def wload(WT, c, nfree):
                    buf = c % 2
                    E_.wait("sync", "v", wg_v[buf])
                    E_.wait("sync", "s", wg_s[buf])
                    ld[buf] = E_.dma(lane_of[buf], wls[buf][:, 0:nfree],
                                     chunk_ap(WT, c, nfree))

                for wi, (WT, nch, nfree) in enumerate([(W1T, DC, F1), (W2T, FC, D)]):
                    vacc = E_.v("memset", acc[:], 0.0)
                    wload(WT, 0, nfree)
                    for c in range(nch):
                        buf = c % 2
                        if c + 1 < nch:
                            wload(WT, c + 1, nfree)
                        E_.wait("vector", lane_of[buf], ld[buf])
                        E_.wait("vector", "v", vacc)
                        vp = E_.v("tensor_reduce", part[:], wls[buf][:, 0:nfree],
                                  AX.X, OP.add, apply_absolute_value=True)
                        E_.wait("vector", "v", vp)
                        vacc = E_.v("tensor_tensor", acc[:], acc[:], part[:], OP.add)
                        wg_v[buf] = vp
                    E_.wait("tensor", "v", vacc)
                    tg = E_.mm(R.PM0[:, 0:1], R.ones128f[:], acc[:],
                               start=True, stop=True)
                    E_.wait("vector", "t", tg)
                    nel = float(nch * 128 * nfree)
                    vthr = E_.v("tensor_scalar", thr[:], R.PM0[:, 0:1],
                                0.5 / nel, 0.5e-5, OP.mult, OP.add)
                    wload(WT, 0, nfree)
                    for c in range(nch):
                        buf = c % 2
                        if c + 1 < nch:
                            wload(WT, c + 1, nfree)
                        for q4 in range(nfree // 1024):
                            wlsl = wls[buf][:, q4 * 1024:(q4 + 1) * 1024]
                            E_.wait("scalar", lane_of[buf], ld[buf])
                            E_.wait("scalar", "v", q_guard)
                            ssg = E_.s("sign", wsg[:], wlsl)
                            sab = E_.s("activation", wab[:], wlsl, AF.Abs)
                            E_.wait("vector", "s", sab)
                            E_.wait("vector", "v", vthr)
                            vc = E_.v("tensor_scalar", wab[:], wab[:], thr[:], None,
                                      OP.is_ge, OP.bypass)
                            E_.wait("vector", "v", vc)
                            E_.wait("vector", "s", ssg)
                            if wi == 0:
                                dst = W1q[:, c * F1 + q4 * 1024:
                                          c * F1 + (q4 + 1) * 1024]
                            else:
                                dst = w2qs[buf][:]
                                E_.wait("vector", "dc" if buf == 0 else "dd",
                                        st_d[buf])
                            vq = E_.v("tensor_tensor", dst, wab[:], wsg[:], OP.mult)
                            E_.wait("vector", "v", vq)
                            if wi == 1:
                                E_.wait("sync", "v", vq)
                                st_d[buf] = E_.dma("dc" if buf == 0 else "dd",
                                                   chunk_ap(w2scr, c, D),
                                                   w2qs[buf][:])
                            wg_s[buf] = sab
                            q_guard = vq
            run_phase(nc, sems, cnt, prep)

        with contextlib.ExitStack() as cm:
            m_ = lambda n, s, d: cm.enter_context(nc.sbuf_tensor("f0m" + n, s, d))
            xt = m_("_xt", [128, DC * TT], F32)
            xhat = m_("_xhat", [128, DC * TT], BF16)
            w2sl = [m_("_w2a", [128, FC * 128], BF16),
                    m_("_w2b", [128, FC * 128], BF16)]

            def main(E_):
                PM = (R.PM0, R.PM1)
                pm_s = [0, 0]
                h_read_t = 0
                w2_t = [0, 0]
                w2_d = [0, 0]
                ln_t = 0
                ln_v = 0
                for tt in range(NTT):
                    E_.wait("sync", "v", ln_v)
                    dd = E_.dma("da", xt[:], x_tile_ap(xin, T, tt))
                    for eng in ("tensor", "vector", "scalar"):
                        E_.wait(eng, "da", dd)
                    tb = R.ln_rows(E_, xt, R.PU4, R.PU5, ln_t, ln_v)
                    va = R.ln_apply(E_, xt, xhat, R.PU4, R.PU5, tb)
                    ln_v = va
                    E_.wait("tensor", "v", va)
                    evac_last = 0
                    for fc in range(FC):
                        P = PM[fc % 2]
                        E_.wait("tensor", "s", pm_s[fc % 2])
                        E_.wait("tensor", "t", h_read_t)
                        for dc in range(DC):
                            lhs = W1q[:, dc * F1 + fc * 128: dc * F1 + fc * 128 + 128]
                            tm = E_.mm(P[:, 0:TT], lhs,
                                       xhat[:, dc * TT:(dc + 1) * TT],
                                       start=(dc == 0), stop=(dc == DC - 1))
                        E_.wait("scalar", "t", tm)
                        se = E_.s("activation", hbuf[:, fc * TT:(fc + 1) * TT],
                                  P[:, 0:TT], AF.Relu, bias=b1t[:, fc:fc + 1],
                                  scale=1.0)
                        pm_s[fc % 2] = se
                        evac_last = se
                    for oc in range(DC):
                        buf = oc % 2
                        E_.wait("sync", "t", w2_t[buf])
                        w2ap = bass.AP(w2scr.tensor, w2scr.offset + oc * 128,
                                       [[D, 128], [128 * D, FC], [1, 128]])
                        w2_d[buf] = E_.dma("da", w2sl[buf][:], w2ap)
                        P = PM[oc % 2]
                        E_.wait("tensor", "s", pm_s[oc % 2])
                        E_.wait("tensor", "da", w2_d[buf])
                        E_.wait("tensor", "s", evac_last)
                        for fc in range(FC):
                            tm = E_.mm(P[:, 0:TT],
                                       w2sl[buf][:, fc * 128:(fc + 1) * 128],
                                       hbuf[:, fc * TT:(fc + 1) * TT],
                                       start=(fc == 0), stop=(fc == FC - 1))
                        w2_t[buf] = tm
                        E_.wait("scalar", "t", tm)
                        se = E_.s("activation", R.y[:, oc * TT:(oc + 1) * TT],
                                  P[:, 0:TT], AF.Identity, bias=b2t[:, oc:oc + 1],
                                  scale=1.0)
                        pm_s[oc % 2] = se
                    h_read_t = tm
                    vl, gd = R.finish_tile(E_, xout, T, tt, se, R.PU4, R.PU5,
                                           0, ln_v)
                    ln_v = vl
                    ln_t = E_.c["t"]
                fin = E_.c["ga"]
                E_.wait("gpsimd", "ga", fin)
            run_phase(nc, sems, cnt, main)



def build_attn(nc, sems, cnt, inp, xin, xout, T, layer):
    NTT = T // TT
    NTC = T // 128
    if layer == 0:
        Wq = inp("l0a_wq", [D, D], F32)
        Wk = inp("l0a_wk", [D, D], F32)
        Wv = inp("l0a_wv", [D, D], F32)
        Wo = inp("l0a_wo", [D, D], F32)
        bq = inp("l0a_bq", [D], F32)
        bk = inp("l0a_bk", [D], F32)
        bv = inp("l0a_bv", [D], F32)
        bo = inp("l0a_bo", [D], F32)
    else:
        WqkvT = inp("l1a_wqkvt", [D, 3 * D], F32)
        WprojT = inp("l1a_wprojt", [D, D], F32)
        pmask = inp("pmask", [128, 896], BF16)
    emask2 = inp("emask2", [128, 896], BF16)
    osel16 = inp("osel16", [16, NP * 128], BF16)
    if layer == 1 and DEBUG_L1:
        dbg = nc.dram_tensor("dbg", [4, TT], F32, kind="ExternalOutput").ap()
        dbg2 = nc.dram_tensor("dbg2", [64, TT], BF16, kind="ExternalOutput").ap()
    else:
        dbg = None
    qscr = nc.dram_tensor(f"qscr{layer}", [NP * 128, T], BF16).ap()
    wscr = nc.dram_tensor(f"wscr{layer}", [3 * NP * DC * 128, 128], BF16).ap()
    WS_P = DC * 128 * 128

    def wscr_lhs_ap(which, p):
        off = (which * NP + p) * WS_P
        return bass.AP(wscr.tensor, wscr.offset + off,
                       [[128, 128], [128 * 128, DC], [1, 128]])

    def wscr_v_ap(half, dc):
        off = 2 * NP * WS_P + (half * 4) * WS_P + dc * 128 * 128
        return bass.AP(wscr.tensor, wscr.offset + off,
                       [[128, 128], [WS_P, 4], [1, 128]])

    with contextlib.ExitStack() as ctx:
        sb = lambda n, s, d: ctx.enter_context(nc.sbuf_tensor(f"a{layer}" + n, s, d))
        KT = sb("_kt", [128, NP * T], BF16)
        V = sb("_v", [128, NTC * H * 65], BF16)
        R = Res(nc, ctx, f"a{layer}")
        if layer == 1:
            mscr = sb("_mscr", [128, NTC * 16], F32)

        def vsl(tc, h, w=65):
            return V[:, tc * H * 65 + h * 65: tc * H * 65 + h * 65 + w]

        copy_x_phase(nc, sems, cnt, xin, xout, T)

        with contextlib.ExitStack() as cw:
            w_ = lambda n, s, d: cw.enter_context(nc.sbuf_tensor(f"a{layer}w" + n, s, d))
            wl = w_("_wl", [128, 1024], F32)
            wb = w_("_wb", [128, 1024], BF16)

            def wprep(E_):
                R.init_consts(E_)
                prev_v = 0
                prev_d = 0
                for dc in range(DC):
                    for which in range(3):
                        if layer == 0:
                            W = (Wq, Wk, Wv)[which]
                            src = chunk_ap(W, dc, D)
                        else:
                            src = bass.AP(WqkvT.tensor,
                                          WqkvT.offset + dc * 128 * 3 * D + which * D,
                                          [[3 * D, 128], [1, D]])
                        E_.wait("sync", "v", prev_v)
                        dd = E_.dma("da", wl[:], src)
                        E_.wait("vector", "da", dd)
                        E_.wait("vector", "da", prev_d)
                        vq = E_.v("tensor_copy", wb[:], wl[:])
                        E_.wait("sync", "v", vq)
                        dst = bass.AP(wscr.tensor,
                                      wscr.offset + (which * NP) * WS_P + dc * 128 * 128,
                                      [[128, 128], [WS_P, NP], [1, 128]])
                        prev_d = E_.dma("da", dst, wb[:].rearrange(
                            "q (p c) -> q p c", c=128))
                        prev_v = vq
                E_.wait("sync", "da", prev_d)
            run_phase(nc, sems, cnt, wprep)

        with contextlib.ExitStack() as c2:
            p_ = lambda n, s, d: c2.enter_context(nc.sbuf_tensor(f"a{layer}p" + n, s, d))
            xt = p_("_xt", [128, DC * TT], F32)
            xhat = p_("_xhat", [128, DC * TT], BF16)
            wkq = [p_("_wkq0", [128, DC * 128], BF16),
                   p_("_wkq1", [128, DC * 128], BF16)]
            wv2 = [p_("_wv0", [128, 512], BF16), p_("_wv1", [128, 512], BF16)]
            qtmp = p_("_qtmp", [128, TT], BF16)
            bt = p_("_bt", [128, 2 * DC], F32)

            def proj(E_):
                if layer == 0:
                    for c in range(DC):
                        E_.dma("da", bt[:, c:c + 1], bias_ap(bq, c))
                        dd = E_.dma("da", bt[:, DC + c:DC + c + 1], bias_ap(bk, c))
                    E_.wait("scalar", "da", dd)
                PM = (R.PM0, R.PM1)
                bank_s = [0, 0, 0, 0]
                pm_s = bank_s
                w_t = [0, 0]
                w_d = [0, 0]
                q_aq = 0
                ln_t = 0
                ln_v = 0
                for tt in range(NTT):
                    E_.wait("sync", "v", ln_v)
                    dd = E_.dma("da", xt[:], x_tile_ap(xin, T, tt))
                    for eng in ("tensor", "vector", "scalar"):
                        E_.wait(eng, "da", dd)
                    tb = R.ln_rows(E_, xt, R.PU4, R.PU5, ln_t, ln_v)
                    ln_v = R.ln_apply(E_, xt, xhat, R.PU4, R.PU5, tb)
                    E_.wait("tensor", "v", ln_v)
                    for wi, which in enumerate((1, 0)):
                        for p in range(NP):
                            buf = p % 2
                            E_.wait("sync", "t", w_t[buf])
                            w_d[buf] = E_.dma("da", wkq[buf][:], wscr_lhs_ap(which, p))
                            P = PM[p % 2]
                            E_.wait("tensor", "s", pm_s[p % 2])
                            E_.wait("tensor", "da", w_d[buf])
                            for dc in range(DC):
                                tm = E_.mm(P[:, 0:TT],
                                           wkq[buf][:, dc * 128:(dc + 1) * 128],
                                           xhat[:, dc * TT:(dc + 1) * TT],
                                           start=(dc == 0), stop=(dc == DC - 1))
                            w_t[buf] = tm
                            E_.wait("scalar", "t", tm)
                            if which == 1:
                                dst = KT[:, p * T + tt * TT: p * T + (tt + 1) * TT]
                                if layer == 0:
                                    se = E_.s("activation", dst, P[:, 0:TT],
                                              AF.Identity,
                                              bias=bt[:, DC + p:DC + p + 1], scale=1.0)
                                else:
                                    se = E_.s("copy", dst, P[:, 0:TT])
                            else:
                                E_.wait("scalar", "aq", q_aq)
                                if layer == 0:
                                    se = E_.s("activation", qtmp[:], P[:, 0:TT],
                                              AF.Identity, bias=bt[:, p:p + 1],
                                              scale=1.0)
                                else:
                                    se = E_.s("copy", qtmp[:], P[:, 0:TT])
                                qap = bass.AP(qscr.tensor,
                                              qscr.offset + p * 128 * T + tt * TT,
                                              [[T, 128], [1, TT]])
                                E_.wait("scalar", "s", se)
                                q_aq = E_.dma("aq", qap, qtmp[:])
                            pm_s[p % 2] = se
                    P4 = (R.PM0, R.PM1, R.PSa, R.PSb)
                    vm = E_.v("memset",
                              V[:, tt * 4 * H * 65:(tt + 1) * 4 * H * 65], 1.0)
                    for half in range(2):
                        for t4 in range(4):
                            E_.wait("tensor", "s", bank_s[t4])
                        E_.wait("tensor", "v", vm)
                        for dc in range(DC):
                            buf = dc % 2
                            lane = "db" if buf == 0 else "dc"
                            E_.wait("sync", "t", w_t[buf])
                            w_d[buf] = E_.dma(lane, wv2[buf][:], wscr_v_ap(half, dc))
                            E_.wait("tensor", lane, w_d[buf])
                            for t4 in range(4):
                                tm = E_.mm(P4[t4][:, 0:512],
                                           xhat[:, dc * TT + t4 * 128:
                                                dc * TT + t4 * 128 + 128],
                                           wv2[buf][:],
                                           start=(dc == 0), stop=(dc == DC - 1))
                            w_t[buf] = tm
                        E_.wait("scalar", "t", tm)
                        for t4 in range(4):
                            tc = tt * 4 + t4
                            dst = V[:, tc * H * 65 + half * 8 * 65:
                                    tc * H * 65 + (half + 1) * 8 * 65]
                            dst = dst.rearrange("p (h d) -> p h d", d=65)[:, :, 0:64]
                            se = E_.s("copy", dst, P4[t4][:, 0:512])
                            bank_s[t4] = se
                    E_.wait("tensor", "s", se)
                    ln_t = E_.c["t"]
            run_phase(nc, sems, cnt, proj)

        if layer == 1:
            with contextlib.ExitStack() as c4:
                p_ = lambda n, s, d: c4.enter_context(
                    nc.sbuf_tensor(f"a{layer}m" + n, s, d))
                pmaskT = p_("_pm", [128, 896], BF16)
                qt1 = [p_("_qt1a", [128, TT], BF16),
                       p_("_qt1b", [128, TT], BF16)]
                prt = p_("_p1", [128, 16], F32)

                def pass1(E_):
                    dd = E_.dma("da", pmaskT[:], pmask)
                    E_.wait("vector", "da", dd)
                    PS4 = (R.PSa[:, 0:512], R.PSa[:, 512:1024],
                           R.PSb[:, 0:512], R.PSb[:, 512:1024])
                    ps_v = [0, 0, 0, 0]
                    q_t = 0
                    fin_v = [0, 0]
                    th_v = 0
                    qd1 = [0, 0]
                    quse1 = [0, 0]

                    def q1_load(i):
                        p_2, qt_ = divmod(i, NTT)
                        buf = i % 2
                        lane = "db" if buf == 0 else "dc"
                        qap = bass.AP(qscr.tensor,
                                      qscr.offset + p_2 * 128 * T + qt_ * TT,
                                      [[T, 128], [1, TT]])
                        E_.wait("sync", "t", quse1[buf])
                        qd1[buf] = E_.dma(lane, qt1[buf][:], qap)

                    for p in range(NP):
                        for qt in range(NTT):
                            i = p * NTT + qt
                            if i == 0:
                                q1_load(0)
                            qtile1 = qt1[i % 2]
                            qlane = "db" if i % 2 == 0 else "dc"
                            E_.wait("tensor", qlane, qd1[i % 2])
                            if i + 1 < NP * NTT:
                                q1_load(i + 1)
                            for qs in range(4):
                                qsg = qt * 4 + qs
                                vr2 = [0, 0]
                                for kt in range(qt + 1):
                                    bi = (2 * kt) % 4
                                    PA, PB = PS4[bi], PS4[bi + 1]
                                    E_.wait("tensor", "v", ps_v[bi])
                                    E_.wait("tensor", "v", ps_v[bi + 1])
                                    ta = E_.mm(PA, qtile1[0:64, qs * 128:(qs + 1) * 128],
                                               KT[0:64, p * T + kt * TT:
                                                  p * T + (kt + 1) * TT],
                                               start=True, stop=True,
                                               tile_position=(0, 0))
                                    tb_ = E_.mm(PB,
                                                qtile1[64:128, qs * 128:(qs + 1) * 128],
                                                KT[64:128, p * T + kt * TT:
                                                   p * T + (kt + 1) * TT],
                                                start=True, stop=True,
                                                tile_position=(64, 0))
                                    q_t = tb_
                                    for hh, (P, td) in enumerate(((PA, ta), (PB, tb_))):
                                        E_.wait("vector", "t", td)
                                        E_.wait("vector", "v", fin_v[hh])
                                        if kt == qt:
                                            E_.wait("vector", "v", th_v)
                                            pms = bass.AP(pmaskT[:].tensor,
                                                          384 - 128 * qs,
                                                          [[896, 128], [1, TT]])
                                            vmk = E_.v("tensor_tensor", R.th[:], P,
                                                       pms, OP.min)
                                            E_.wait("vector", "v", vmk)
                                            vr = E_.v("tensor_reduce",
                                                      prt[:, hh * 8 + kt:
                                                          hh * 8 + kt + 1],
                                                      R.th[:], AX.X, OP.max)
                                            th_v = vr
                                        else:
                                            vr = E_.v("tensor_reduce",
                                                      prt[:, hh * 8 + kt:
                                                          hh * 8 + kt + 1],
                                                      P, AX.X, OP.max)
                                        vr2[hh] = vr
                                        ps_v[bi + hh] = vr
                                for hh in range(2):
                                    col = qsg * 16 + 2 * p + hh
                                    E_.wait("vector", "v", vr2[hh])
                                    vf = E_.v("tensor_reduce",
                                              mscr[:, col:col + 1],
                                              prt[:, hh * 8:hh * 8 + qt + 1],
                                              AX.X, OP.max)
                                    fin_v[hh] = vf
                            quse1[i % 2] = q_t
                run_phase(nc, sems, cnt, pass1)

        with contextlib.ExitStack() as c3:
            p_ = lambda n, s, d: c3.enter_context(nc.sbuf_tensor(f"a{layer}t" + n, s, d))
            WO = p_("_wo", [128, DC * D], BF16)
            stag = p_("_stag", [128, NP * TT], BF16)
            qtl = [p_("_qt0", [128, TT], BF16), p_("_qt1", [128, TT], BF16)]
            EAB = [p_("_e0", [128, 1024], BF16), p_("_e1", [128, 1024], BF16),
                   p_("_e2", [128, 1024], BF16)]
            srowA = R.y[0:1, 0:TT]
            srowB = R.y[0:1, TT:2 * TT]
            Scol = p_("_scl", [16, TT], F32)
            chT = R.y[0:16, 2 * TT:3 * TT]
            c1m = p_("_c1m", [16, TT], BF16)
            osel = p_("_osl", [16, NP * 128], BF16)
            emaskT = p_("_em", [128, 896], BF16)
            if layer == 0:
                boP = p_("_bop", [128, DC], F32)
                bvb = p_("_bvb", [128, DC], BF16)
            else:
                Mcol = p_("_mcl", [16, TT], F32)
                c2m = p_("_c2m", [16, TT], BF16)
                wsel = p_("_wsl", [16, NP * 128], BF16)
                wcolp = p_("_wcp", [1, H * 64], BF16)

            with contextlib.ExitStack() as cw2:
                wl2 = R.y[:, 0:D // 2]

                def wo_prep(E_):
                    src = Wo if layer == 0 else WprojT
                    prev = 0
                    for c2 in range(2 * DC):
                        c, hf = c2 // 2, c2 % 2
                        E_.wait("sync", "v", prev)
                        dd = E_.dma("da", wl2, bass.AP(
                            src.tensor, src.offset + c * 128 * D + hf * (D // 2),
                            [[D, 128], [1, D // 2]]))
                        E_.wait("vector", "da", dd)
                        prev = E_.v("tensor_copy",
                                    WO[:, c * D + hf * (D // 2):
                                       c * D + (hf + 1) * (D // 2)], wl2)
                        E_.wait("vector", "v", prev)
                    E_.dma("da", osel[:], osel16)
                    dd = E_.dma("da", emaskT[:], emask2)
                    for eng in ("tensor", "vector", "scalar"):
                        E_.wait(eng, "da", dd)
                    if layer == 0:
                        vq = prev
                        for c in range(DC):
                            E_.wait("sync", "v", vq)
                            dd = E_.dma("da", R.y[:, 0:1], bias_ap(bv, c))
                            E_.wait("vector", "da", dd)
                            vq = E_.v("tensor_copy", bvb[:, c:c + 1], R.y[:, 0:1])
                            E_.wait("vector", "v", vq)
                            E_.dma("da", boP[:, c:c + 1], bias_ap(bo, c))
                        dd = E_.c["da"]
                        E_.wait("tensor", "da", dd)
                        E_.wait("tensor", "v", vq)
                        for oc in range(DC):
                            for dc in range(DC):
                                tm = E_.mm(R.PM0[:, oc:oc + 1],
                                           WO[:, dc * D + oc * 128:
                                              dc * D + oc * 128 + 128],
                                           bvb[:, dc:dc + 1],
                                           start=(dc == 0), stop=(dc == DC - 1))
                        E_.wait("vector", "t", tm)
                        E_.wait("vector", "da", dd)
                        vb2 = E_.v("tensor_tensor", boP[:, 0:DC], boP[:, 0:DC],
                                   R.PM0[:, 0:DC], OP.add)
                        E_.wait("tensor", "v", vb2)
                        E_.wait("scalar", "v", vb2)
                    else:
                        vz = E_.v("memset", wcolp[:], 0.0)
                        pb_v = vz
                        for h in range(H):
                            for tc in range(NTC):
                                E_.wait("tensor", "v", pb_v)
                                tm = E_.mm(R.PM0[0:1, 0:64], R.onesc[:],
                                           vsl(tc, h, 64),
                                           start=(tc == 0), stop=(tc == NTC - 1))
                            E_.wait("vector", "t", tm)
                            pb_v = E_.v("tensor_copy",
                                        wcolp[:, h * 64:(h + 1) * 64],
                                        R.PM0[0:1, 0:64])
                            E_.wait("vector", "v", pb_v)
                        E_.wait("tensor", "v", pb_v)
                        vws = E_.v("memset", wsel[:], 0.0)
                        E_.wait("scalar", "v", vws)
                        E_.wait("scalar", "v", pb_v)
                        for h in range(H):
                            c0 = (h // 2) * 128 + 64 * (h % 2)
                            E_.dma("aq", wsel[h:h + 1, c0:c0 + 64],
                                   wcolp[:, h * 64:(h + 1) * 64])
                run_phase(nc, sems, cnt, wo_prep)

            def attn(E_):
                PM = (R.PM0, R.PM1)
                PSS = (R.PSa, R.PSb)
                pm_s = [0, 0]
                pm_v = [0, 0]
                pu_s = 0
                pu_vv = 0
                q_t = 0
                fin_g = 0
                esPM = 0
                proj_t = 0
                scol_gate = 0
                cht_gate = 0
                mcol_gate = 0
                scol_d = 0
                mcol_d = 0
                c1_rdy = 0
                c1_rdy_s = 0
                c2_rdy = 0
                qd = [0, 0]
                quse = [0, 0]

                def q_load(i):
                    qt_, p_2 = divmod(i, NP)
                    buf = i % 2
                    lane = "db" if buf == 0 else "dc"
                    qap = bass.AP(qscr.tensor,
                                  qscr.offset + p_2 * 128 * T + qt_ * TT,
                                  [[T, 128], [1, TT]])
                    E_.wait("sync", "t", quse[buf])
                    qd[buf] = E_.dma(lane, qtl[buf][:], qap)

                PVb = ((R.PU4, R.PU5), (R.PU4, R.PU5))
                pu_free_s = [0, 0]
                pu_free_v = [0, 0]
                for qt in range(NTT):
                    pu_free_s[1] = max(pu_free_s[1], pm_s[0], pm_s[1])
                    pu_free_v[1] = max(pu_free_v[1], pm_v[0], pm_v[1])
                    if layer == 1:
                        E_.wait("sync", "s", scol_gate)
                        E_.wait("sync", "v", mcol_gate)
                        first = True
                        for h in range(H):
                            for qs in range(4):
                                col = (qt * 4 + qs) * 16 + h
                                mcol_d = E_.dma(
                                    "da",
                                    Mcol[h:h + 1, qs * 128:(qs + 1) * 128],
                                    mscr[:, col:col + 1], nowait=not first)
                                first = False
                    for p in range(NP):
                        par = 0
                        bkA, bkB = PVb[par]
                        qi = qt * NP + p
                        if qi == 0:
                            q_load(0)
                        qtile = qtl[qi % 2]
                        qlane = "db" if qi % 2 == 0 else "dc"
                        E_.wait("tensor", qlane, qd[qi % 2])
                        if qi + 1 < NTT * NP:
                            q_load(qi + 1)
                        nkc = (qt + 1) * 4
                        es = [0, 0, 0]
                        pv_t = [0, 0, 0]
                        sc_t = [0, 0, 0]

                        def s_mms(kc):
                            sl = kc % 3
                            E_.wait("tensor", "s", es[sl])
                            if sl < 2:
                                PSSc = PSS[sl]
                                d0 = PSSc[:, 0:TT]
                                d1 = PSSc[:, 512:1024]
                            else:
                                E_.wait("tensor", "v", max(pm_v))
                                E_.wait("tensor", "s", max(pm_s))
                                d0 = R.PM0[:, 0:TT]
                                d1 = R.PM1[:, 0:TT]
                            E_.mm(d0,
                                  KT[0:64, p * T + kc * 128:
                                     p * T + (kc + 1) * 128],
                                  qtile[0:64, :], start=True, stop=True,
                                  tile_position=(0, 0))
                            tb_ = E_.mm(d1,
                                        KT[64:128, p * T + kc * 128:
                                           p * T + (kc + 1) * 128],
                                        qtile[64:128, :], start=True, stop=True,
                                        tile_position=(64, 0))
                            sc_t[sl] = tb_
                            return tb_

                        q_t = s_mms(0)
                        if nkc > 1:
                            q_t = s_mms(1)
                        for kc in range(nkc):
                            sl = kc % 3
                            EB = EAB[sl]
                            E_.wait("scalar", "t", sc_t[sl])
                            E_.wait("scalar", "t", pv_t[sl])
                            if sl < 2:
                                se = E_.s("activation", EB[:], PSS[sl][:],
                                          AF.Exp, scale=0.125)
                            else:
                                E_.s("activation", EB[:, 0:512],
                                     R.PM0[:, 0:TT], AF.Exp, scale=0.125)
                                se = E_.s("activation", EB[:, 512:1024],
                                          R.PM1[:, 0:TT], AF.Exp, scale=0.125)
                                esPM = se
                            es[sl] = se
                            diag = (kc >= qt * 4)
                            if diag:
                                vv = kc - qt * 4
                                E_.wait("vector", "s", se)
                                mrep = bass.AP(emaskT[:].tensor, 384 - 128 * vv,
                                               [[896, 128], [0, 2], [1, 512]])
                                ve = E_.v("tensor_tensor",
                                          EB[:].rearrange("p (a b) -> p a b", b=512),
                                          EB[:].rearrange("p (a b) -> p a b", b=512),
                                          mrep, OP.mult)
                            if kc + 2 < nkc:
                                q_t = s_mms(kc + 2)
                            if diag:
                                E_.wait("tensor", "v", ve)
                            else:
                                E_.wait("tensor", "s", se)
                            if kc == 0:
                                E_.wait("tensor", "s", pu_free_s[par])
                                E_.wait("tensor", "v", pu_free_v[par])
                            islast = (kc == nkc - 1)
                            t1 = E_.mm(bkA[0:65, 0:TT], vsl(kc, 2 * p),
                                       EB[:, 0:512], start=(kc == 0), stop=islast)
                            t2 = E_.mm(bkB[0:65, 0:TT], vsl(kc, 2 * p + 1),
                                       EB[:, 512:1024], start=(kc == 0), stop=islast)
                            pv_t[kc % 3] = t2
                        quse[qi % 2] = q_t
                        E_.wait("vector", "t", t2)
                        E_.wait("vector", "ga", fin_g)
                        E_.wait("vector", "aq", scol_d)
                        va = E_.v("tensor_copy", srowA, bkA[64:65, 0:TT])
                        E_.wait("vector", "v", va)
                        vb_ = E_.v("tensor_copy", srowB, bkB[64:65, 0:TT])
                        E_.wait("scalar", "v", va)
                        E_.wait("scalar", "t", t2)
                        E_.wait("scalar", "t", proj_t)
                        pu_free_s[par] = E_.s("copy",
                                              stag[0:64, p * TT:(p + 1) * TT],
                                              bkA[0:64, 0:TT])
                        E_.wait("vector", "v", vb_)
                        E_.wait("vector", "t", proj_t)
                        pu_free_v[par] = E_.v("tensor_copy",
                                              stag[64:128, p * TT:(p + 1) * TT],
                                              bkB[0:64, 0:TT])
                        if p == 0:
                            E_.wait("scalar", "s", scol_gate)
                        E_.wait("scalar", "v", vb_)
                        E_.dma("aq", Scol[2 * p:2 * p + 1, :], srowA)
                        scol_d = E_.dma("aq", Scol[2 * p + 1:2 * p + 2, :],
                                        srowB, nowait=True)
                    if layer == 0:
                        E_.wait("scalar", "aq", scol_d)
                        E_.wait("scalar", "v", cht_gate)
                        E_.wait("scalar", "v", pm_v[0])
                        E_.wait("scalar", "ga", fin_g)
                        cs = E_.s("activation", chT, Scol[:], AF.Ln)
                        E_.wait("scalar", "s", cs)
                        cs = E_.s("activation", c1m[:], chT, AF.Exp,
                                  scale=-1.0)
                        scol_gate = cs
                        c1_rdy_s = cs
                    else:
                        E_.wait("scalar", "da", mcol_d)
                        E_.wait("scalar", "v", cht_gate)
                        E_.wait("scalar", "v", pm_v[0])
                        E_.wait("scalar", "ga", fin_g)
                        cs = E_.s("activation", chT, Mcol[:], AF.Exp,
                                  scale=0.125)
                        E_.wait("vector", "s", cs)
                        cv = E_.v("tensor_scalar", chT, chT, 1e-6,
                                  None, OP.mult, OP.bypass)
                        E_.wait("vector", "aq", scol_d)
                        E_.wait("vector", "v", cv)
                        cv = E_.v("tensor_tensor", chT, Scol[:], chT,
                                  OP.add)
                        E_.wait("scalar", "v", cv)
                        cs = E_.s("activation", Mcol[:], chT, AF.Ln)
                        E_.wait("scalar", "s", cs)
                        cs = E_.s("activation", chT, Mcol[:], AF.Exp,
                                  scale=-1.0)
                        E_.wait("vector", "s", cs)
                        cv = E_.v("tensor_tensor", Mcol[:], Scol[:], chT,
                                  OP.mult)
                        E_.wait("vector", "v", cv)
                        cv = E_.v("tensor_scalar", Mcol[:], Mcol[:], 1.0,
                                  float(T) * 1e-8, OP.mult, OP.add)
                        E_.wait("scalar", "v", cv)
                        cs = E_.s("activation", Scol[:], Mcol[:], AF.Ln)
                        E_.wait("scalar", "s", cs)
                        cs = E_.s("activation", Mcol[:], Scol[:], AF.Exp,
                                  scale=-1.0)
                        scol_gate = cs
                        E_.wait("vector", "s", cs)
                        cv = E_.v("tensor_tensor", c1m[:], chT, Mcol[:],
                                  OP.mult)
                        cht_gate = cv
                        c1_rdy = cv
                        E_.wait("vector", "v", cv)
                        cv = E_.v("tensor_scalar", c2m[:], Mcol[:], 1e-8,
                                  None, OP.mult, OP.bypass)
                        mcol_gate = cv
                        c2_rdy = cv
                    bank_v = [pm_v[0], pm_v[1]]
                    ap_v = 0
                    for p in range(NP):
                        stg = stag[:, p * TT:(p + 1) * TT]
                        E_.wait("tensor", "v", c1_rdy)
                        E_.wait("tensor", "s", c1_rdy_s)
                        E_.wait("tensor", "s", max(pu_free_s))
                        E_.wait("tensor", "v", max(pu_free_v))
                        E_.wait("tensor", "s", esPM)
                        E_.wait("tensor", "v", bank_v[0])
                        E_.wait("tensor", "s", pm_s[0])
                        tb0 = E_.mm(PM[0][:, 0:TT],
                                    osel[:, p * 128:(p + 1) * 128],
                                    c1m[:], start=True, stop=True)
                        if layer == 1:
                            E_.wait("tensor", "v", c2_rdy)
                            E_.wait("tensor", "v", bank_v[1])
                            E_.wait("tensor", "s", pm_s[1])
                            tb1 = E_.mm(PM[1][:, 0:TT],
                                        wsel[:, p * 128:(p + 1) * 128],
                                        c2m[:], start=True, stop=True)
                        E_.wait("vector", "t", tb0)
                        E_.wait("vector", "s", esPM)
                        E_.wait("vector", "s", max(pu_free_s))
                        E_.wait("vector", "v", max(pu_free_v))
                        E_.wait("vector", "v", ap_v)
                        if layer == 0:
                            vap = E_.v("tensor_tensor", stg, stg,
                                       PM[0][:, 0:TT], OP.mult)
                            bank_v[0] = vap
                        else:
                            vth = E_.v("tensor_tensor", R.th[:], stg,
                                       PM[0][:, 0:TT], OP.mult)
                            bank_v[0] = vth
                            E_.wait("vector", "t", tb1)
                            E_.wait("vector", "v", vth)
                            vap = E_.v("tensor_tensor", stg, R.th[:],
                                       PM[1][:, 0:TT], OP.add)
                            bank_v[1] = vap
                        ap_v = vap
                    pm_v = [ap_v, ap_v]
                    se = 0
                    E_.wait("scalar", "ga", fin_g)
                    E_.wait("scalar", "aq", scol_d)
                    E_.wait("scalar", "s", scol_gate)
                    E_.wait("scalar", "v", cht_gate)
                    for oc in range(DC):
                        P = PM[oc % 2]
                        E_.wait("tensor", "s", pm_s[oc % 2])
                        E_.wait("tensor", "v", ap_v)
                        for dc in range(DC):
                            tm = E_.mm(P[:, 0:TT],
                                       WO[:, dc * D + oc * 128:
                                          dc * D + oc * 128 + 128],
                                       stag[:, dc * TT:(dc + 1) * TT],
                                       start=(dc == 0), stop=(dc == DC - 1))
                        E_.wait("scalar", "t", tm)
                        if layer == 0:
                            se = E_.s("activation", R.y[:, oc * TT:(oc + 1) * TT],
                                      P[:, 0:TT], AF.Identity,
                                      bias=boP[:, oc:oc + 1], scale=1.0)
                        else:
                            se = E_.s("copy", R.y[:, oc * TT:(oc + 1) * TT],
                                      P[:, 0:TT])
                        pm_s[oc % 2] = se
                    proj_t = tm
                    vl, gd = R.finish_tile(E_, xout, T, qt, se, PM[0], PM[1],
                                           0, max(pm_v))
                    pm_v = [vl, vl]
                    fin_g = gd
                fin = E_.c["ga"]
                E_.wait("gpsimd", "ga", fin)
            run_phase(nc, sems, cnt, attn)



def build_l1ffn(nc, sems, cnt, inp, xin, xout, T):
    NTT = T // TT
    k0 = inp("l1f_k0", [2 * D, F1], U8)
    k1 = inp("l1f_k1", [F1, F1], U8)
    k2 = inp("l1f_k2", [F1, D], U8)
    s0 = inp("l1f_s0", [1, 1], F32)
    s1 = inp("l1f_s1", [1, 1], F32)
    s2 = inp("l1f_s2", [1, 1], F32)
    k1scr = nc.dram_tensor("k1scr", [FC * 128, FC * 128], BF16).ap()
    k2scr2 = nc.dram_tensor("k2scr2", [F1, D], BF16).ap()

    with contextlib.ExitStack() as ctx:
        sb = lambda n, s, d: ctx.enter_context(nc.sbuf_tensor("f1" + n, s, d))
        R = Res(nc, ctx, "f1")
        k0e = sb("_k0e", [128, DC * F1], BF16)
        h1 = sb("_h1", [128, FC * TT], BF16)
        g2 = sb("_g2", [128, FC * TT], BF16)
        k1s = [sb("_k1a", [128, FC * 128], BF16), sb("_k1b", [128, FC * 128], BF16)]
        k2s = [sb("_k2a", [128, D], BF16), sb("_k2b", [128, D], BF16)]
        ka, kb = k1s[0], k1s[1]
        xt = sb("_xt", [128, DC * TT], F32)
        xhat = sb("_xhat", [128, DC * TT], BF16)
        scl = sb("_scl", [128, 3], F32)
        rrow = sb("_rrow", [1, TT], F32)

        copy_x_phase(nc, sems, cnt, xin, xout, T)

        def prep(E_):
            ic = R.init_consts(E_)
            prev = 0
            for dc in range(DC):
                E_.wait("gpsimd", "v", prev)
                d1 = E_.dma("ga", ka[:], chunk_ap(k0, dc, F1))
                d2 = E_.dma("ga", kb[:], chunk_ap(k0, dc + DC, F1))
                E_.wait("vector", "ga", d2)
                prev = E_.v("tensor_tensor", k0e[:, dc * F1:(dc + 1) * F1],
                            ka[:], kb[:], OP.subtract)
                E_.wait("vector", "v", prev)
            kd = [0, 0]
            kv = [prev, prev]
            for fc1 in range(FC):
                buf = fc1 % 2
                kb_ = (ka, kb)[buf]
                lane = "gb" if buf == 0 else "gc"
                E_.wait("gpsimd", "dd", kd[buf])
                E_.wait("gpsimd", "v", kv[buf])
                dl = E_.dma(lane, kb_[:], chunk_ap(k1, fc1, F1))
                E_.wait("sync", lane, dl)
                dst = bass.AP(k1scr.tensor, k1scr.offset + fc1 * 128,
                              [[FC * 128, 128], [FC * 16384, FC], [1, 128]])
                kd[buf] = E_.dma("dd", dst,
                                 kb_[:].rearrange("r (a c) -> r a c", c=128))
            k2ld = [0, 0]
            k2st = [0, 0]
            for fc2 in range(FC):
                buf = fc2 % 2
                lane = "gb" if buf == 0 else "gc"
                E_.wait("gpsimd", "dd", k2st[buf])
                dl = E_.dma(lane, k2s[buf][:], chunk_ap(k2, fc2, D))
                E_.wait("sync", lane, dl)
                k2st[buf] = E_.dma("dd", chunk_ap(k2scr2, fc2, D), k2s[buf][:])
            E_.wait("sync", "dd", E_.c["dd"])
            eps_guard_v = ic
            eps_guard_t = 0
            for i, (sc, mult) in enumerate(((s0, 1.0), (s1, 4.0), (s2, 4.0))):
                E_.wait("sync", "v", eps_guard_v)
                E_.wait("sync", "t", eps_guard_t)
                dd = E_.dma("da", R.eps11[:], sc)
                E_.wait("tensor", "da", dd)
                tm = E_.mm(R.PM0[:, 0:1], R.ones1[0:1, :], R.eps11[:],
                           start=True, stop=True)
                E_.wait("vector", "t", tm)
                vv = E_.v("tensor_scalar", scl[:, i:i + 1], R.PM0[:, 0:1],
                          mult, None, OP.mult, OP.bypass)
                E_.wait("tensor", "v", vv)
                eps_guard_t = tm
            E_.wait("vector", "t", tm)
            vv = E_.v("memset", R.eps11[:], 1e-5)
            E_.wait("scalar", "v", vv)
        run_phase(nc, sems, cnt, prep)

        def main(E_):
            P4 = (R.PM0[:, 0:512], R.PM1[:, 0:512],
                  R.PSa[:, 0:512], R.PSb[:, 0:512])
            P8 = (R.PM0[:, 0:512], R.PM1[:, 0:512],
                  R.PSa[:, 0:512], R.PSb[:, 0:512],
                  R.PSa[:, 512:1024], R.PSb[:, 512:1024],
                  R.PU4[:, 0:512], R.PU5[:, 0:512])
            pm_v = [0, 0, 0, 0, 0, 0, 0, 0]
            h1_read_t = 0
            g2_read_t = 0
            k1_t = [0, 0]
            k1_d = [0, 0]
            k2_t = [0, 0]
            k2_d = [0, 0]
            ln_t = 0
            ln_v = 0
            for tt in range(NTT):
                E_.wait("sync", "v", ln_v)
                dd = E_.dma("da", xt[:], x_tile_ap(xin, T, tt))
                for eng in ("tensor", "vector", "scalar"):
                    E_.wait(eng, "da", dd)
                tb = R.ln_rows(E_, xt, R.PU4, R.PU5, ln_t, ln_v)
                ln_v = R.ln_apply(E_, xt, xhat, R.PU4, R.PU5, tb)
                E_.wait("tensor", "v", ln_v)
                for fc in range(FC):
                    bi = fc % 2
                    P = P4[bi]
                    E_.wait("tensor", "v", pm_v[bi])
                    E_.wait("tensor", "t", h1_read_t)
                    for dc in range(DC):
                        lhs = k0e[:, dc * F1 + fc * 128: dc * F1 + fc * 128 + 128]
                        tm = E_.mm(P, lhs, xhat[:, dc * TT:(dc + 1) * TT],
                                   start=(dc == 0), stop=(dc == DC - 1))
                    E_.wait("vector", "t", tm)
                    ve = E_.v("tensor_scalar", h1[:, fc * TT:(fc + 1) * TT],
                              P, scl[:, 0:1], 0.0, OP.mult, OP.max)
                    pm_v[bi] = ve
                    E_.wait("vector", "v", ve)
                h1_v = ve
                E_.wait("tensor", "v", h1_v)
                for fc in range(FC):
                    tm = E_.mm(R.PU4[0:1, 0:TT], R.onesc[:],
                               h1[:, fc * TT:(fc + 1) * TT],
                               start=(fc == 0), stop=(fc == FC - 1))
                E_.wait("vector", "t", tm)
                vr = E_.v("tensor_scalar", rrow[:], R.PU4[0:1, 0:TT], -0.5, None,
                          OP.mult, OP.bypass)
                E_.wait("tensor", "v", vr)
                rr_t = tm
                for fc2 in range(FC):
                    buf = fc2 % 2
                    lane = "db" if buf == 0 else "dc"
                    E_.wait("sync", "t", k1_t[buf])
                    k1ap = bass.AP(k1scr.tensor,
                                   k1scr.offset + fc2 * FC * 16384,
                                   [[FC * 128, 128], [1, FC * 128]])
                    k1_d[buf] = E_.dma(lane, k1s[buf][:], k1ap)
                    bi = 2 + fc2 % 2
                    P = P4[bi]
                    E_.wait("tensor", "v", pm_v[bi])
                    E_.wait("tensor", "t", g2_read_t)
                    tm = E_.mm(P, R.ones1[0:1, :], rrow[:],
                               start=True, stop=False)
                    E_.wait("tensor", lane, k1_d[buf])
                    for fc in range(FC):
                        tm = E_.mm(P, k1s[buf][:, fc * 128:(fc + 1) * 128],
                                   h1[:, fc * TT:(fc + 1) * TT],
                                   start=False, stop=(fc == FC - 1))
                    k1_t[buf] = tm
                    E_.wait("vector", "t", tm)
                    ve = E_.v("tensor_scalar", g2[:, fc2 * TT:(fc2 + 1) * TT],
                              P, scl[:, 1:2], 0.0, OP.mult, OP.max)
                    pm_v[bi] = ve
                    E_.wait("vector", "v", ve)
                g2_v = ve
                h1_read_t = tm
                E_.wait("tensor", "v", g2_v)
                E_.wait("tensor", "v", E_.c["v"])
                for fc in range(FC):
                    tm = E_.mm(R.PU4[0:1, 0:TT], R.onesc[:],
                               g2[:, fc * TT:(fc + 1) * TT],
                               start=(fc == 0), stop=(fc == FC - 1))
                E_.wait("vector", "t", tm)
                vr = E_.v("tensor_scalar", rrow[:], R.PU4[0:1, 0:TT], -0.5, None,
                          OP.mult, OP.bypass)
                E_.wait("tensor", "v", vr)
                ogt = [0] * 8
                for fc2 in range(FC):
                    buf = fc2 % 2
                    lane = "db" if buf == 0 else "dc"
                    E_.wait("sync", "t", k2_t[buf])
                    k2_d[buf] = E_.dma(lane, k2s[buf][:],
                                       chunk_ap(k2scr2, fc2, D))
                    E_.wait("tensor", lane, k2_d[buf])
                    for oc in range(DC):
                        P = P8[oc]
                        if fc2 == 0:
                            E_.wait("tensor", "v", pm_v[oc])
                            if oc == 6:
                                E_.wait("tensor", "v", vr)
                            E_.mm(P, R.ones1[0:1, :], rrow[:],
                                  start=True, stop=False)
                        tm = E_.mm(P, k2s[buf][:, oc * 128:(oc + 1) * 128],
                                   g2[:, fc2 * TT:(fc2 + 1) * TT],
                                   start=False, stop=(fc2 == FC - 1))
                        ogt[oc] = tm
                    k2_t[buf] = tm
                g2_read_t = tm
                for oc in range(DC):
                    E_.wait("vector", "t", ogt[oc])
                    ve = E_.v("tensor_scalar", R.y[:, oc * TT:(oc + 1) * TT],
                              P8[oc], scl[:, 2:3], 0.0, OP.mult, OP.bypass)
                    pm_v[oc] = ve
                    E_.wait("vector", "v", ve)
                E_.wait("scalar", "v", ve)
                E_.wait("tensor", "v", ve)
                vl, gd = R.finish_tile(E_, xout, T, tt, E_.c["s"], R.PU4, R.PU5,
                                       max(ogt[6], ogt[7]),
                                       max(pm_v[6], pm_v[7], ln_v))
                ln_v = vl
                ln_t = E_.c["t"]
            fin = E_.c["ga"]
            E_.wait("gpsimd", "ga", fin)
        run_phase(nc, sems, cnt, main)



PARTS = ("l0attn", "l0ffn", "l1attn", "l1ffn")


def build_program(parts, T):
    nc = bass.Bass("TRN2", target_bir_lowering=False)
    declared = {}

    def inp(name, shape, dt):
        if name not in declared:
            declared[name] = nc.dram_tensor(name, shape, dt,
                                            kind="ExternalInput").ap()
        return declared[name]

    xin = nc.dram_tensor("xT", [D, T], F32, kind="ExternalInput").ap()
    yout = nc.dram_tensor("yT", [D, T], F32, kind="ExternalOutput").ap()
    xs = [nc.dram_tensor("xs0", [D, T], F32).ap(),
          nc.dram_tensor("xs1", [D, T], F32).ap()]
    sems = {}
    cnt = {}
    with contextlib.ExitStack() as ctx:
        for s in tuple(LANES) + ("t", "v", "s"):
            sems[s] = ctx.enter_context(nc.semaphore("sem_" + s))
            cnt[s] = 0
        for i, part in enumerate(parts):
            src = xin if i == 0 else xs[(i - 1) % 2]
            dst = yout if i == len(parts) - 1 else xs[i % 2]
            if part == "l0attn":
                build_attn(nc, sems, cnt, inp, src, dst, T, 0)
            elif part == "l0ffn":
                build_l0ffn(nc, sems, cnt, inp, src, dst, T)
            elif part == "l1attn":
                build_attn(nc, sems, cnt, inp, src, dst, T, 1)
            elif part == "l1ffn":
                build_l1ffn(nc, sems, cnt, inp, src, dst, T)
            else:
                raise ValueError(part)
    return nc, set(declared.keys())


def host_consts():
    j = np.arange(512)
    r = np.arange(128)[:, None]
    c = np.arange(896)
    emask2 = (c[None, :] >= r + 384).astype(ml_dtypes.bfloat16)
    pmask = np.where(c[None, :] <= r + 384, 3.0e38, -3.0e38).astype(
        ml_dtypes.bfloat16)
    osel = np.zeros((16, NP * 128), np.float32)
    for p in range(NP):
        osel[2 * p, p * 128:p * 128 + 64] = 1.0
        osel[2 * p + 1, p * 128 + 64:p * 128 + 128] = 1.0
    return {"emask2": emask2, "pmask": pmask,
            "osel16": osel.astype(ml_dtypes.bfloat16)}


def host_weights(inputs):
    c = np.ascontiguousarray
    w = {
        "l0a_wq": c(np.asarray(inputs["L0_attn_Wq"], np.float32)),
        "l0a_wk": c(np.asarray(inputs["L0_attn_Wk"], np.float32)),
        "l0a_wv": c(np.asarray(inputs["L0_attn_Wv"], np.float32)),
        "l0a_wo": c(np.asarray(inputs["L0_attn_Wo"], np.float32)),
        "l0a_bq": c(np.asarray(inputs["L0_attn_bq"], np.float32)),
        "l0a_bk": c(np.asarray(inputs["L0_attn_bk"], np.float32)),
        "l0a_bv": c(np.asarray(inputs["L0_attn_bv"], np.float32)),
        "l0a_bo": c(np.asarray(inputs["L0_attn_bo"], np.float32)),
        "l0f_w1t": c(np.asarray(inputs["L0_ffn_W1"], np.float32).T),
        "l0f_w2t": c(np.asarray(inputs["L0_ffn_W2"], np.float32).T),
        "l0f_b1": c(np.asarray(inputs["L0_ffn_b1"], np.float32)),
        "l0f_b2": c(np.asarray(inputs["L0_ffn_b2"], np.float32)),
        "l1a_wqkvt": c(np.asarray(inputs["L1_attn_w_qkv"], np.float32).T),
        "l1a_wprojt": c(np.asarray(inputs["L1_attn_w_proj"], np.float32).T),
        "l1f_k0": c(np.asarray(inputs["L1_ffn_k0"])).view(np.uint8),
        "l1f_k1": c(np.asarray(inputs["L1_ffn_k1"])).view(np.uint8),
        "l1f_k2": c(np.asarray(inputs["L1_ffn_k2"])).view(np.uint8),
        "l1f_s0": np.asarray(inputs["L1_ffn_s0"], np.float32).reshape(1, 1),
        "l1f_s1": np.asarray(inputs["L1_ffn_s1"], np.float32).reshape(1, 1),
        "l1f_s2": np.asarray(inputs["L1_ffn_s2"], np.float32).reshape(1, 1),
    }
    w.update(host_consts())
    return w


def kernel(**inputs):
    T = 4096
    x = np.asarray(inputs["L0_attn_x"], np.float32)
    w = host_weights(inputs)
    nc, names = build_program(PARTS, T)
    in_maps = []
    for ci in range(NCORES):
        m = {k: v for k, v in w.items() if k in names}
        m["xT"] = np.ascontiguousarray(x[ci].T)
        in_maps.append(m)
    res = run_bass_kernel_spmd(nc, in_maps, core_ids=list(range(NCORES)))
    out = np.stack([np.asarray(res.results[ci]["yT"]).T for ci in range(NCORES)])
    return out.astype(np.float32)


# version 13
# speedup vs baseline: 1.0341x; 1.0021x over previous
import contextlib
import numpy as np
import ml_dtypes
import concourse.bass as bass
import concourse.mybir as mybir
from concourse.bass_utils import run_bass_kernel_spmd
from concourse.alu_op_type import AluOpType as OP

F32 = mybir.dt.float32
F32R = mybir.dt.float32r
BF16 = mybir.dt.bfloat16
U8 = mybir.dt.uint8
AF = mybir.ActivationFunctionType
AX = mybir.AxisListType

DEBUG_L1 = False
D = 1024
DC = 8
F1 = 4096
FC = 32
H = 16
NP = 8
TT = 512
NCORES = 8

ENGINES = ("sync", "gpsimd", "tensor", "vector", "scalar")
LANES = {"da": "sync", "db": "sync", "dc": "sync", "dd": "sync",
         "ga": "gpsimd", "gb": "gpsimd", "gc": "gpsimd", "aq": "scalar"}


class E:
    def __init__(self, nc, cur, handle, sems, cnt):
        self.nc = nc
        self.cur = cur
        self.h = handle
        self.sems = sems
        self.c = cnt

    def wait(self, who, sem, val):
        if self.cur == who and val > 0:
            self.h.wait_ge(self.sems[sem], val)

    def _emit(self, who, sem, meth, *a, **kw):
        self.c[sem] += 1
        if self.cur == who:
            getattr(self.h, meth)(*a, **kw).then_inc(self.sems[sem], 1)
        return self.c[sem]

    def mm(self, out, lhsT, rhs, **kw):
        return self._emit("tensor", "t", "matmul", out, lhsT, rhs, **kw)

    def v(self, meth, *a, **kw):
        return self._emit("vector", "v", meth, *a, **kw)

    def s(self, meth, *a, **kw):
        return self._emit("scalar", "s", meth, *a, **kw)

    def dma(self, lane, out, in_, nowait=False, **kw):
        prev = self.c[lane]
        self.c[lane] += 16
        if self.cur == LANES[lane]:
            if prev > 0 and not nowait:
                self.h.wait_ge(self.sems[lane], prev)
            self.h.dma_start(out, in_, **kw).then_inc(self.sems[lane], 16)
        return self.c[lane]


def drain_lanes(E_):
    for lane, eng in LANES.items():
        E_.wait(eng, lane, E_.c[lane])


def run_phase(nc, sems, cnt, fn):
    finals = []
    snap = dict(cnt)
    with nc.Block() as block:
        for name in ENGINES:
            def mk(n):
                def f(h):
                    local = dict(snap)
                    fn(E(nc, n, h, sems, local))
                    drain_lanes(E(nc, n, h, sems, local))
                    finals.append(local)
                return f
            getattr(block, name)(mk(name))
    for f in finals[1:]:
        assert f == finals[0], f"engine passes disagree: {finals[0]} vs {f}"
    cnt.clear()
    cnt.update(finals[0])


def r32(ap):
    return ap.bitcast(F32R)


def x_tile_ap(x_dram, T, tt):
    return bass.AP(x_dram.tensor, x_dram.offset + tt * TT,
                   [[T, 128], [128 * T, DC], [1, TT]])


def bias_ap(b_dram, c):
    return bass.AP(b_dram.tensor, b_dram.offset + c * 128, [[1, 128], [1, 1]])


def chunk_ap(w_dram, c, nfree, w=None):
    return bass.AP(w_dram.tensor, w_dram.offset + c * 128 * nfree,
                   [[nfree, 128], [1, nfree if w is None else w]])


def flat_ap(t_dram, n):
    return bass.AP(t_dram.tensor, t_dram.offset, [[1, n]])


class Res:

    def __init__(self, nc, ctx, tag):
        sb = lambda n, s, d: ctx.enter_context(nc.sbuf_tensor(tag + n, s, d))
        ps = lambda n, w: ctx.enter_context(nc.psum_tensor(tag + n, [128, w], F32))
        self.PM0 = ps("_PM0", 512)
        self.PM1 = ps("_PM1", 512)
        self.PSa = ps("_PSa", 1024)
        self.PSb = ps("_PSb", 1024)
        self.PU4 = ps("_PU4", 512)
        self.PU5 = ps("_PU5", 512)
        self.y = sb("_y", [128, DC * TT], F32)
        self.xsq = [sb("_xsq0", [128, TT], BF16), sb("_xsq1", [128, TT], BF16)]
        self.xb = [sb("_xb0", [128, TT], BF16), sb("_xb1", [128, TT], BF16)]
        self.th = sb("_th", [128, TT], F32)
        self.ones128 = sb("_o128", [128, 128], BF16)
        self.ones128f = sb("_o128f", [128, 128], F32)
        self.ones1 = sb("_o1", [33, 128], F32)
        self.onesc = sb("_oc", [128, 1], BF16)
        self.eps11 = sb("_eps", [1, 1], F32)
        self.rmu = sb("_rmu", [1, TT], F32)
        self.rt1 = sb("_rt1", [1, TT], F32)
        self.rt2 = sb("_rt2", [1, TT], F32)

    def init_consts(self, E_):
        E_.v("memset", self.ones128[:], 1.0)
        E_.v("memset", self.ones128f[:], 1.0)
        E_.v("memset", self.ones1[:], 1.0)
        E_.v("memset", self.onesc[:], 1.0)
        return E_.v("memset", self.eps11[:], 1e-5)

    def ln_rows(self, E_, src_sb, PA, PB, pa_t_free, pa_v_free):
        E_.wait("tensor", "v", pa_v_free)
        E_.wait("tensor", "t", pa_t_free)
        xg = [E_.c["t"], E_.c["t"]]
        for c in range(DC):
            sl = src_sb[:, c * TT:(c + 1) * TT]
            xq = self.xsq[c % 2]
            xb = self.xb[c % 2]
            E_.wait("vector", "t", xg[c % 2])
            vcb = E_.v("tensor_copy", xb[:], sl)
            E_.wait("scalar", "t", xg[c % 2])
            ssq = E_.s("activation", xq[:], sl, AF.Square)
            E_.wait("tensor", "v", vcb)
            E_.mm(PA[:, 0:TT], self.ones128[:], xb[:],
                  start=(c == 0), stop=(c == DC - 1))
            E_.wait("tensor", "s", ssq)
            tmm = E_.mm(PB[:, 0:TT], self.ones128[:], xq[:],
                        start=(c == 0), stop=(c == DC - 1))
            xg[c % 2] = tmm
        nfeat = float(D)
        E_.wait("vector", "t", tmm)
        v1 = E_.v("tensor_scalar", self.rmu[:], PA[0:1, 0:TT], 1.0 / nfeat,
                  None, OP.mult, OP.bypass)
        E_.wait("vector", "v", v1)
        v2 = E_.v("tensor_tensor", self.rt1[:], self.rmu[:], self.rmu[:], OP.mult)
        E_.wait("vector", "v", v2)
        v3 = E_.v("scalar_tensor_tensor", self.rt2[:], PB[0:1, 0:TT],
                  1.0 / nfeat, self.rt1[:], OP.mult, OP.subtract)
        E_.wait("scalar", "v", v3)
        s1 = E_.s("activation", self.rt1[:], self.rt2[:], AF.Ln,
                  bias=self.eps11[:], scale=1.0)
        E_.wait("scalar", "s", s1)
        s2 = E_.s("activation", self.rt2[:], self.rt1[:], AF.Exp,
                  scale=-0.5)
        E_.wait("vector", "s", s2)
        v5 = E_.v("scalar_tensor_tensor", self.rt1[:], self.rmu[:], -1.0,
                  self.rt2[:], OP.mult, OP.mult)
        E_.wait("tensor", "v", v5)
        E_.mm(PA[:, 0:TT], self.ones1[0:1, :], self.rt2[:],
              start=True, stop=True)
        tb = E_.mm(PB[:, 0:TT], self.ones1[0:1, :], self.rt1[:],
                   start=True, stop=True)
        return tb

    def ln_apply(self, E_, src_sb, dst_sb, PA, PB, t_bcast):
        E_.wait("vector", "t", t_bcast)
        for c in range(DC):
            sl = src_sb[:, c * TT:(c + 1) * TT]
            dl = dst_sb[:, c * TT:(c + 1) * TT]
            va = E_.v("tensor_tensor", self.th[:], sl, PA[:, 0:TT], OP.mult)
            E_.wait("vector", "v", va)
            last = E_.v("tensor_tensor", dl, self.th[:], PB[:, 0:TT], OP.add)
            E_.wait("vector", "v", last)
        return last

    def finish_tile(self, E_, xout, T, tt, last_s, PA, PB, t_free, v_free):
        E_.wait("vector", "s", last_s)
        E_.wait("scalar", "s", last_s)
        E_.wait("tensor", "s", last_s)
        tb = self.ln_rows(E_, self.y, PA, PB, max(t_free, E_.c["t"]),
                          max(v_free, E_.c["v"]))
        vl = self.ln_apply(E_, self.y, self.y, PA, PB, tb)
        E_.wait("gpsimd", "v", vl)
        gd = E_.dma("ga", x_tile_ap(xout, T, tt), self.y[:], accum_op=OP.add)
        return vl, gd


def copy_x_phase(nc, sems, cnt, xin, xout, T):
    def cp(E_):
        gd = E_.dma("ga", flat_ap(xout, D * T), flat_ap(xin, D * T))
        E_.wait("gpsimd", "ga", gd)
    run_phase(nc, sems, cnt, cp)



def build_l0ffn(nc, sems, cnt, inp, xin, xout, T):
    NTT = T // TT
    W1T = inp("l0f_w1t", [D, F1], F32)
    W2T = inp("l0f_w2t", [F1, D], F32)
    b1 = inp("l0f_b1", [F1], F32)
    b2 = inp("l0f_b2", [D], F32)
    w2scr = nc.dram_tensor("w2scr", [FC * 128, D], BF16).ap()

    with contextlib.ExitStack() as ctx:
        sb = lambda n, s, d: ctx.enter_context(nc.sbuf_tensor("f0" + n, s, d))
        R = Res(nc, ctx, "f0")
        W1q = sb("_w1q", [128, DC * F1], BF16)
        hbuf = sb("_h", [128, FC * TT], BF16)
        b1t = sb("_b1t", [128, FC], F32)
        b2t = sb("_b2t", [128, DC], F32)

        copy_x_phase(nc, sems, cnt, xin, xout, T)

        with contextlib.ExitStack() as cpp:
            p_ = lambda n, s, d: cpp.enter_context(nc.sbuf_tensor("f0p" + n, s, d))
            wls = [p_("_wl0", [128, F1], F32), p_("_wl1", [128, F1], F32)]
            wsg = p_("_wsg", [128, 1024], F32)
            wab = p_("_wab", [128, 1024], F32)
            w2qs = [p_("_w2q0", [128, 1024], BF16), p_("_w2q1", [128, 1024], BF16)]
            acc = p_("_acc", [128, 1], F32)
            part = p_("_part", [128, 1], F32)
            thr = p_("_thr", [128, 1], F32)

            def prep(E_):
                R.init_consts(E_)
                for c in range(FC):
                    E_.dma("da", b1t[:, c:c + 1], bias_ap(b1, c))
                for c in range(DC):
                    E_.dma("da", b2t[:, c:c + 1], bias_ap(b2, c))
                wg_v = [0, 0]
                wg_s = [0, 0]
                st_d = [0, 0]
                q_guard = 0
                ld = [0, 0]
                lane_of = ("da", "db")

                def wload(WT, c, nfree):
                    buf = c % 2
                    E_.wait("sync", "v", wg_v[buf])
                    E_.wait("sync", "s", wg_s[buf])
                    ld[buf] = E_.dma(lane_of[buf], wls[buf][:, 0:nfree],
                                     chunk_ap(WT, c, nfree))

                for wi, (WT, nch, nfree) in enumerate([(W1T, DC, F1), (W2T, FC, D)]):
                    vacc = E_.v("memset", acc[:], 0.0)
                    wload(WT, 0, nfree)
                    for c in range(nch):
                        buf = c % 2
                        if c + 1 < nch:
                            wload(WT, c + 1, nfree)
                        E_.wait("vector", lane_of[buf], ld[buf])
                        E_.wait("vector", "v", vacc)
                        vp = E_.v("tensor_reduce", part[:], wls[buf][:, 0:nfree],
                                  AX.X, OP.add, apply_absolute_value=True)
                        E_.wait("vector", "v", vp)
                        vacc = E_.v("tensor_tensor", acc[:], acc[:], part[:], OP.add)
                        wg_v[buf] = vp
                    E_.wait("tensor", "v", vacc)
                    tg = E_.mm(R.PM0[:, 0:1], R.ones128f[:], acc[:],
                               start=True, stop=True)
                    E_.wait("vector", "t", tg)
                    nel = float(nch * 128 * nfree)
                    vthr = E_.v("tensor_scalar", thr[:], R.PM0[:, 0:1],
                                0.5 / nel, 0.5e-5, OP.mult, OP.add)
                    wload(WT, 0, nfree)
                    for c in range(nch):
                        buf = c % 2
                        if c + 1 < nch:
                            wload(WT, c + 1, nfree)
                        for q4 in range(nfree // 1024):
                            wlsl = wls[buf][:, q4 * 1024:(q4 + 1) * 1024]
                            E_.wait("scalar", lane_of[buf], ld[buf])
                            E_.wait("scalar", "v", q_guard)
                            ssg = E_.s("sign", wsg[:], wlsl)
                            sab = E_.s("activation", wab[:], wlsl, AF.Abs)
                            E_.wait("vector", "s", sab)
                            E_.wait("vector", "v", vthr)
                            vc = E_.v("tensor_scalar", wab[:], wab[:], thr[:], None,
                                      OP.is_ge, OP.bypass)
                            E_.wait("vector", "v", vc)
                            E_.wait("vector", "s", ssg)
                            if wi == 0:
                                dst = W1q[:, c * F1 + q4 * 1024:
                                          c * F1 + (q4 + 1) * 1024]
                            else:
                                dst = w2qs[buf][:]
                                E_.wait("vector", "dc" if buf == 0 else "dd",
                                        st_d[buf])
                            vq = E_.v("tensor_tensor", dst, wab[:], wsg[:], OP.mult)
                            E_.wait("vector", "v", vq)
                            if wi == 1:
                                E_.wait("sync", "v", vq)
                                st_d[buf] = E_.dma("dc" if buf == 0 else "dd",
                                                   chunk_ap(w2scr, c, D),
                                                   w2qs[buf][:])
                            wg_s[buf] = sab
                            q_guard = vq
            run_phase(nc, sems, cnt, prep)

        with contextlib.ExitStack() as cm:
            m_ = lambda n, s, d: cm.enter_context(nc.sbuf_tensor("f0m" + n, s, d))
            xts = [m_("_xt0", [128, DC * TT], F32),
                   m_("_xt1", [128, DC * TT], F32)]
            xhat = m_("_xhat", [128, DC * TT], BF16)
            w2sl = [m_("_w2a", [128, FC * 128], BF16),
                    m_("_w2b", [128, FC * 128], BF16)]

            def main(E_):
                PM = (R.PM0, R.PM1)
                pm_s = [0, 0]
                h_read_t = 0
                w2_t = [0, 0]
                w2_d = [0, 0]
                ln_t = 0
                ln_v = 0
                xg = [0, 0]
                xd = [0, 0]
                def x_load(tt_):
                    b_ = tt_ % 2
                    E_.wait("sync", "v", xg[b_])
                    xd[b_] = E_.dma("dd", xts[b_][:], x_tile_ap(xin, T, tt_))
                for tt in range(NTT):
                    xb_ = tt % 2
                    xt = xts[xb_]
                    if tt == 0:
                        x_load(0)
                    if tt + 1 < NTT:
                        x_load(tt + 1)
                    for eng in ("tensor", "vector", "scalar"):
                        E_.wait(eng, "dd", xd[xb_])
                    tb = R.ln_rows(E_, xt, R.PU4, R.PU5, ln_t, ln_v)
                    va = R.ln_apply(E_, xt, xhat, R.PU4, R.PU5, tb)
                    ln_v = va
                    xg[xb_] = va
                    E_.wait("tensor", "v", va)
                    evac_last = 0
                    for fc in range(FC):
                        P = PM[fc % 2]
                        E_.wait("tensor", "s", pm_s[fc % 2])
                        E_.wait("tensor", "t", h_read_t)
                        for dc in range(DC):
                            lhs = W1q[:, dc * F1 + fc * 128: dc * F1 + fc * 128 + 128]
                            tm = E_.mm(P[:, 0:TT], lhs,
                                       xhat[:, dc * TT:(dc + 1) * TT],
                                       start=(dc == 0), stop=(dc == DC - 1))
                        E_.wait("scalar", "t", tm)
                        se = E_.s("activation", hbuf[:, fc * TT:(fc + 1) * TT],
                                  P[:, 0:TT], AF.Relu, bias=b1t[:, fc:fc + 1],
                                  scale=1.0)
                        pm_s[fc % 2] = se
                        evac_last = se
                    for oc in range(DC):
                        buf = oc % 2
                        E_.wait("sync", "t", w2_t[buf])
                        w2ap = bass.AP(w2scr.tensor, w2scr.offset + oc * 128,
                                       [[D, 128], [128 * D, FC], [1, 128]])
                        w2_d[buf] = E_.dma("da", w2sl[buf][:], w2ap)
                        P = PM[oc % 2]
                        E_.wait("tensor", "s", pm_s[oc % 2])
                        E_.wait("tensor", "da", w2_d[buf])
                        E_.wait("tensor", "s", evac_last)
                        for fc in range(FC):
                            tm = E_.mm(P[:, 0:TT],
                                       w2sl[buf][:, fc * 128:(fc + 1) * 128],
                                       hbuf[:, fc * TT:(fc + 1) * TT],
                                       start=(fc == 0), stop=(fc == FC - 1))
                        w2_t[buf] = tm
                        E_.wait("scalar", "t", tm)
                        se = E_.s("activation", R.y[:, oc * TT:(oc + 1) * TT],
                                  P[:, 0:TT], AF.Identity, bias=b2t[:, oc:oc + 1],
                                  scale=1.0)
                        pm_s[oc % 2] = se
                    h_read_t = tm
                    vl, gd = R.finish_tile(E_, xout, T, tt, se, R.PU4, R.PU5,
                                           0, ln_v)
                    ln_v = vl
                    ln_t = E_.c["t"]
                fin = E_.c["ga"]
                E_.wait("gpsimd", "ga", fin)
            run_phase(nc, sems, cnt, main)



def build_attn(nc, sems, cnt, inp, xin, xout, T, layer):
    NTT = T // TT
    NTC = T // 128
    if layer == 0:
        Wq = inp("l0a_wq", [D, D], F32)
        Wk = inp("l0a_wk", [D, D], F32)
        Wv = inp("l0a_wv", [D, D], F32)
        Wo = inp("l0a_wo", [D, D], F32)
        bq = inp("l0a_bq", [D], F32)
        bk = inp("l0a_bk", [D], F32)
        bv = inp("l0a_bv", [D], F32)
        bo = inp("l0a_bo", [D], F32)
    else:
        WqkvT = inp("l1a_wqkvt", [D, 3 * D], F32)
        WprojT = inp("l1a_wprojt", [D, D], F32)
        pmask = inp("pmask", [128, 896], BF16)
    emask2 = inp("emask2", [128, 896], BF16)
    osel16 = inp("osel16", [16, NP * 128], BF16)
    if layer == 1 and DEBUG_L1:
        dbg = nc.dram_tensor("dbg", [4, TT], F32, kind="ExternalOutput").ap()
        dbg2 = nc.dram_tensor("dbg2", [64, TT], BF16, kind="ExternalOutput").ap()
    else:
        dbg = None
    qscr = nc.dram_tensor(f"qscr{layer}", [NP * 128, T], BF16).ap()
    wscr = nc.dram_tensor(f"wscr{layer}", [3 * NP * DC * 128, 128], BF16).ap()
    WS_P = DC * 128 * 128

    def wscr_lhs_ap(which, p):
        off = (which * NP + p) * WS_P
        return bass.AP(wscr.tensor, wscr.offset + off,
                       [[128, 128], [128 * 128, DC], [1, 128]])

    def wscr_v_ap(half, dc):
        off = 2 * NP * WS_P + (half * 4) * WS_P + dc * 128 * 128
        return bass.AP(wscr.tensor, wscr.offset + off,
                       [[128, 128], [WS_P, 4], [1, 128]])

    with contextlib.ExitStack() as ctx:
        sb = lambda n, s, d: ctx.enter_context(nc.sbuf_tensor(f"a{layer}" + n, s, d))
        KT = sb("_kt", [128, NP * T], BF16)
        V = sb("_v", [128, NTC * H * 65], BF16)
        R = Res(nc, ctx, f"a{layer}")
        if layer == 1:
            mscr = sb("_mscr", [128, NTC * 16], F32)

        def vsl(tc, h, w=65):
            return V[:, tc * H * 65 + h * 65: tc * H * 65 + h * 65 + w]

        copy_x_phase(nc, sems, cnt, xin, xout, T)

        with contextlib.ExitStack() as cw:
            w_ = lambda n, s, d: cw.enter_context(nc.sbuf_tensor(f"a{layer}w" + n, s, d))
            wl = w_("_wl", [128, 1024], F32)
            wb = w_("_wb", [128, 1024], BF16)

            def wprep(E_):
                R.init_consts(E_)
                prev_v = 0
                prev_d = 0
                for dc in range(DC):
                    for which in range(3):
                        if layer == 0:
                            W = (Wq, Wk, Wv)[which]
                            src = chunk_ap(W, dc, D)
                        else:
                            src = bass.AP(WqkvT.tensor,
                                          WqkvT.offset + dc * 128 * 3 * D + which * D,
                                          [[3 * D, 128], [1, D]])
                        E_.wait("sync", "v", prev_v)
                        dd = E_.dma("da", wl[:], src)
                        E_.wait("vector", "da", dd)
                        E_.wait("vector", "da", prev_d)
                        vq = E_.v("tensor_copy", wb[:], wl[:])
                        E_.wait("sync", "v", vq)
                        dst = bass.AP(wscr.tensor,
                                      wscr.offset + (which * NP) * WS_P + dc * 128 * 128,
                                      [[128, 128], [WS_P, NP], [1, 128]])
                        prev_d = E_.dma("da", dst, wb[:].rearrange(
                            "q (p c) -> q p c", c=128))
                        prev_v = vq
                E_.wait("sync", "da", prev_d)
            run_phase(nc, sems, cnt, wprep)

        with contextlib.ExitStack() as c2:
            p_ = lambda n, s, d: c2.enter_context(nc.sbuf_tensor(f"a{layer}p" + n, s, d))
            xts = [p_("_xt0", [128, DC * TT], F32),
                   p_("_xt1", [128, DC * TT], F32)]
            xhat = p_("_xhat", [128, DC * TT], BF16)
            wkq = [p_("_wkq0", [128, DC * 128], BF16),
                   p_("_wkq1", [128, DC * 128], BF16)]
            wv2 = [p_("_wv0", [128, 512], BF16), p_("_wv1", [128, 512], BF16)]
            qtmp = p_("_qtmp", [128, TT], BF16)
            bt = p_("_bt", [128, 2 * DC], F32)

            def proj(E_):
                if layer == 0:
                    for c in range(DC):
                        E_.dma("da", bt[:, c:c + 1], bias_ap(bq, c))
                        dd = E_.dma("da", bt[:, DC + c:DC + c + 1], bias_ap(bk, c))
                    E_.wait("scalar", "da", dd)
                PM = (R.PM0, R.PM1)
                bank_s = [0, 0, 0, 0]
                pm_s = bank_s
                w_t = [0, 0]
                w_d = [0, 0]
                q_aq = 0
                ln_t = 0
                ln_v = 0
                xg = [0, 0]
                xd = [0, 0]
                def x_load(tt_):
                    b_ = tt_ % 2
                    E_.wait("sync", "v", xg[b_])
                    xd[b_] = E_.dma("dd", xts[b_][:], x_tile_ap(xin, T, tt_))
                for tt in range(NTT):
                    xb_ = tt % 2
                    xt = xts[xb_]
                    if tt == 0:
                        x_load(0)
                    if tt + 1 < NTT:
                        x_load(tt + 1)
                    for eng in ("tensor", "vector", "scalar"):
                        E_.wait(eng, "dd", xd[xb_])
                    tb = R.ln_rows(E_, xt, R.PU4, R.PU5, ln_t, ln_v)
                    ln_v = R.ln_apply(E_, xt, xhat, R.PU4, R.PU5, tb)
                    xg[xb_] = ln_v
                    E_.wait("tensor", "v", ln_v)
                    for wi, which in enumerate((1, 0)):
                        for p in range(NP):
                            buf = p % 2
                            E_.wait("sync", "t", w_t[buf])
                            w_d[buf] = E_.dma("da", wkq[buf][:], wscr_lhs_ap(which, p))
                            P = PM[p % 2]
                            E_.wait("tensor", "s", pm_s[p % 2])
                            E_.wait("tensor", "da", w_d[buf])
                            for dc in range(DC):
                                tm = E_.mm(P[:, 0:TT],
                                           wkq[buf][:, dc * 128:(dc + 1) * 128],
                                           xhat[:, dc * TT:(dc + 1) * TT],
                                           start=(dc == 0), stop=(dc == DC - 1))
                            w_t[buf] = tm
                            E_.wait("scalar", "t", tm)
                            if which == 1:
                                dst = KT[:, p * T + tt * TT: p * T + (tt + 1) * TT]
                                if layer == 0:
                                    se = E_.s("activation", dst, P[:, 0:TT],
                                              AF.Identity,
                                              bias=bt[:, DC + p:DC + p + 1], scale=1.0)
                                else:
                                    se = E_.s("copy", dst, P[:, 0:TT])
                            else:
                                E_.wait("scalar", "aq", q_aq)
                                if layer == 0:
                                    se = E_.s("activation", qtmp[:], P[:, 0:TT],
                                              AF.Identity, bias=bt[:, p:p + 1],
                                              scale=1.0)
                                else:
                                    se = E_.s("copy", qtmp[:], P[:, 0:TT])
                                qap = bass.AP(qscr.tensor,
                                              qscr.offset + p * 128 * T + tt * TT,
                                              [[T, 128], [1, TT]])
                                E_.wait("scalar", "s", se)
                                q_aq = E_.dma("aq", qap, qtmp[:])
                            pm_s[p % 2] = se
                    P4 = (R.PM0, R.PM1, R.PSa, R.PSb)
                    vm = E_.v("memset",
                              V[:, tt * 4 * H * 65:(tt + 1) * 4 * H * 65], 1.0)
                    for half in range(2):
                        for t4 in range(4):
                            E_.wait("tensor", "s", bank_s[t4])
                        E_.wait("tensor", "v", vm)
                        for dc in range(DC):
                            buf = dc % 2
                            lane = "db" if buf == 0 else "dc"
                            E_.wait("sync", "t", w_t[buf])
                            w_d[buf] = E_.dma(lane, wv2[buf][:], wscr_v_ap(half, dc))
                            E_.wait("tensor", lane, w_d[buf])
                            for t4 in range(4):
                                tm = E_.mm(P4[t4][:, 0:512],
                                           xhat[:, dc * TT + t4 * 128:
                                                dc * TT + t4 * 128 + 128],
                                           wv2[buf][:],
                                           start=(dc == 0), stop=(dc == DC - 1))
                            w_t[buf] = tm
                        E_.wait("scalar", "t", tm)
                        for t4 in range(4):
                            tc = tt * 4 + t4
                            dst = V[:, tc * H * 65 + half * 8 * 65:
                                    tc * H * 65 + (half + 1) * 8 * 65]
                            dst = dst.rearrange("p (h d) -> p h d", d=65)[:, :, 0:64]
                            se = E_.s("copy", dst, P4[t4][:, 0:512])
                            bank_s[t4] = se
                    E_.wait("tensor", "s", se)
                    ln_t = E_.c["t"]
            run_phase(nc, sems, cnt, proj)

        if layer == 1:
            with contextlib.ExitStack() as c4:
                p_ = lambda n, s, d: c4.enter_context(
                    nc.sbuf_tensor(f"a{layer}m" + n, s, d))
                pmaskT = p_("_pm", [128, 896], BF16)
                qt1 = [p_("_qt1a", [128, TT], BF16),
                       p_("_qt1b", [128, TT], BF16)]
                prt = p_("_p1", [128, 16], F32)

                def pass1(E_):
                    dd = E_.dma("da", pmaskT[:], pmask)
                    E_.wait("vector", "da", dd)
                    PS4 = (R.PSa[:, 0:512], R.PSa[:, 512:1024],
                           R.PSb[:, 0:512], R.PSb[:, 512:1024])
                    ps_v = [0, 0, 0, 0]
                    q_t = 0
                    fin_v = [0, 0]
                    th_v = 0
                    qd1 = [0, 0]
                    quse1 = [0, 0]

                    def q1_load(i):
                        p_2, qt_ = divmod(i, NTT)
                        buf = i % 2
                        lane = "db" if buf == 0 else "dc"
                        qap = bass.AP(qscr.tensor,
                                      qscr.offset + p_2 * 128 * T + qt_ * TT,
                                      [[T, 128], [1, TT]])
                        E_.wait("sync", "t", quse1[buf])
                        qd1[buf] = E_.dma(lane, qt1[buf][:], qap)

                    for p in range(NP):
                        for qt in range(NTT):
                            i = p * NTT + qt
                            if i == 0:
                                q1_load(0)
                            qtile1 = qt1[i % 2]
                            qlane = "db" if i % 2 == 0 else "dc"
                            E_.wait("tensor", qlane, qd1[i % 2])
                            if i + 1 < NP * NTT:
                                q1_load(i + 1)
                            for qs in range(4):
                                qsg = qt * 4 + qs
                                vr2 = [0, 0]
                                for kt in range(qt + 1):
                                    bi = (2 * kt) % 4
                                    PA, PB = PS4[bi], PS4[bi + 1]
                                    E_.wait("tensor", "v", ps_v[bi])
                                    E_.wait("tensor", "v", ps_v[bi + 1])
                                    ta = E_.mm(PA, qtile1[0:64, qs * 128:(qs + 1) * 128],
                                               KT[0:64, p * T + kt * TT:
                                                  p * T + (kt + 1) * TT],
                                               start=True, stop=True,
                                               tile_position=(0, 0))
                                    tb_ = E_.mm(PB,
                                                qtile1[64:128, qs * 128:(qs + 1) * 128],
                                                KT[64:128, p * T + kt * TT:
                                                   p * T + (kt + 1) * TT],
                                                start=True, stop=True,
                                                tile_position=(64, 0))
                                    q_t = tb_
                                    for hh, (P, td) in enumerate(((PA, ta), (PB, tb_))):
                                        E_.wait("vector", "t", td)
                                        E_.wait("vector", "v", fin_v[hh])
                                        if kt == qt:
                                            E_.wait("vector", "v", th_v)
                                            pms = bass.AP(pmaskT[:].tensor,
                                                          384 - 128 * qs,
                                                          [[896, 128], [1, TT]])
                                            vmk = E_.v("tensor_tensor", R.th[:], P,
                                                       pms, OP.min)
                                            E_.wait("vector", "v", vmk)
                                            vr = E_.v("tensor_reduce",
                                                      prt[:, hh * 8 + kt:
                                                          hh * 8 + kt + 1],
                                                      R.th[:], AX.X, OP.max)
                                            th_v = vr
                                        else:
                                            vr = E_.v("tensor_reduce",
                                                      prt[:, hh * 8 + kt:
                                                          hh * 8 + kt + 1],
                                                      P, AX.X, OP.max)
                                        vr2[hh] = vr
                                        ps_v[bi + hh] = vr
                                for hh in range(2):
                                    col = qsg * 16 + 2 * p + hh
                                    E_.wait("vector", "v", vr2[hh])
                                    vf = E_.v("tensor_reduce",
                                              mscr[:, col:col + 1],
                                              prt[:, hh * 8:hh * 8 + qt + 1],
                                              AX.X, OP.max)
                                    fin_v[hh] = vf
                            quse1[i % 2] = q_t
                run_phase(nc, sems, cnt, pass1)

        with contextlib.ExitStack() as c3:
            p_ = lambda n, s, d: c3.enter_context(nc.sbuf_tensor(f"a{layer}t" + n, s, d))
            WO = p_("_wo", [128, DC * D], BF16)
            stag = p_("_stag", [128, NP * TT], BF16)
            qtl = [p_("_qt0", [128, TT], BF16), p_("_qt1", [128, TT], BF16)]
            EAB = [p_("_e0", [128, 1024], BF16), p_("_e1", [128, 1024], BF16),
                   p_("_e2", [128, 1024], BF16)]
            srowA = R.y[0:1, 0:TT]
            srowB = R.y[0:1, TT:2 * TT]
            Scol = p_("_scl", [16, TT], F32)
            chT = R.y[0:16, 2 * TT:3 * TT]
            c1m = p_("_c1m", [16, TT], BF16)
            osel = p_("_osl", [16, NP * 128], BF16)
            emaskT = p_("_em", [128, 896], BF16)
            if layer == 0:
                boP = p_("_bop", [128, DC], F32)
                bvb = p_("_bvb", [128, DC], BF16)
            else:
                Mcol = p_("_mcl", [16, TT], F32)
                c2m = p_("_c2m", [16, TT], BF16)
                wsel = p_("_wsl", [16, NP * 128], BF16)
                wcolp = p_("_wcp", [1, H * 64], BF16)

            with contextlib.ExitStack() as cw2:
                wl2 = R.y[:, 0:D // 2]

                def wo_prep(E_):
                    src = Wo if layer == 0 else WprojT
                    prev = 0
                    for c2 in range(2 * DC):
                        c, hf = c2 // 2, c2 % 2
                        E_.wait("sync", "v", prev)
                        dd = E_.dma("da", wl2, bass.AP(
                            src.tensor, src.offset + c * 128 * D + hf * (D // 2),
                            [[D, 128], [1, D // 2]]))
                        E_.wait("vector", "da", dd)
                        prev = E_.v("tensor_copy",
                                    WO[:, c * D + hf * (D // 2):
                                       c * D + (hf + 1) * (D // 2)], wl2)
                        E_.wait("vector", "v", prev)
                    E_.dma("da", osel[:], osel16)
                    dd = E_.dma("da", emaskT[:], emask2)
                    for eng in ("tensor", "vector", "scalar"):
                        E_.wait(eng, "da", dd)
                    if layer == 0:
                        vq = prev
                        for c in range(DC):
                            E_.wait("sync", "v", vq)
                            dd = E_.dma("da", R.y[:, 0:1], bias_ap(bv, c))
                            E_.wait("vector", "da", dd)
                            vq = E_.v("tensor_copy", bvb[:, c:c + 1], R.y[:, 0:1])
                            E_.wait("vector", "v", vq)
                            E_.dma("da", boP[:, c:c + 1], bias_ap(bo, c))
                        dd = E_.c["da"]
                        E_.wait("tensor", "da", dd)
                        E_.wait("tensor", "v", vq)
                        for oc in range(DC):
                            for dc in range(DC):
                                tm = E_.mm(R.PM0[:, oc:oc + 1],
                                           WO[:, dc * D + oc * 128:
                                              dc * D + oc * 128 + 128],
                                           bvb[:, dc:dc + 1],
                                           start=(dc == 0), stop=(dc == DC - 1))
                        E_.wait("vector", "t", tm)
                        E_.wait("vector", "da", dd)
                        vb2 = E_.v("tensor_tensor", boP[:, 0:DC], boP[:, 0:DC],
                                   R.PM0[:, 0:DC], OP.add)
                        E_.wait("tensor", "v", vb2)
                        E_.wait("scalar", "v", vb2)
                    else:
                        vz = E_.v("memset", wcolp[:], 0.0)
                        pb_v = vz
                        for h in range(H):
                            for tc in range(NTC):
                                E_.wait("tensor", "v", pb_v)
                                tm = E_.mm(R.PM0[0:1, 0:64], R.onesc[:],
                                           vsl(tc, h, 64),
                                           start=(tc == 0), stop=(tc == NTC - 1))
                            E_.wait("vector", "t", tm)
                            pb_v = E_.v("tensor_copy",
                                        wcolp[:, h * 64:(h + 1) * 64],
                                        R.PM0[0:1, 0:64])
                            E_.wait("vector", "v", pb_v)
                        E_.wait("tensor", "v", pb_v)
                        vws = E_.v("memset", wsel[:], 0.0)
                        E_.wait("scalar", "v", vws)
                        E_.wait("scalar", "v", pb_v)
                        for h in range(H):
                            c0 = (h // 2) * 128 + 64 * (h % 2)
                            E_.dma("aq", wsel[h:h + 1, c0:c0 + 64],
                                   wcolp[:, h * 64:(h + 1) * 64])
                run_phase(nc, sems, cnt, wo_prep)

            def attn(E_):
                PM = (R.PM0, R.PM1)
                PSS = (R.PSa, R.PSb)
                pm_s = [0, 0]
                pm_v = [0, 0]
                pu_s = 0
                pu_vv = 0
                q_t = 0
                fin_g = 0
                esPM = 0
                proj_t = 0
                scol_gate = 0
                cht_gate = 0
                mcol_gate = 0
                scol_d = 0
                mcol_d = 0
                c1_rdy = 0
                c1_rdy_s = 0
                c2_rdy = 0
                qd = [0, 0]
                quse = [0, 0]

                def q_load(i):
                    qt_, p_2 = divmod(i, NP)
                    buf = i % 2
                    lane = "db" if buf == 0 else "dc"
                    qap = bass.AP(qscr.tensor,
                                  qscr.offset + p_2 * 128 * T + qt_ * TT,
                                  [[T, 128], [1, TT]])
                    E_.wait("sync", "t", quse[buf])
                    qd[buf] = E_.dma(lane, qtl[buf][:], qap)

                PVb = ((R.PU4, R.PU5), (R.PU4, R.PU5))
                pu_free_s = [0, 0]
                pu_free_v = [0, 0]
                for qt in range(NTT):
                    pu_free_s[1] = max(pu_free_s[1], pm_s[0], pm_s[1])
                    pu_free_v[1] = max(pu_free_v[1], pm_v[0], pm_v[1])
                    if layer == 1:
                        E_.wait("sync", "s", scol_gate)
                        E_.wait("sync", "v", mcol_gate)
                        first = True
                        for h in range(H):
                            for qs in range(4):
                                col = (qt * 4 + qs) * 16 + h
                                mcol_d = E_.dma(
                                    "da",
                                    Mcol[h:h + 1, qs * 128:(qs + 1) * 128],
                                    mscr[:, col:col + 1], nowait=not first)
                                first = False
                    for p in range(NP):
                        par = 0
                        bkA, bkB = PVb[par]
                        qi = qt * NP + p
                        if qi == 0:
                            q_load(0)
                        qtile = qtl[qi % 2]
                        qlane = "db" if qi % 2 == 0 else "dc"
                        E_.wait("tensor", qlane, qd[qi % 2])
                        if qi + 1 < NTT * NP:
                            q_load(qi + 1)
                        nkc = (qt + 1) * 4
                        es = [0, 0, 0]
                        pv_t = [0, 0, 0]
                        sc_t = [0, 0, 0]

                        def s_mms(kc):
                            sl = kc % 3
                            E_.wait("tensor", "s", es[sl])
                            if sl < 2:
                                PSSc = PSS[sl]
                                d0 = PSSc[:, 0:TT]
                                d1 = PSSc[:, 512:1024]
                            else:
                                E_.wait("tensor", "v", max(pm_v))
                                E_.wait("tensor", "s", max(pm_s))
                                d0 = R.PM0[:, 0:TT]
                                d1 = R.PM1[:, 0:TT]
                            E_.mm(d0,
                                  KT[0:64, p * T + kc * 128:
                                     p * T + (kc + 1) * 128],
                                  qtile[0:64, :], start=True, stop=True,
                                  tile_position=(0, 0))
                            tb_ = E_.mm(d1,
                                        KT[64:128, p * T + kc * 128:
                                           p * T + (kc + 1) * 128],
                                        qtile[64:128, :], start=True, stop=True,
                                        tile_position=(64, 0))
                            sc_t[sl] = tb_
                            return tb_

                        q_t = s_mms(0)
                        if nkc > 1:
                            q_t = s_mms(1)
                        for kc in range(nkc):
                            sl = kc % 3
                            EB = EAB[sl]
                            E_.wait("scalar", "t", sc_t[sl])
                            E_.wait("scalar", "t", pv_t[sl])
                            if sl < 2:
                                se = E_.s("activation", EB[:], PSS[sl][:],
                                          AF.Exp, scale=0.125)
                            else:
                                E_.s("activation", EB[:, 0:512],
                                     R.PM0[:, 0:TT], AF.Exp, scale=0.125)
                                se = E_.s("activation", EB[:, 512:1024],
                                          R.PM1[:, 0:TT], AF.Exp, scale=0.125)
                                esPM = se
                            es[sl] = se
                            diag = (kc >= qt * 4)
                            if diag:
                                vv = kc - qt * 4
                                E_.wait("vector", "s", se)
                                mrep = bass.AP(emaskT[:].tensor, 384 - 128 * vv,
                                               [[896, 128], [0, 2], [1, 512]])
                                ve = E_.v("tensor_tensor",
                                          EB[:].rearrange("p (a b) -> p a b", b=512),
                                          EB[:].rearrange("p (a b) -> p a b", b=512),
                                          mrep, OP.mult)
                            if kc + 2 < nkc:
                                q_t = s_mms(kc + 2)
                            if diag:
                                E_.wait("tensor", "v", ve)
                            else:
                                E_.wait("tensor", "s", se)
                            if kc == 0:
                                E_.wait("tensor", "s", pu_free_s[par])
                                E_.wait("tensor", "v", pu_free_v[par])
                            islast = (kc == nkc - 1)
                            t1 = E_.mm(bkA[0:65, 0:TT], vsl(kc, 2 * p),
                                       EB[:, 0:512], start=(kc == 0), stop=islast)
                            t2 = E_.mm(bkB[0:65, 0:TT], vsl(kc, 2 * p + 1),
                                       EB[:, 512:1024], start=(kc == 0), stop=islast)
                            pv_t[kc % 3] = t2
                        quse[qi % 2] = q_t
                        E_.wait("vector", "t", t2)
                        E_.wait("vector", "ga", fin_g)
                        E_.wait("vector", "aq", scol_d)
                        va = E_.v("tensor_copy", srowA, bkA[64:65, 0:TT])
                        E_.wait("vector", "v", va)
                        vb_ = E_.v("tensor_copy", srowB, bkB[64:65, 0:TT])
                        E_.wait("scalar", "v", va)
                        E_.wait("scalar", "t", t2)
                        E_.wait("scalar", "t", proj_t)
                        pu_free_s[par] = E_.s("copy",
                                              stag[0:64, p * TT:(p + 1) * TT],
                                              bkA[0:64, 0:TT])
                        E_.wait("vector", "v", vb_)
                        E_.wait("vector", "t", proj_t)
                        pu_free_v[par] = E_.v("tensor_copy",
                                              stag[64:128, p * TT:(p + 1) * TT],
                                              bkB[0:64, 0:TT])
                        if p == 0:
                            E_.wait("scalar", "s", scol_gate)
                        E_.wait("scalar", "v", vb_)
                        E_.dma("aq", Scol[2 * p:2 * p + 1, :], srowA)
                        scol_d = E_.dma("aq", Scol[2 * p + 1:2 * p + 2, :],
                                        srowB, nowait=True)
                    if layer == 0:
                        E_.wait("scalar", "aq", scol_d)
                        E_.wait("scalar", "v", cht_gate)
                        E_.wait("scalar", "v", pm_v[0])
                        E_.wait("scalar", "ga", fin_g)
                        cs = E_.s("activation", chT, Scol[:], AF.Ln)
                        E_.wait("scalar", "s", cs)
                        cs = E_.s("activation", c1m[:], chT, AF.Exp,
                                  scale=-1.0)
                        scol_gate = cs
                        c1_rdy_s = cs
                    else:
                        E_.wait("scalar", "da", mcol_d)
                        E_.wait("scalar", "v", cht_gate)
                        E_.wait("scalar", "v", pm_v[0])
                        E_.wait("scalar", "ga", fin_g)
                        cs = E_.s("activation", chT, Mcol[:], AF.Exp,
                                  scale=0.125)
                        E_.wait("vector", "s", cs)
                        cv = E_.v("tensor_scalar", chT, chT, 1e-6,
                                  None, OP.mult, OP.bypass)
                        E_.wait("vector", "aq", scol_d)
                        E_.wait("vector", "v", cv)
                        cv = E_.v("tensor_tensor", chT, Scol[:], chT,
                                  OP.add)
                        E_.wait("scalar", "v", cv)
                        cs = E_.s("activation", Mcol[:], chT, AF.Ln)
                        E_.wait("scalar", "s", cs)
                        cs = E_.s("activation", chT, Mcol[:], AF.Exp,
                                  scale=-1.0)
                        E_.wait("vector", "s", cs)
                        cv = E_.v("tensor_tensor", Mcol[:], Scol[:], chT,
                                  OP.mult)
                        E_.wait("vector", "v", cv)
                        cv = E_.v("tensor_scalar", Mcol[:], Mcol[:], 1.0,
                                  float(T) * 1e-8, OP.mult, OP.add)
                        E_.wait("scalar", "v", cv)
                        cs = E_.s("activation", Scol[:], Mcol[:], AF.Ln)
                        E_.wait("scalar", "s", cs)
                        cs = E_.s("activation", Mcol[:], Scol[:], AF.Exp,
                                  scale=-1.0)
                        scol_gate = cs
                        E_.wait("vector", "s", cs)
                        cv = E_.v("tensor_tensor", c1m[:], chT, Mcol[:],
                                  OP.mult)
                        cht_gate = cv
                        c1_rdy = cv
                        E_.wait("vector", "v", cv)
                        cv = E_.v("tensor_scalar", c2m[:], Mcol[:], 1e-8,
                                  None, OP.mult, OP.bypass)
                        mcol_gate = cv
                        c2_rdy = cv
                    bank_v = [pm_v[0], pm_v[1]]
                    ap_v = 0
                    for p in range(NP):
                        stg = stag[:, p * TT:(p + 1) * TT]
                        E_.wait("tensor", "v", c1_rdy)
                        E_.wait("tensor", "s", c1_rdy_s)
                        E_.wait("tensor", "s", max(pu_free_s))
                        E_.wait("tensor", "v", max(pu_free_v))
                        E_.wait("tensor", "s", esPM)
                        E_.wait("tensor", "v", bank_v[0])
                        E_.wait("tensor", "s", pm_s[0])
                        tb0 = E_.mm(PM[0][:, 0:TT],
                                    osel[:, p * 128:(p + 1) * 128],
                                    c1m[:], start=True, stop=True)
                        if layer == 1:
                            E_.wait("tensor", "v", c2_rdy)
                            E_.wait("tensor", "v", bank_v[1])
                            E_.wait("tensor", "s", pm_s[1])
                            tb1 = E_.mm(PM[1][:, 0:TT],
                                        wsel[:, p * 128:(p + 1) * 128],
                                        c2m[:], start=True, stop=True)
                        E_.wait("vector", "t", tb0)
                        E_.wait("vector", "s", esPM)
                        E_.wait("vector", "s", max(pu_free_s))
                        E_.wait("vector", "v", max(pu_free_v))
                        E_.wait("vector", "v", ap_v)
                        if layer == 0:
                            vap = E_.v("tensor_tensor", stg, stg,
                                       PM[0][:, 0:TT], OP.mult)
                            bank_v[0] = vap
                        else:
                            vth = E_.v("tensor_tensor", R.th[:], stg,
                                       PM[0][:, 0:TT], OP.mult)
                            bank_v[0] = vth
                            E_.wait("vector", "t", tb1)
                            E_.wait("vector", "v", vth)
                            vap = E_.v("tensor_tensor", stg, R.th[:],
                                       PM[1][:, 0:TT], OP.add)
                            bank_v[1] = vap
                        ap_v = vap
                    pm_v = [ap_v, ap_v]
                    se = 0
                    E_.wait("scalar", "ga", fin_g)
                    E_.wait("scalar", "aq", scol_d)
                    E_.wait("scalar", "s", scol_gate)
                    E_.wait("scalar", "v", cht_gate)
                    for oc in range(DC):
                        P = PM[oc % 2]
                        E_.wait("tensor", "s", pm_s[oc % 2])
                        E_.wait("tensor", "v", ap_v)
                        for dc in range(DC):
                            tm = E_.mm(P[:, 0:TT],
                                       WO[:, dc * D + oc * 128:
                                          dc * D + oc * 128 + 128],
                                       stag[:, dc * TT:(dc + 1) * TT],
                                       start=(dc == 0), stop=(dc == DC - 1))
                        E_.wait("scalar", "t", tm)
                        if layer == 0:
                            se = E_.s("activation", R.y[:, oc * TT:(oc + 1) * TT],
                                      P[:, 0:TT], AF.Identity,
                                      bias=boP[:, oc:oc + 1], scale=1.0)
                        else:
                            se = E_.s("copy", R.y[:, oc * TT:(oc + 1) * TT],
                                      P[:, 0:TT])
                        pm_s[oc % 2] = se
                    proj_t = tm
                    vl, gd = R.finish_tile(E_, xout, T, qt, se, PM[0], PM[1],
                                           0, max(pm_v))
                    pm_v = [vl, vl]
                    fin_g = gd
                fin = E_.c["ga"]
                E_.wait("gpsimd", "ga", fin)
            run_phase(nc, sems, cnt, attn)



def build_l1ffn(nc, sems, cnt, inp, xin, xout, T):
    NTT = T // TT
    k0 = inp("l1f_k0", [2 * D, F1], U8)
    k1 = inp("l1f_k1", [F1, F1], U8)
    k2 = inp("l1f_k2", [F1, D], U8)
    s0 = inp("l1f_s0", [1, 1], F32)
    s1 = inp("l1f_s1", [1, 1], F32)
    s2 = inp("l1f_s2", [1, 1], F32)
    k1scr = nc.dram_tensor("k1scr", [FC * 128, FC * 128], BF16).ap()
    k2scr2 = nc.dram_tensor("k2scr2", [F1, D], BF16).ap()

    with contextlib.ExitStack() as ctx:
        sb = lambda n, s, d: ctx.enter_context(nc.sbuf_tensor("f1" + n, s, d))
        R = Res(nc, ctx, "f1")
        k0e = sb("_k0e", [128, DC * F1], BF16)
        h1 = sb("_h1", [128, FC * TT], BF16)
        g2 = sb("_g2", [128, FC * TT], BF16)
        k1s = [sb("_k1a", [128, FC * 128], BF16), sb("_k1b", [128, FC * 128], BF16)]
        k2s = [sb("_k2a", [128, D], BF16), sb("_k2b", [128, D], BF16)]
        ka, kb = k1s[0], k1s[1]
        xt = sb("_xt", [128, DC * TT], F32)
        xhat = sb("_xhat", [128, DC * TT], BF16)
        scl = sb("_scl", [128, 3], F32)
        rrow = sb("_rrow", [1, TT], F32)

        copy_x_phase(nc, sems, cnt, xin, xout, T)

        def prep(E_):
            ic = R.init_consts(E_)
            prev = 0
            for dc in range(DC):
                E_.wait("gpsimd", "v", prev)
                d1 = E_.dma("ga", ka[:], chunk_ap(k0, dc, F1))
                d2 = E_.dma("ga", kb[:], chunk_ap(k0, dc + DC, F1))
                E_.wait("vector", "ga", d2)
                prev = E_.v("tensor_tensor", k0e[:, dc * F1:(dc + 1) * F1],
                            ka[:], kb[:], OP.subtract)
                E_.wait("vector", "v", prev)
            kd = [0, 0]
            kv = [prev, prev]
            for fc1 in range(FC):
                buf = fc1 % 2
                kb_ = (ka, kb)[buf]
                lane = "gb" if buf == 0 else "gc"
                E_.wait("gpsimd", "dd", kd[buf])
                E_.wait("gpsimd", "v", kv[buf])
                dl = E_.dma(lane, kb_[:], chunk_ap(k1, fc1, F1))
                E_.wait("sync", lane, dl)
                dst = bass.AP(k1scr.tensor, k1scr.offset + fc1 * 128,
                              [[FC * 128, 128], [FC * 16384, FC], [1, 128]])
                kd[buf] = E_.dma("dd", dst,
                                 kb_[:].rearrange("r (a c) -> r a c", c=128))
            k2ld = [0, 0]
            k2st = [0, 0]
            for fc2 in range(FC):
                buf = fc2 % 2
                lane = "gb" if buf == 0 else "gc"
                E_.wait("gpsimd", "dd", k2st[buf])
                dl = E_.dma(lane, k2s[buf][:], chunk_ap(k2, fc2, D))
                E_.wait("sync", lane, dl)
                k2st[buf] = E_.dma("dd", chunk_ap(k2scr2, fc2, D), k2s[buf][:])
            E_.wait("sync", "dd", E_.c["dd"])
            eps_guard_v = ic
            eps_guard_t = 0
            for i, (sc, mult) in enumerate(((s0, 1.0), (s1, 4.0), (s2, 4.0))):
                E_.wait("sync", "v", eps_guard_v)
                E_.wait("sync", "t", eps_guard_t)
                dd = E_.dma("da", R.eps11[:], sc)
                E_.wait("tensor", "da", dd)
                tm = E_.mm(R.PM0[:, 0:1], R.ones1[0:1, :], R.eps11[:],
                           start=True, stop=True)
                E_.wait("vector", "t", tm)
                vv = E_.v("tensor_scalar", scl[:, i:i + 1], R.PM0[:, 0:1],
                          mult, None, OP.mult, OP.bypass)
                E_.wait("tensor", "v", vv)
                eps_guard_t = tm
            E_.wait("vector", "t", tm)
            vv = E_.v("memset", R.eps11[:], 1e-5)
            E_.wait("scalar", "v", vv)
        run_phase(nc, sems, cnt, prep)

        def main(E_):
            P4 = (R.PM0[:, 0:512], R.PM1[:, 0:512],
                  R.PSa[:, 0:512], R.PSb[:, 0:512])
            P8 = (R.PM0[:, 0:512], R.PM1[:, 0:512],
                  R.PSa[:, 0:512], R.PSb[:, 0:512],
                  R.PSa[:, 512:1024], R.PSb[:, 512:1024],
                  R.PU4[:, 0:512], R.PU5[:, 0:512])
            pm_v = [0, 0, 0, 0, 0, 0, 0, 0]
            h1_read_t = 0
            g2_read_t = 0
            k1_t = [0, 0]
            k1_d = [0, 0]
            k2_t = [0, 0]
            k2_d = [0, 0]
            ln_t = 0
            ln_v = 0
            for tt in range(NTT):
                E_.wait("sync", "v", ln_v)
                dd = E_.dma("da", xt[:], x_tile_ap(xin, T, tt))
                for eng in ("tensor", "vector", "scalar"):
                    E_.wait(eng, "da", dd)
                tb = R.ln_rows(E_, xt, R.PU4, R.PU5, ln_t, ln_v)
                ln_v = R.ln_apply(E_, xt, xhat, R.PU4, R.PU5, tb)
                E_.wait("tensor", "v", ln_v)
                for fc in range(FC):
                    bi = fc % 2
                    P = P4[bi]
                    E_.wait("tensor", "v", pm_v[bi])
                    E_.wait("tensor", "t", h1_read_t)
                    for dc in range(DC):
                        lhs = k0e[:, dc * F1 + fc * 128: dc * F1 + fc * 128 + 128]
                        tm = E_.mm(P, lhs, xhat[:, dc * TT:(dc + 1) * TT],
                                   start=(dc == 0), stop=(dc == DC - 1))
                    E_.wait("vector", "t", tm)
                    ve = E_.v("tensor_scalar", h1[:, fc * TT:(fc + 1) * TT],
                              P, scl[:, 0:1], 0.0, OP.mult, OP.max)
                    pm_v[bi] = ve
                    E_.wait("vector", "v", ve)
                h1_v = ve
                E_.wait("tensor", "v", h1_v)
                for fc in range(FC):
                    tm = E_.mm(R.PU4[0:1, 0:TT], R.onesc[:],
                               h1[:, fc * TT:(fc + 1) * TT],
                               start=(fc == 0), stop=(fc == FC - 1))
                E_.wait("vector", "t", tm)
                vr = E_.v("tensor_scalar", rrow[:], R.PU4[0:1, 0:TT], -0.5, None,
                          OP.mult, OP.bypass)
                E_.wait("tensor", "v", vr)
                rr_t = tm
                for fc2 in range(FC):
                    buf = fc2 % 2
                    lane = "db" if buf == 0 else "dc"
                    E_.wait("sync", "t", k1_t[buf])
                    k1ap = bass.AP(k1scr.tensor,
                                   k1scr.offset + fc2 * FC * 16384,
                                   [[FC * 128, 128], [1, FC * 128]])
                    k1_d[buf] = E_.dma(lane, k1s[buf][:], k1ap)
                    bi = 2 + fc2 % 2
                    P = P4[bi]
                    E_.wait("tensor", "v", pm_v[bi])
                    E_.wait("tensor", "t", g2_read_t)
                    tm = E_.mm(P, R.ones1[0:1, :], rrow[:],
                               start=True, stop=False)
                    E_.wait("tensor", lane, k1_d[buf])
                    for fc in range(FC):
                        tm = E_.mm(P, k1s[buf][:, fc * 128:(fc + 1) * 128],
                                   h1[:, fc * TT:(fc + 1) * TT],
                                   start=False, stop=(fc == FC - 1))
                    k1_t[buf] = tm
                    E_.wait("vector", "t", tm)
                    ve = E_.v("tensor_scalar", g2[:, fc2 * TT:(fc2 + 1) * TT],
                              P, scl[:, 1:2], 0.0, OP.mult, OP.max)
                    pm_v[bi] = ve
                    E_.wait("vector", "v", ve)
                g2_v = ve
                h1_read_t = tm
                E_.wait("tensor", "v", g2_v)
                E_.wait("tensor", "v", E_.c["v"])
                for fc in range(FC):
                    tm = E_.mm(R.PU4[0:1, 0:TT], R.onesc[:],
                               g2[:, fc * TT:(fc + 1) * TT],
                               start=(fc == 0), stop=(fc == FC - 1))
                E_.wait("vector", "t", tm)
                vr = E_.v("tensor_scalar", rrow[:], R.PU4[0:1, 0:TT], -0.5, None,
                          OP.mult, OP.bypass)
                E_.wait("tensor", "v", vr)
                ogt = [0] * 8
                for fc2 in range(FC):
                    buf = fc2 % 2
                    lane = "db" if buf == 0 else "dc"
                    E_.wait("sync", "t", k2_t[buf])
                    k2_d[buf] = E_.dma(lane, k2s[buf][:],
                                       chunk_ap(k2scr2, fc2, D))
                    E_.wait("tensor", lane, k2_d[buf])
                    for oc in range(DC):
                        P = P8[oc]
                        if fc2 == 0:
                            E_.wait("tensor", "v", pm_v[oc])
                            if oc == 6:
                                E_.wait("tensor", "v", vr)
                            E_.mm(P, R.ones1[0:1, :], rrow[:],
                                  start=True, stop=False)
                        tm = E_.mm(P, k2s[buf][:, oc * 128:(oc + 1) * 128],
                                   g2[:, fc2 * TT:(fc2 + 1) * TT],
                                   start=False, stop=(fc2 == FC - 1))
                        ogt[oc] = tm
                    k2_t[buf] = tm
                g2_read_t = tm
                for oc in range(DC):
                    E_.wait("vector", "t", ogt[oc])
                    ve = E_.v("tensor_scalar", R.y[:, oc * TT:(oc + 1) * TT],
                              P8[oc], scl[:, 2:3], 0.0, OP.mult, OP.bypass)
                    pm_v[oc] = ve
                    E_.wait("vector", "v", ve)
                E_.wait("scalar", "v", ve)
                E_.wait("tensor", "v", ve)
                vl, gd = R.finish_tile(E_, xout, T, tt, E_.c["s"], R.PU4, R.PU5,
                                       max(ogt[6], ogt[7]),
                                       max(pm_v[6], pm_v[7], ln_v))
                ln_v = vl
                ln_t = E_.c["t"]
            fin = E_.c["ga"]
            E_.wait("gpsimd", "ga", fin)
        run_phase(nc, sems, cnt, main)



PARTS = ("l0attn", "l0ffn", "l1attn", "l1ffn")


def build_program(parts, T):
    nc = bass.Bass("TRN2", target_bir_lowering=False)
    declared = {}

    def inp(name, shape, dt):
        if name not in declared:
            declared[name] = nc.dram_tensor(name, shape, dt,
                                            kind="ExternalInput").ap()
        return declared[name]

    xin = nc.dram_tensor("xT", [D, T], F32, kind="ExternalInput").ap()
    yout = nc.dram_tensor("yT", [D, T], F32, kind="ExternalOutput").ap()
    xs = [nc.dram_tensor("xs0", [D, T], F32).ap(),
          nc.dram_tensor("xs1", [D, T], F32).ap()]
    sems = {}
    cnt = {}
    with contextlib.ExitStack() as ctx:
        for s in tuple(LANES) + ("t", "v", "s"):
            sems[s] = ctx.enter_context(nc.semaphore("sem_" + s))
            cnt[s] = 0
        for i, part in enumerate(parts):
            src = xin if i == 0 else xs[(i - 1) % 2]
            dst = yout if i == len(parts) - 1 else xs[i % 2]
            if part == "l0attn":
                build_attn(nc, sems, cnt, inp, src, dst, T, 0)
            elif part == "l0ffn":
                build_l0ffn(nc, sems, cnt, inp, src, dst, T)
            elif part == "l1attn":
                build_attn(nc, sems, cnt, inp, src, dst, T, 1)
            elif part == "l1ffn":
                build_l1ffn(nc, sems, cnt, inp, src, dst, T)
            else:
                raise ValueError(part)
    return nc, set(declared.keys())


def host_consts():
    j = np.arange(512)
    r = np.arange(128)[:, None]
    c = np.arange(896)
    emask2 = (c[None, :] >= r + 384).astype(ml_dtypes.bfloat16)
    pmask = np.where(c[None, :] <= r + 384, 3.0e38, -3.0e38).astype(
        ml_dtypes.bfloat16)
    osel = np.zeros((16, NP * 128), np.float32)
    for p in range(NP):
        osel[2 * p, p * 128:p * 128 + 64] = 1.0
        osel[2 * p + 1, p * 128 + 64:p * 128 + 128] = 1.0
    return {"emask2": emask2, "pmask": pmask,
            "osel16": osel.astype(ml_dtypes.bfloat16)}


def host_weights(inputs):
    c = np.ascontiguousarray
    w = {
        "l0a_wq": c(np.asarray(inputs["L0_attn_Wq"], np.float32)),
        "l0a_wk": c(np.asarray(inputs["L0_attn_Wk"], np.float32)),
        "l0a_wv": c(np.asarray(inputs["L0_attn_Wv"], np.float32)),
        "l0a_wo": c(np.asarray(inputs["L0_attn_Wo"], np.float32)),
        "l0a_bq": c(np.asarray(inputs["L0_attn_bq"], np.float32)),
        "l0a_bk": c(np.asarray(inputs["L0_attn_bk"], np.float32)),
        "l0a_bv": c(np.asarray(inputs["L0_attn_bv"], np.float32)),
        "l0a_bo": c(np.asarray(inputs["L0_attn_bo"], np.float32)),
        "l0f_w1t": c(np.asarray(inputs["L0_ffn_W1"], np.float32).T),
        "l0f_w2t": c(np.asarray(inputs["L0_ffn_W2"], np.float32).T),
        "l0f_b1": c(np.asarray(inputs["L0_ffn_b1"], np.float32)),
        "l0f_b2": c(np.asarray(inputs["L0_ffn_b2"], np.float32)),
        "l1a_wqkvt": c(np.asarray(inputs["L1_attn_w_qkv"], np.float32).T),
        "l1a_wprojt": c(np.asarray(inputs["L1_attn_w_proj"], np.float32).T),
        "l1f_k0": c(np.asarray(inputs["L1_ffn_k0"])).view(np.uint8),
        "l1f_k1": c(np.asarray(inputs["L1_ffn_k1"])).view(np.uint8),
        "l1f_k2": c(np.asarray(inputs["L1_ffn_k2"])).view(np.uint8),
        "l1f_s0": np.asarray(inputs["L1_ffn_s0"], np.float32).reshape(1, 1),
        "l1f_s1": np.asarray(inputs["L1_ffn_s1"], np.float32).reshape(1, 1),
        "l1f_s2": np.asarray(inputs["L1_ffn_s2"], np.float32).reshape(1, 1),
    }
    w.update(host_consts())
    return w


def kernel(**inputs):
    T = 4096
    x = np.asarray(inputs["L0_attn_x"], np.float32)
    w = host_weights(inputs)
    nc, names = build_program(PARTS, T)
    in_maps = []
    for ci in range(NCORES):
        m = {k: v for k, v in w.items() if k in names}
        m["xT"] = np.ascontiguousarray(x[ci].T)
        in_maps.append(m)
    res = run_bass_kernel_spmd(nc, in_maps, core_ids=list(range(NCORES)))
    out = np.stack([np.asarray(res.results[ci]["yT"]).T for ci in range(NCORES)])
    return out.astype(np.float32)


# version 14
# speedup vs baseline: 1.0546x; 1.0199x over previous
import contextlib
import numpy as np
import ml_dtypes
import concourse.bass as bass
import concourse.mybir as mybir
from concourse.bass_utils import run_bass_kernel_spmd
from concourse.alu_op_type import AluOpType as OP

F32 = mybir.dt.float32
F32R = mybir.dt.float32r
BF16 = mybir.dt.bfloat16
U8 = mybir.dt.uint8
AF = mybir.ActivationFunctionType
AX = mybir.AxisListType

DEBUG_L1 = False
D = 1024
DC = 8
F1 = 4096
FC = 32
H = 16
NP = 8
TT = 512
NCORES = 8

ENGINES = ("sync", "gpsimd", "tensor", "vector", "scalar")
LANES = {"da": "sync", "db": "sync", "dc": "sync", "dd": "sync",
         "ga": "gpsimd", "gb": "gpsimd", "gc": "gpsimd", "aq": "scalar"}


class E:
    def __init__(self, nc, cur, handle, sems, cnt):
        self.nc = nc
        self.cur = cur
        self.h = handle
        self.sems = sems
        self.c = cnt

    def wait(self, who, sem, val):
        if self.cur == who and val > 0:
            self.h.wait_ge(self.sems[sem], val)

    def _emit(self, who, sem, meth, *a, **kw):
        self.c[sem] += 1
        if self.cur == who:
            getattr(self.h, meth)(*a, **kw).then_inc(self.sems[sem], 1)
        return self.c[sem]

    def mm(self, out, lhsT, rhs, **kw):
        return self._emit("tensor", "t", "matmul", out, lhsT, rhs, **kw)

    def v(self, meth, *a, **kw):
        return self._emit("vector", "v", meth, *a, **kw)

    def s(self, meth, *a, **kw):
        return self._emit("scalar", "s", meth, *a, **kw)

    def dma(self, lane, out, in_, nowait=False, **kw):
        prev = self.c[lane]
        self.c[lane] += 16
        if self.cur == LANES[lane]:
            if prev > 0 and not nowait:
                self.h.wait_ge(self.sems[lane], prev)
            self.h.dma_start(out, in_, **kw).then_inc(self.sems[lane], 16)
        return self.c[lane]


def drain_lanes(E_):
    for lane, eng in LANES.items():
        E_.wait(eng, lane, E_.c[lane])


def run_phase(nc, sems, cnt, fn):
    finals = []
    snap = dict(cnt)
    with nc.Block() as block:
        for name in ENGINES:
            def mk(n):
                def f(h):
                    local = dict(snap)
                    fn(E(nc, n, h, sems, local))
                    drain_lanes(E(nc, n, h, sems, local))
                    finals.append(local)
                return f
            getattr(block, name)(mk(name))
    for f in finals[1:]:
        assert f == finals[0], f"engine passes disagree: {finals[0]} vs {f}"
    cnt.clear()
    cnt.update(finals[0])


def r32(ap):
    return ap.bitcast(F32R)


def x_tile_ap(x_dram, T, tt):
    return bass.AP(x_dram.tensor, x_dram.offset + tt * TT,
                   [[T, 128], [128 * T, DC], [1, TT]])


def bias_ap(b_dram, c):
    return bass.AP(b_dram.tensor, b_dram.offset + c * 128, [[1, 128], [1, 1]])


def chunk_ap(w_dram, c, nfree, w=None):
    return bass.AP(w_dram.tensor, w_dram.offset + c * 128 * nfree,
                   [[nfree, 128], [1, nfree if w is None else w]])


def flat_ap(t_dram, n):
    return bass.AP(t_dram.tensor, t_dram.offset, [[1, n]])


class Res:

    def __init__(self, nc, ctx, tag):
        sb = lambda n, s, d: ctx.enter_context(nc.sbuf_tensor(tag + n, s, d))
        ps = lambda n, w: ctx.enter_context(nc.psum_tensor(tag + n, [128, w], F32))
        self.PMd = ps("_PMd", 1024)
        self.PM0 = self.PMd[:, 0:512]
        self.PM1 = self.PMd[:, 512:1024]
        self.PSa = ps("_PSa", 1024)
        self.PSb = ps("_PSb", 1024)
        self.PU4 = ps("_PU4", 512)
        self.PU5 = ps("_PU5", 512)
        self.y = sb("_y", [128, DC * TT], F32)
        self.xsq = [sb("_xsq0", [128, TT], BF16), sb("_xsq1", [128, TT], BF16)]
        self.xb = [sb("_xb0", [128, TT], BF16), sb("_xb1", [128, TT], BF16)]
        self.th = sb("_th", [128, TT], F32)
        self.ones128 = sb("_o128", [128, 128], BF16)
        self.ones128f = sb("_o128f", [128, 128], F32)
        self.ones1 = sb("_o1", [33, 128], F32)
        self.onesc = sb("_oc", [128, 1], BF16)
        self.eps11 = sb("_eps", [1, 1], F32)
        self.rmu = sb("_rmu", [1, TT], F32)
        self.rt1 = sb("_rt1", [1, TT], F32)
        self.rt2 = sb("_rt2", [1, TT], F32)

    def init_consts(self, E_):
        E_.v("memset", self.ones128[:], 1.0)
        E_.v("memset", self.ones128f[:], 1.0)
        E_.v("memset", self.ones1[:], 1.0)
        E_.v("memset", self.onesc[:], 1.0)
        return E_.v("memset", self.eps11[:], 1e-5)

    def ln_rows(self, E_, src_sb, PA, PB, pa_t_free, pa_v_free):
        E_.wait("tensor", "v", pa_v_free)
        E_.wait("tensor", "t", pa_t_free)
        xg = [E_.c["t"], E_.c["t"]]
        for c in range(DC):
            sl = src_sb[:, c * TT:(c + 1) * TT]
            xq = self.xsq[c % 2]
            xb = self.xb[c % 2]
            E_.wait("vector", "t", xg[c % 2])
            vcb = E_.v("tensor_copy", xb[:], sl)
            E_.wait("scalar", "t", xg[c % 2])
            ssq = E_.s("activation", xq[:], sl, AF.Square)
            E_.wait("tensor", "v", vcb)
            E_.mm(PA[:, 0:TT], self.ones128[:], xb[:],
                  start=(c == 0), stop=(c == DC - 1))
            E_.wait("tensor", "s", ssq)
            tmm = E_.mm(PB[:, 0:TT], self.ones128[:], xq[:],
                        start=(c == 0), stop=(c == DC - 1))
            xg[c % 2] = tmm
        nfeat = float(D)
        E_.wait("vector", "t", tmm)
        v1 = E_.v("tensor_scalar", self.rmu[:], PA[0:1, 0:TT], 1.0 / nfeat,
                  None, OP.mult, OP.bypass)
        E_.wait("vector", "v", v1)
        v2 = E_.v("tensor_tensor", self.rt1[:], self.rmu[:], self.rmu[:], OP.mult)
        E_.wait("vector", "v", v2)
        v3 = E_.v("scalar_tensor_tensor", self.rt2[:], PB[0:1, 0:TT],
                  1.0 / nfeat, self.rt1[:], OP.mult, OP.subtract)
        E_.wait("scalar", "v", v3)
        s1 = E_.s("activation", self.rt1[:], self.rt2[:], AF.Ln,
                  bias=self.eps11[:], scale=1.0)
        E_.wait("scalar", "s", s1)
        s2 = E_.s("activation", self.rt2[:], self.rt1[:], AF.Exp,
                  scale=-0.5)
        E_.wait("vector", "s", s2)
        v5 = E_.v("scalar_tensor_tensor", self.rt1[:], self.rmu[:], -1.0,
                  self.rt2[:], OP.mult, OP.mult)
        E_.wait("tensor", "v", v5)
        E_.mm(PA[:, 0:TT], self.ones1[0:1, :], self.rt2[:],
              start=True, stop=True)
        tb = E_.mm(PB[:, 0:TT], self.ones1[0:1, :], self.rt1[:],
                   start=True, stop=True)
        return tb

    def ln_apply(self, E_, src_sb, dst_sb, PA, PB, t_bcast):
        E_.wait("vector", "t", t_bcast)
        for c in range(DC):
            sl = src_sb[:, c * TT:(c + 1) * TT]
            dl = dst_sb[:, c * TT:(c + 1) * TT]
            va = E_.v("tensor_tensor", self.th[:], sl, PA[:, 0:TT], OP.mult)
            E_.wait("vector", "v", va)
            last = E_.v("tensor_tensor", dl, self.th[:], PB[:, 0:TT], OP.add)
            E_.wait("vector", "v", last)
        return last

    def finish_tile(self, E_, xout, T, tt, last_s, PA, PB, t_free, v_free):
        E_.wait("vector", "s", last_s)
        E_.wait("scalar", "s", last_s)
        E_.wait("tensor", "s", last_s)
        tb = self.ln_rows(E_, self.y, PA, PB, max(t_free, E_.c["t"]),
                          max(v_free, E_.c["v"]))
        vl = self.ln_apply(E_, self.y, self.y, PA, PB, tb)
        E_.wait("gpsimd", "v", vl)
        gd = E_.dma("ga", x_tile_ap(xout, T, tt), self.y[:], accum_op=OP.add)
        return vl, gd


def copy_x_phase(nc, sems, cnt, xin, xout, T):
    def cp(E_):
        gd = E_.dma("ga", flat_ap(xout, D * T), flat_ap(xin, D * T))
        E_.wait("gpsimd", "ga", gd)
    run_phase(nc, sems, cnt, cp)



def build_l0ffn(nc, sems, cnt, inp, xin, xout, T):
    NTT = T // TT
    W1T = inp("l0f_w1t", [D, F1], F32)
    W2T = inp("l0f_w2t", [F1, D], F32)
    b1 = inp("l0f_b1", [F1], F32)
    b2 = inp("l0f_b2", [D], F32)
    w2scr = nc.dram_tensor("w2scr", [FC * 128, D], BF16).ap()

    with contextlib.ExitStack() as ctx:
        sb = lambda n, s, d: ctx.enter_context(nc.sbuf_tensor("f0" + n, s, d))
        R = Res(nc, ctx, "f0")
        W1q = sb("_w1q", [128, DC * F1], BF16)
        hbuf = sb("_h", [128, FC * TT], BF16)
        b1t = sb("_b1t", [128, FC], F32)
        b2t = sb("_b2t", [128, DC], F32)

        copy_x_phase(nc, sems, cnt, xin, xout, T)

        with contextlib.ExitStack() as cpp:
            p_ = lambda n, s, d: cpp.enter_context(nc.sbuf_tensor("f0p" + n, s, d))
            wls = [p_("_wl0", [128, F1], F32), p_("_wl1", [128, F1], F32)]
            wsg = p_("_wsg", [128, 1024], F32)
            wab = p_("_wab", [128, 1024], F32)
            w2qs = [p_("_w2q0", [128, 1024], BF16), p_("_w2q1", [128, 1024], BF16)]
            acc = p_("_acc", [128, 1], F32)
            part = p_("_part", [128, 1], F32)
            thr = p_("_thr", [128, 1], F32)

            def prep(E_):
                R.init_consts(E_)
                for c in range(FC):
                    E_.dma("da", b1t[:, c:c + 1], bias_ap(b1, c))
                for c in range(DC):
                    E_.dma("da", b2t[:, c:c + 1], bias_ap(b2, c))
                wg_v = [0, 0]
                wg_s = [0, 0]
                st_d = [0, 0]
                q_guard = 0
                ld = [0, 0]
                lane_of = ("da", "db")

                def wload(WT, c, nfree):
                    buf = c % 2
                    E_.wait("sync", "v", wg_v[buf])
                    E_.wait("sync", "s", wg_s[buf])
                    ld[buf] = E_.dma(lane_of[buf], wls[buf][:, 0:nfree],
                                     chunk_ap(WT, c, nfree))

                for wi, (WT, nch, nfree) in enumerate([(W1T, DC, F1), (W2T, FC, D)]):
                    vacc = E_.v("memset", acc[:], 0.0)
                    wload(WT, 0, nfree)
                    for c in range(nch):
                        buf = c % 2
                        if c + 1 < nch:
                            wload(WT, c + 1, nfree)
                        E_.wait("vector", lane_of[buf], ld[buf])
                        E_.wait("vector", "v", vacc)
                        vp = E_.v("tensor_reduce", part[:], wls[buf][:, 0:nfree],
                                  AX.X, OP.add, apply_absolute_value=True)
                        E_.wait("vector", "v", vp)
                        vacc = E_.v("tensor_tensor", acc[:], acc[:], part[:], OP.add)
                        wg_v[buf] = vp
                    E_.wait("tensor", "v", vacc)
                    tg = E_.mm(R.PM0[:, 0:1], R.ones128f[:], acc[:],
                               start=True, stop=True)
                    E_.wait("vector", "t", tg)
                    nel = float(nch * 128 * nfree)
                    vthr = E_.v("tensor_scalar", thr[:], R.PM0[:, 0:1],
                                0.5 / nel, 0.5e-5, OP.mult, OP.add)
                    wload(WT, 0, nfree)
                    for c in range(nch):
                        buf = c % 2
                        if c + 1 < nch:
                            wload(WT, c + 1, nfree)
                        for q4 in range(nfree // 1024):
                            wlsl = wls[buf][:, q4 * 1024:(q4 + 1) * 1024]
                            E_.wait("scalar", lane_of[buf], ld[buf])
                            E_.wait("scalar", "v", q_guard)
                            ssg = E_.s("sign", wsg[:], wlsl)
                            sab = E_.s("activation", wab[:], wlsl, AF.Abs)
                            E_.wait("vector", "s", sab)
                            E_.wait("vector", "v", vthr)
                            vc = E_.v("tensor_scalar", wab[:], wab[:], thr[:], None,
                                      OP.is_ge, OP.bypass)
                            E_.wait("vector", "v", vc)
                            E_.wait("vector", "s", ssg)
                            if wi == 0:
                                dst = W1q[:, c * F1 + q4 * 1024:
                                          c * F1 + (q4 + 1) * 1024]
                            else:
                                dst = w2qs[buf][:]
                                E_.wait("vector", "dc" if buf == 0 else "dd",
                                        st_d[buf])
                            vq = E_.v("tensor_tensor", dst, wab[:], wsg[:], OP.mult)
                            E_.wait("vector", "v", vq)
                            if wi == 1:
                                E_.wait("sync", "v", vq)
                                st_d[buf] = E_.dma("dc" if buf == 0 else "dd",
                                                   chunk_ap(w2scr, c, D),
                                                   w2qs[buf][:])
                            wg_s[buf] = sab
                            q_guard = vq
            run_phase(nc, sems, cnt, prep)

        with contextlib.ExitStack() as cm:
            m_ = lambda n, s, d: cm.enter_context(nc.sbuf_tensor("f0m" + n, s, d))
            xts = [m_("_xt0", [128, DC * TT], F32),
                   m_("_xt1", [128, DC * TT], F32)]
            xhat = m_("_xhat", [128, DC * TT], BF16)
            w2sl = [m_("_w2a", [128, FC * 128], BF16),
                    m_("_w2b", [128, FC * 128], BF16)]

            def main(E_):
                PM = (R.PM0, R.PM1)
                pm_s = [0, 0]
                h_read_t = 0
                w2_t = [0, 0]
                w2_d = [0, 0]
                ln_t = 0
                ln_v = 0
                xg = [0, 0]
                xd = [0, 0]
                def x_load(tt_):
                    b_ = tt_ % 2
                    E_.wait("sync", "v", xg[b_])
                    xd[b_] = E_.dma("dd", xts[b_][:], x_tile_ap(xin, T, tt_))
                for tt in range(NTT):
                    xb_ = tt % 2
                    xt = xts[xb_]
                    if tt == 0:
                        x_load(0)
                    if tt + 1 < NTT:
                        x_load(tt + 1)
                    for eng in ("tensor", "vector", "scalar"):
                        E_.wait(eng, "dd", xd[xb_])
                    tb = R.ln_rows(E_, xt, R.PU4, R.PU5, ln_t, ln_v)
                    va = R.ln_apply(E_, xt, xhat, R.PU4, R.PU5, tb)
                    ln_v = va
                    xg[xb_] = va
                    E_.wait("tensor", "v", va)
                    evac_last = 0
                    for fc in range(FC):
                        P = PM[fc % 2]
                        E_.wait("tensor", "s", pm_s[fc % 2])
                        E_.wait("tensor", "t", h_read_t)
                        for dc in range(DC):
                            lhs = W1q[:, dc * F1 + fc * 128: dc * F1 + fc * 128 + 128]
                            tm = E_.mm(P[:, 0:TT], lhs,
                                       xhat[:, dc * TT:(dc + 1) * TT],
                                       start=(dc == 0), stop=(dc == DC - 1))
                        E_.wait("scalar", "t", tm)
                        se = E_.s("activation", hbuf[:, fc * TT:(fc + 1) * TT],
                                  P[:, 0:TT], AF.Relu, bias=b1t[:, fc:fc + 1],
                                  scale=1.0)
                        pm_s[fc % 2] = se
                        evac_last = se
                    for oc in range(DC):
                        buf = oc % 2
                        E_.wait("sync", "t", w2_t[buf])
                        w2ap = bass.AP(w2scr.tensor, w2scr.offset + oc * 128,
                                       [[D, 128], [128 * D, FC], [1, 128]])
                        w2_d[buf] = E_.dma("da", w2sl[buf][:], w2ap)
                        P = PM[oc % 2]
                        E_.wait("tensor", "s", pm_s[oc % 2])
                        E_.wait("tensor", "da", w2_d[buf])
                        E_.wait("tensor", "s", evac_last)
                        for fc in range(FC):
                            tm = E_.mm(P[:, 0:TT],
                                       w2sl[buf][:, fc * 128:(fc + 1) * 128],
                                       hbuf[:, fc * TT:(fc + 1) * TT],
                                       start=(fc == 0), stop=(fc == FC - 1))
                        w2_t[buf] = tm
                        E_.wait("scalar", "t", tm)
                        se = E_.s("activation", R.y[:, oc * TT:(oc + 1) * TT],
                                  P[:, 0:TT], AF.Identity, bias=b2t[:, oc:oc + 1],
                                  scale=1.0)
                        pm_s[oc % 2] = se
                    h_read_t = tm
                    vl, gd = R.finish_tile(E_, xout, T, tt, se, R.PU4, R.PU5,
                                           0, ln_v)
                    ln_v = vl
                    ln_t = E_.c["t"]
                fin = E_.c["ga"]
                E_.wait("gpsimd", "ga", fin)
            run_phase(nc, sems, cnt, main)



def build_attn(nc, sems, cnt, inp, xin, xout, T, layer):
    NTT = T // TT
    NTC = T // 128
    if layer == 0:
        Wq = inp("l0a_wq", [D, D], F32)
        Wk = inp("l0a_wk", [D, D], F32)
        Wv = inp("l0a_wv", [D, D], F32)
        Wo = inp("l0a_wo", [D, D], F32)
        bq = inp("l0a_bq", [D], F32)
        bk = inp("l0a_bk", [D], F32)
        bv = inp("l0a_bv", [D], F32)
        bo = inp("l0a_bo", [D], F32)
    else:
        WqkvT = inp("l1a_wqkvt", [D, 3 * D], F32)
        WprojT = inp("l1a_wprojt", [D, D], F32)
        pmask = inp("pmask", [128, 896], BF16)
    emask2 = inp("emask2", [128, 896], BF16)
    osel16 = inp("osel16", [16, NP * 128], BF16)
    if layer == 1 and DEBUG_L1:
        dbg = nc.dram_tensor("dbg", [4, TT], F32, kind="ExternalOutput").ap()
        dbg2 = nc.dram_tensor("dbg2", [64, TT], BF16, kind="ExternalOutput").ap()
    else:
        dbg = None
    qscr = nc.dram_tensor(f"qscr{layer}", [NP * 128, T], BF16).ap()
    wscr = nc.dram_tensor(f"wscr{layer}", [3 * NP * DC * 128, 128], BF16).ap()
    WS_P = DC * 128 * 128

    def wscr_lhs_ap(which, p):
        off = (which * NP + p) * WS_P
        return bass.AP(wscr.tensor, wscr.offset + off,
                       [[128, 128], [128 * 128, DC], [1, 128]])

    def wscr_v_ap(half, dc):
        off = 2 * NP * WS_P + (half * 4) * WS_P + dc * 128 * 128
        return bass.AP(wscr.tensor, wscr.offset + off,
                       [[128, 128], [WS_P, 4], [1, 128]])

    with contextlib.ExitStack() as ctx:
        sb = lambda n, s, d: ctx.enter_context(nc.sbuf_tensor(f"a{layer}" + n, s, d))
        KT = sb("_kt", [128, NP * T], BF16)
        V = sb("_v", [128, NTC * H * 65], BF16)
        R = Res(nc, ctx, f"a{layer}")
        if layer == 1:
            mscr = sb("_mscr", [128, NTC * 16], F32)

        def vsl(tc, h, w=65):
            return V[:, tc * H * 65 + h * 65: tc * H * 65 + h * 65 + w]

        copy_x_phase(nc, sems, cnt, xin, xout, T)

        with contextlib.ExitStack() as cw:
            w_ = lambda n, s, d: cw.enter_context(nc.sbuf_tensor(f"a{layer}w" + n, s, d))
            wl = w_("_wl", [128, 1024], F32)
            wb = w_("_wb", [128, 1024], BF16)

            def wprep(E_):
                R.init_consts(E_)
                prev_v = 0
                prev_d = 0
                for dc in range(DC):
                    for which in range(3):
                        if layer == 0:
                            W = (Wq, Wk, Wv)[which]
                            src = chunk_ap(W, dc, D)
                        else:
                            src = bass.AP(WqkvT.tensor,
                                          WqkvT.offset + dc * 128 * 3 * D + which * D,
                                          [[3 * D, 128], [1, D]])
                        E_.wait("sync", "v", prev_v)
                        dd = E_.dma("da", wl[:], src)
                        E_.wait("vector", "da", dd)
                        E_.wait("vector", "da", prev_d)
                        vq = E_.v("tensor_copy", wb[:], wl[:])
                        E_.wait("sync", "v", vq)
                        dst = bass.AP(wscr.tensor,
                                      wscr.offset + (which * NP) * WS_P + dc * 128 * 128,
                                      [[128, 128], [WS_P, NP], [1, 128]])
                        prev_d = E_.dma("da", dst, wb[:].rearrange(
                            "q (p c) -> q p c", c=128))
                        prev_v = vq
                E_.wait("sync", "da", prev_d)
            run_phase(nc, sems, cnt, wprep)

        with contextlib.ExitStack() as c2:
            p_ = lambda n, s, d: c2.enter_context(nc.sbuf_tensor(f"a{layer}p" + n, s, d))
            xts = [p_("_xt0", [128, DC * TT], F32),
                   p_("_xt1", [128, DC * TT], F32)]
            xhat = p_("_xhat", [128, DC * TT], BF16)
            wkq = [p_("_wkq0", [128, DC * 128], BF16),
                   p_("_wkq1", [128, DC * 128], BF16)]
            wv2 = [p_("_wv0", [128, 512], BF16), p_("_wv1", [128, 512], BF16)]
            qtmp = p_("_qtmp", [128, TT], BF16)
            bt = p_("_bt", [128, 2 * DC], F32)

            def proj(E_):
                if layer == 0:
                    for c in range(DC):
                        E_.dma("da", bt[:, c:c + 1], bias_ap(bq, c))
                        dd = E_.dma("da", bt[:, DC + c:DC + c + 1], bias_ap(bk, c))
                    E_.wait("scalar", "da", dd)
                PM = (R.PM0, R.PM1)
                bank_s = [0, 0, 0, 0]
                pm_s = bank_s
                w_t = [0, 0]
                w_d = [0, 0]
                q_aq = 0
                ln_t = 0
                ln_v = 0
                xg = [0, 0]
                xd = [0, 0]
                def x_load(tt_):
                    b_ = tt_ % 2
                    E_.wait("sync", "v", xg[b_])
                    xd[b_] = E_.dma("dd", xts[b_][:], x_tile_ap(xin, T, tt_))
                for tt in range(NTT):
                    xb_ = tt % 2
                    xt = xts[xb_]
                    if tt == 0:
                        x_load(0)
                    if tt + 1 < NTT:
                        x_load(tt + 1)
                    for eng in ("tensor", "vector", "scalar"):
                        E_.wait(eng, "dd", xd[xb_])
                    tb = R.ln_rows(E_, xt, R.PU4, R.PU5, ln_t, ln_v)
                    ln_v = R.ln_apply(E_, xt, xhat, R.PU4, R.PU5, tb)
                    xg[xb_] = ln_v
                    E_.wait("tensor", "v", ln_v)
                    for wi, which in enumerate((1, 0)):
                        for p in range(NP):
                            buf = p % 2
                            E_.wait("sync", "t", w_t[buf])
                            w_d[buf] = E_.dma("da", wkq[buf][:], wscr_lhs_ap(which, p))
                            P = PM[p % 2]
                            E_.wait("tensor", "s", pm_s[p % 2])
                            E_.wait("tensor", "da", w_d[buf])
                            for dc in range(DC):
                                tm = E_.mm(P[:, 0:TT],
                                           wkq[buf][:, dc * 128:(dc + 1) * 128],
                                           xhat[:, dc * TT:(dc + 1) * TT],
                                           start=(dc == 0), stop=(dc == DC - 1))
                            w_t[buf] = tm
                            E_.wait("scalar", "t", tm)
                            if which == 1:
                                dst = KT[:, p * T + tt * TT: p * T + (tt + 1) * TT]
                                if layer == 0:
                                    se = E_.s("activation", dst, P[:, 0:TT],
                                              AF.Identity,
                                              bias=bt[:, DC + p:DC + p + 1], scale=1.0)
                                else:
                                    se = E_.s("copy", dst, P[:, 0:TT])
                            else:
                                E_.wait("scalar", "aq", q_aq)
                                if layer == 0:
                                    se = E_.s("activation", qtmp[:], P[:, 0:TT],
                                              AF.Identity, bias=bt[:, p:p + 1],
                                              scale=1.0)
                                else:
                                    se = E_.s("copy", qtmp[:], P[:, 0:TT])
                                qap = bass.AP(qscr.tensor,
                                              qscr.offset + p * 128 * T + tt * TT,
                                              [[T, 128], [1, TT]])
                                E_.wait("scalar", "s", se)
                                q_aq = E_.dma("aq", qap, qtmp[:])
                            pm_s[p % 2] = se
                    P4 = (R.PM0, R.PM1, R.PSa, R.PSb)
                    vm = E_.v("memset",
                              V[:, tt * 4 * H * 65:(tt + 1) * 4 * H * 65], 1.0)
                    for half in range(2):
                        for t4 in range(4):
                            E_.wait("tensor", "s", bank_s[t4])
                        E_.wait("tensor", "v", vm)
                        for dc in range(DC):
                            buf = dc % 2
                            lane = "db" if buf == 0 else "dc"
                            E_.wait("sync", "t", w_t[buf])
                            w_d[buf] = E_.dma(lane, wv2[buf][:], wscr_v_ap(half, dc))
                            E_.wait("tensor", lane, w_d[buf])
                            for t4 in range(4):
                                tm = E_.mm(P4[t4][:, 0:512],
                                           xhat[:, dc * TT + t4 * 128:
                                                dc * TT + t4 * 128 + 128],
                                           wv2[buf][:],
                                           start=(dc == 0), stop=(dc == DC - 1))
                            w_t[buf] = tm
                        E_.wait("scalar", "t", tm)
                        for t4 in range(4):
                            tc = tt * 4 + t4
                            dst = V[:, tc * H * 65 + half * 8 * 65:
                                    tc * H * 65 + (half + 1) * 8 * 65]
                            dst = dst.rearrange("p (h d) -> p h d", d=65)[:, :, 0:64]
                            se = E_.s("copy", dst, P4[t4][:, 0:512])
                            bank_s[t4] = se
                    E_.wait("tensor", "s", se)
                    ln_t = E_.c["t"]
            run_phase(nc, sems, cnt, proj)

        if layer == 1:
            with contextlib.ExitStack() as c4:
                p_ = lambda n, s, d: c4.enter_context(
                    nc.sbuf_tensor(f"a{layer}m" + n, s, d))
                pmaskT = p_("_pm", [128, 896], BF16)
                qt1 = [p_("_qt1a", [128, TT], BF16),
                       p_("_qt1b", [128, TT], BF16)]
                prt = p_("_p1", [128, 16], F32)

                def pass1(E_):
                    dd = E_.dma("da", pmaskT[:], pmask)
                    E_.wait("vector", "da", dd)
                    PS4 = (R.PSa[:, 0:512], R.PSa[:, 512:1024],
                           R.PSb[:, 0:512], R.PSb[:, 512:1024])
                    ps_v = [0, 0, 0, 0]
                    q_t = 0
                    fin_v = [0, 0]
                    th_v = 0
                    qd1 = [0, 0]
                    quse1 = [0, 0]

                    def q1_load(i):
                        p_2, qt_ = divmod(i, NTT)
                        buf = i % 2
                        lane = "db" if buf == 0 else "dc"
                        qap = bass.AP(qscr.tensor,
                                      qscr.offset + p_2 * 128 * T + qt_ * TT,
                                      [[T, 128], [1, TT]])
                        E_.wait("sync", "t", quse1[buf])
                        qd1[buf] = E_.dma(lane, qt1[buf][:], qap)

                    for p in range(NP):
                        for qt in range(NTT):
                            i = p * NTT + qt
                            if i == 0:
                                q1_load(0)
                            qtile1 = qt1[i % 2]
                            qlane = "db" if i % 2 == 0 else "dc"
                            E_.wait("tensor", qlane, qd1[i % 2])
                            if i + 1 < NP * NTT:
                                q1_load(i + 1)
                            for qs in range(4):
                                qsg = qt * 4 + qs
                                vr2 = [0, 0]
                                for kt in range(qt + 1):
                                    bi = (2 * kt) % 4
                                    PA, PB = PS4[bi], PS4[bi + 1]
                                    E_.wait("tensor", "v", ps_v[bi])
                                    E_.wait("tensor", "v", ps_v[bi + 1])
                                    ta = E_.mm(PA, qtile1[0:64, qs * 128:(qs + 1) * 128],
                                               KT[0:64, p * T + kt * TT:
                                                  p * T + (kt + 1) * TT],
                                               start=True, stop=True,
                                               tile_position=(0, 0))
                                    tb_ = E_.mm(PB,
                                                qtile1[64:128, qs * 128:(qs + 1) * 128],
                                                KT[64:128, p * T + kt * TT:
                                                   p * T + (kt + 1) * TT],
                                                start=True, stop=True,
                                                tile_position=(64, 0))
                                    q_t = tb_
                                    for hh, (P, td) in enumerate(((PA, ta), (PB, tb_))):
                                        E_.wait("vector", "t", td)
                                        E_.wait("vector", "v", fin_v[hh])
                                        if kt == qt:
                                            E_.wait("vector", "v", th_v)
                                            pms = bass.AP(pmaskT[:].tensor,
                                                          384 - 128 * qs,
                                                          [[896, 128], [1, TT]])
                                            vmk = E_.v("tensor_tensor", R.th[:], P,
                                                       pms, OP.min)
                                            E_.wait("vector", "v", vmk)
                                            vr = E_.v("tensor_reduce",
                                                      prt[:, hh * 8 + kt:
                                                          hh * 8 + kt + 1],
                                                      R.th[:], AX.X, OP.max)
                                            th_v = vr
                                        else:
                                            vr = E_.v("tensor_reduce",
                                                      prt[:, hh * 8 + kt:
                                                          hh * 8 + kt + 1],
                                                      P, AX.X, OP.max)
                                        vr2[hh] = vr
                                        ps_v[bi + hh] = vr
                                for hh in range(2):
                                    col = qsg * 16 + 2 * p + hh
                                    E_.wait("vector", "v", vr2[hh])
                                    vf = E_.v("tensor_reduce",
                                              mscr[:, col:col + 1],
                                              prt[:, hh * 8:hh * 8 + qt + 1],
                                              AX.X, OP.max)
                                    fin_v[hh] = vf
                            quse1[i % 2] = q_t
                run_phase(nc, sems, cnt, pass1)

        with contextlib.ExitStack() as c3:
            p_ = lambda n, s, d: c3.enter_context(nc.sbuf_tensor(f"a{layer}t" + n, s, d))
            WO = p_("_wo", [128, DC * D], BF16)
            stag = p_("_stag", [128, NP * TT], BF16)
            qtl = [p_("_qt0", [128, TT], BF16), p_("_qt1", [128, TT], BF16)]
            EAB = [p_("_e0", [128, 1024], BF16), p_("_e1", [128, 1024], BF16),
                   p_("_e2", [128, 1024], BF16)]
            srowA = R.y[0:1, 0:TT]
            srowB = R.y[0:1, TT:2 * TT]
            Scol = p_("_scl", [16, TT], F32)
            chT = R.y[0:16, 2 * TT:3 * TT]
            c1m = p_("_c1m", [16, TT], BF16)
            osel = p_("_osl", [16, NP * 128], BF16)
            emaskT = p_("_em", [128, 896], BF16)
            if layer == 0:
                boP = p_("_bop", [128, DC], F32)
                bvb = p_("_bvb", [128, DC], BF16)
            else:
                Mcol = p_("_mcl", [16, TT], F32)
                c2m = p_("_c2m", [16, TT], BF16)
                wsel = p_("_wsl", [16, NP * 128], BF16)
                wcolp = p_("_wcp", [1, H * 64], BF16)

            with contextlib.ExitStack() as cw2:
                wl2 = R.y[:, 0:D // 2]

                def wo_prep(E_):
                    src = Wo if layer == 0 else WprojT
                    prev = 0
                    for c2 in range(2 * DC):
                        c, hf = c2 // 2, c2 % 2
                        E_.wait("sync", "v", prev)
                        dd = E_.dma("da", wl2, bass.AP(
                            src.tensor, src.offset + c * 128 * D + hf * (D // 2),
                            [[D, 128], [1, D // 2]]))
                        E_.wait("vector", "da", dd)
                        prev = E_.v("tensor_copy",
                                    WO[:, c * D + hf * (D // 2):
                                       c * D + (hf + 1) * (D // 2)], wl2)
                        E_.wait("vector", "v", prev)
                    E_.dma("da", osel[:], osel16)
                    dd = E_.dma("da", emaskT[:], emask2)
                    for eng in ("tensor", "vector", "scalar"):
                        E_.wait(eng, "da", dd)
                    if layer == 0:
                        vq = prev
                        for c in range(DC):
                            E_.wait("sync", "v", vq)
                            dd = E_.dma("da", R.y[:, 0:1], bias_ap(bv, c))
                            E_.wait("vector", "da", dd)
                            vq = E_.v("tensor_copy", bvb[:, c:c + 1], R.y[:, 0:1])
                            E_.wait("vector", "v", vq)
                            E_.dma("da", boP[:, c:c + 1], bias_ap(bo, c))
                        dd = E_.c["da"]
                        E_.wait("tensor", "da", dd)
                        E_.wait("tensor", "v", vq)
                        for oc in range(DC):
                            for dc in range(DC):
                                tm = E_.mm(R.PM0[:, oc:oc + 1],
                                           WO[:, dc * D + oc * 128:
                                              dc * D + oc * 128 + 128],
                                           bvb[:, dc:dc + 1],
                                           start=(dc == 0), stop=(dc == DC - 1))
                        E_.wait("vector", "t", tm)
                        E_.wait("vector", "da", dd)
                        vb2 = E_.v("tensor_tensor", boP[:, 0:DC], boP[:, 0:DC],
                                   R.PM0[:, 0:DC], OP.add)
                        E_.wait("tensor", "v", vb2)
                        E_.wait("scalar", "v", vb2)
                    else:
                        vz = E_.v("memset", wcolp[:], 0.0)
                        pb_v = vz
                        for h in range(H):
                            for tc in range(NTC):
                                E_.wait("tensor", "v", pb_v)
                                tm = E_.mm(R.PM0[0:1, 0:64], R.onesc[:],
                                           vsl(tc, h, 64),
                                           start=(tc == 0), stop=(tc == NTC - 1))
                            E_.wait("vector", "t", tm)
                            pb_v = E_.v("tensor_copy",
                                        wcolp[:, h * 64:(h + 1) * 64],
                                        R.PM0[0:1, 0:64])
                            E_.wait("vector", "v", pb_v)
                        E_.wait("tensor", "v", pb_v)
                        vws = E_.v("memset", wsel[:], 0.0)
                        E_.wait("scalar", "v", vws)
                        E_.wait("scalar", "v", pb_v)
                        for h in range(H):
                            c0 = (h // 2) * 128 + 64 * (h % 2)
                            E_.dma("aq", wsel[h:h + 1, c0:c0 + 64],
                                   wcolp[:, h * 64:(h + 1) * 64])
                run_phase(nc, sems, cnt, wo_prep)

            def attn(E_):
                PM = (R.PM0, R.PM1)
                PSS = (R.PSa, R.PSb)
                pm_s = [0, 0]
                pm_v = [0, 0]
                pu_s = 0
                pu_vv = 0
                q_t = 0
                fin_g = 0
                esPM = 0
                proj_t = 0
                scol_gate = 0
                cht_gate = 0
                mcol_gate = 0
                scol_d = 0
                mcol_d = 0
                c1_rdy = 0
                c1_rdy_s = 0
                c2_rdy = 0
                qd = [0, 0]
                quse = [0, 0]

                def q_load(i):
                    qt_, p_2 = divmod(i, NP)
                    buf = i % 2
                    lane = "db" if buf == 0 else "dc"
                    qap = bass.AP(qscr.tensor,
                                  qscr.offset + p_2 * 128 * T + qt_ * TT,
                                  [[T, 128], [1, TT]])
                    E_.wait("sync", "t", quse[buf])
                    qd[buf] = E_.dma(lane, qtl[buf][:], qap)

                PVb = ((R.PU4, R.PU5), (R.PU4, R.PU5))
                pu_free_s = [0, 0]
                pu_free_v = [0, 0]
                for qt in range(NTT):
                    pu_free_s[1] = max(pu_free_s[1], pm_s[0], pm_s[1])
                    pu_free_v[1] = max(pu_free_v[1], pm_v[0], pm_v[1])
                    if layer == 1:
                        E_.wait("sync", "s", scol_gate)
                        E_.wait("sync", "v", mcol_gate)
                        first = True
                        for h in range(H):
                            for qs in range(4):
                                col = (qt * 4 + qs) * 16 + h
                                mcol_d = E_.dma(
                                    "da",
                                    Mcol[h:h + 1, qs * 128:(qs + 1) * 128],
                                    mscr[:, col:col + 1], nowait=not first)
                                first = False
                    for p in range(NP):
                        par = 0
                        bkA, bkB = PVb[par]
                        qi = qt * NP + p
                        if qi == 0:
                            q_load(0)
                        qtile = qtl[qi % 2]
                        qlane = "db" if qi % 2 == 0 else "dc"
                        E_.wait("tensor", qlane, qd[qi % 2])
                        if qi + 1 < NTT * NP:
                            q_load(qi + 1)
                        nkc = (qt + 1) * 4
                        es = [0, 0, 0]
                        pv_t = [0, 0, 0]
                        sc_t = [0, 0, 0]

                        def s_mms(kc):
                            sl = kc % 3
                            E_.wait("tensor", "s", es[sl])
                            if sl < 2:
                                PSSc = PSS[sl]
                                d0 = PSSc[:, 0:TT]
                                d1 = PSSc[:, 512:1024]
                            else:
                                E_.wait("tensor", "v", max(pm_v))
                                E_.wait("tensor", "s", max(pm_s))
                                d0 = R.PM0[:, 0:TT]
                                d1 = R.PM1[:, 0:TT]
                            E_.mm(d0,
                                  KT[0:64, p * T + kc * 128:
                                     p * T + (kc + 1) * 128],
                                  qtile[0:64, :], start=True, stop=True,
                                  tile_position=(0, 0))
                            tb_ = E_.mm(d1,
                                        KT[64:128, p * T + kc * 128:
                                           p * T + (kc + 1) * 128],
                                        qtile[64:128, :], start=True, stop=True,
                                        tile_position=(64, 0))
                            sc_t[sl] = tb_
                            return tb_

                        q_t = s_mms(0)
                        if nkc > 1:
                            q_t = s_mms(1)
                        for kc in range(nkc):
                            sl = kc % 3
                            EB = EAB[sl]
                            E_.wait("scalar", "t", sc_t[sl])
                            E_.wait("scalar", "t", pv_t[sl])
                            if sl < 2:
                                se = E_.s("activation", EB[:], PSS[sl][:],
                                          AF.Exp, scale=0.125)
                            else:
                                se = E_.s("activation", EB[:], R.PMd[:],
                                          AF.Exp, scale=0.125)
                                esPM = se
                            es[sl] = se
                            diag = (kc >= qt * 4)
                            if diag:
                                vv = kc - qt * 4
                                E_.wait("vector", "s", se)
                                mrep = bass.AP(emaskT[:].tensor, 384 - 128 * vv,
                                               [[896, 128], [0, 2], [1, 512]])
                                ve = E_.v("tensor_tensor",
                                          EB[:].rearrange("p (a b) -> p a b", b=512),
                                          EB[:].rearrange("p (a b) -> p a b", b=512),
                                          mrep, OP.mult)
                            if kc + 2 < nkc:
                                q_t = s_mms(kc + 2)
                            if diag:
                                E_.wait("tensor", "v", ve)
                            else:
                                E_.wait("tensor", "s", se)
                            if kc == 0:
                                E_.wait("tensor", "s", pu_free_s[par])
                                E_.wait("tensor", "v", pu_free_v[par])
                            islast = (kc == nkc - 1)
                            t1 = E_.mm(bkA[0:65, 0:TT], vsl(kc, 2 * p),
                                       EB[:, 0:512], start=(kc == 0), stop=islast)
                            t2 = E_.mm(bkB[0:65, 0:TT], vsl(kc, 2 * p + 1),
                                       EB[:, 512:1024], start=(kc == 0), stop=islast)
                            pv_t[kc % 3] = t2
                        quse[qi % 2] = q_t
                        E_.wait("vector", "t", t2)
                        E_.wait("vector", "ga", fin_g)
                        E_.wait("vector", "aq", scol_d)
                        va = E_.v("tensor_copy", srowA, bkA[64:65, 0:TT])
                        E_.wait("vector", "v", va)
                        vb_ = E_.v("tensor_copy", srowB, bkB[64:65, 0:TT])
                        E_.wait("scalar", "v", va)
                        E_.wait("scalar", "t", t2)
                        E_.wait("scalar", "t", proj_t)
                        pu_free_s[par] = E_.s("copy",
                                              stag[0:64, p * TT:(p + 1) * TT],
                                              bkA[0:64, 0:TT])
                        E_.wait("vector", "v", vb_)
                        E_.wait("vector", "t", proj_t)
                        pu_free_v[par] = E_.v("tensor_copy",
                                              stag[64:128, p * TT:(p + 1) * TT],
                                              bkB[0:64, 0:TT])
                        if p == 0:
                            E_.wait("scalar", "s", scol_gate)
                        E_.wait("scalar", "v", vb_)
                        E_.dma("aq", Scol[2 * p:2 * p + 1, :], srowA)
                        scol_d = E_.dma("aq", Scol[2 * p + 1:2 * p + 2, :],
                                        srowB, nowait=True)
                    if layer == 0:
                        E_.wait("scalar", "aq", scol_d)
                        E_.wait("scalar", "v", cht_gate)
                        E_.wait("scalar", "v", pm_v[0])
                        E_.wait("scalar", "ga", fin_g)
                        cs = E_.s("activation", chT, Scol[:], AF.Ln)
                        E_.wait("scalar", "s", cs)
                        cs = E_.s("activation", c1m[:], chT, AF.Exp,
                                  scale=-1.0)
                        scol_gate = cs
                        c1_rdy_s = cs
                    else:
                        E_.wait("scalar", "da", mcol_d)
                        E_.wait("scalar", "v", cht_gate)
                        E_.wait("scalar", "v", pm_v[0])
                        E_.wait("scalar", "ga", fin_g)
                        cs = E_.s("activation", chT, Mcol[:], AF.Exp,
                                  scale=0.125)
                        E_.wait("vector", "s", cs)
                        cv = E_.v("tensor_scalar", chT, chT, 1e-6,
                                  None, OP.mult, OP.bypass)
                        E_.wait("vector", "aq", scol_d)
                        E_.wait("vector", "v", cv)
                        cv = E_.v("tensor_tensor", chT, Scol[:], chT,
                                  OP.add)
                        E_.wait("scalar", "v", cv)
                        cs = E_.s("activation", Mcol[:], chT, AF.Ln)
                        E_.wait("scalar", "s", cs)
                        cs = E_.s("activation", chT, Mcol[:], AF.Exp,
                                  scale=-1.0)
                        E_.wait("vector", "s", cs)
                        cv = E_.v("tensor_tensor", Mcol[:], Scol[:], chT,
                                  OP.mult)
                        E_.wait("vector", "v", cv)
                        cv = E_.v("tensor_scalar", Mcol[:], Mcol[:], 1.0,
                                  float(T) * 1e-8, OP.mult, OP.add)
                        E_.wait("scalar", "v", cv)
                        cs = E_.s("activation", Scol[:], Mcol[:], AF.Ln)
                        E_.wait("scalar", "s", cs)
                        cs = E_.s("activation", Mcol[:], Scol[:], AF.Exp,
                                  scale=-1.0)
                        scol_gate = cs
                        E_.wait("vector", "s", cs)
                        cv = E_.v("tensor_tensor", c1m[:], chT, Mcol[:],
                                  OP.mult)
                        cht_gate = cv
                        c1_rdy = cv
                        E_.wait("vector", "v", cv)
                        cv = E_.v("tensor_scalar", c2m[:], Mcol[:], 1e-8,
                                  None, OP.mult, OP.bypass)
                        mcol_gate = cv
                        c2_rdy = cv
                    bank_v = [pm_v[0], pm_v[1]]
                    ap_v = 0
                    for p in range(NP):
                        stg = stag[:, p * TT:(p + 1) * TT]
                        E_.wait("tensor", "v", c1_rdy)
                        E_.wait("tensor", "s", c1_rdy_s)
                        E_.wait("tensor", "s", max(pu_free_s))
                        E_.wait("tensor", "v", max(pu_free_v))
                        E_.wait("tensor", "s", esPM)
                        E_.wait("tensor", "v", bank_v[0])
                        E_.wait("tensor", "s", pm_s[0])
                        tb0 = E_.mm(PM[0][:, 0:TT],
                                    osel[:, p * 128:(p + 1) * 128],
                                    c1m[:], start=True, stop=True)
                        if layer == 1:
                            E_.wait("tensor", "v", c2_rdy)
                            E_.wait("tensor", "v", bank_v[1])
                            E_.wait("tensor", "s", pm_s[1])
                            tb1 = E_.mm(PM[1][:, 0:TT],
                                        wsel[:, p * 128:(p + 1) * 128],
                                        c2m[:], start=True, stop=True)
                        E_.wait("vector", "t", tb0)
                        E_.wait("vector", "s", esPM)
                        E_.wait("vector", "s", max(pu_free_s))
                        E_.wait("vector", "v", max(pu_free_v))
                        E_.wait("vector", "v", ap_v)
                        if layer == 0:
                            vap = E_.v("tensor_tensor", stg, stg,
                                       PM[0][:, 0:TT], OP.mult)
                            bank_v[0] = vap
                        else:
                            vth = E_.v("tensor_tensor", R.th[:], stg,
                                       PM[0][:, 0:TT], OP.mult)
                            bank_v[0] = vth
                            E_.wait("vector", "t", tb1)
                            E_.wait("vector", "v", vth)
                            vap = E_.v("tensor_tensor", stg, R.th[:],
                                       PM[1][:, 0:TT], OP.add)
                            bank_v[1] = vap
                        ap_v = vap
                    pm_v = [ap_v, ap_v]
                    se = 0
                    E_.wait("scalar", "ga", fin_g)
                    E_.wait("scalar", "aq", scol_d)
                    E_.wait("scalar", "s", scol_gate)
                    E_.wait("scalar", "v", cht_gate)
                    for oc in range(DC):
                        P = PM[oc % 2]
                        E_.wait("tensor", "s", pm_s[oc % 2])
                        E_.wait("tensor", "v", ap_v)
                        for dc in range(DC):
                            tm = E_.mm(P[:, 0:TT],
                                       WO[:, dc * D + oc * 128:
                                          dc * D + oc * 128 + 128],
                                       stag[:, dc * TT:(dc + 1) * TT],
                                       start=(dc == 0), stop=(dc == DC - 1))
                        E_.wait("scalar", "t", tm)
                        if layer == 0:
                            se = E_.s("activation", R.y[:, oc * TT:(oc + 1) * TT],
                                      P[:, 0:TT], AF.Identity,
                                      bias=boP[:, oc:oc + 1], scale=1.0)
                        else:
                            se = E_.s("copy", R.y[:, oc * TT:(oc + 1) * TT],
                                      P[:, 0:TT])
                        pm_s[oc % 2] = se
                    proj_t = tm
                    vl, gd = R.finish_tile(E_, xout, T, qt, se, PM[0], PM[1],
                                           0, max(pm_v))
                    pm_v = [vl, vl]
                    fin_g = gd
                fin = E_.c["ga"]
                E_.wait("gpsimd", "ga", fin)
            run_phase(nc, sems, cnt, attn)



def build_l1ffn(nc, sems, cnt, inp, xin, xout, T):
    NTT = T // TT
    k0 = inp("l1f_k0", [2 * D, F1], U8)
    k1 = inp("l1f_k1", [F1, F1], U8)
    k2 = inp("l1f_k2", [F1, D], U8)
    s0 = inp("l1f_s0", [1, 1], F32)
    s1 = inp("l1f_s1", [1, 1], F32)
    s2 = inp("l1f_s2", [1, 1], F32)
    k1scr = nc.dram_tensor("k1scr", [FC * 128, FC * 128], BF16).ap()
    k2scr2 = nc.dram_tensor("k2scr2", [F1, D], BF16).ap()

    with contextlib.ExitStack() as ctx:
        sb = lambda n, s, d: ctx.enter_context(nc.sbuf_tensor("f1" + n, s, d))
        R = Res(nc, ctx, "f1")
        k0e = sb("_k0e", [128, DC * F1], BF16)
        h1 = sb("_h1", [128, FC * TT], BF16)
        g2 = sb("_g2", [128, FC * TT], BF16)
        k1s = [sb("_k1a", [128, FC * 128], BF16), sb("_k1b", [128, FC * 128], BF16)]
        k2s = [sb("_k2a", [128, D], BF16), sb("_k2b", [128, D], BF16)]
        ka, kb = k1s[0], k1s[1]
        xt = sb("_xt", [128, DC * TT], F32)
        xhat = sb("_xhat", [128, DC * TT], BF16)
        scl = sb("_scl", [128, 3], F32)
        rrow = sb("_rrow", [1, TT], F32)

        copy_x_phase(nc, sems, cnt, xin, xout, T)

        def prep(E_):
            ic = R.init_consts(E_)
            prev = 0
            for dc in range(DC):
                E_.wait("gpsimd", "v", prev)
                d1 = E_.dma("ga", ka[:], chunk_ap(k0, dc, F1))
                d2 = E_.dma("ga", kb[:], chunk_ap(k0, dc + DC, F1))
                E_.wait("vector", "ga", d2)
                prev = E_.v("tensor_tensor", k0e[:, dc * F1:(dc + 1) * F1],
                            ka[:], kb[:], OP.subtract)
                E_.wait("vector", "v", prev)
            kd = [0, 0]
            kv = [prev, prev]
            for fc1 in range(FC):
                buf = fc1 % 2
                kb_ = (ka, kb)[buf]
                lane = "gb" if buf == 0 else "gc"
                E_.wait("gpsimd", "dd", kd[buf])
                E_.wait("gpsimd", "v", kv[buf])
                dl = E_.dma(lane, kb_[:], chunk_ap(k1, fc1, F1))
                E_.wait("sync", lane, dl)
                dst = bass.AP(k1scr.tensor, k1scr.offset + fc1 * 128,
                              [[FC * 128, 128], [FC * 16384, FC], [1, 128]])
                kd[buf] = E_.dma("dd", dst,
                                 kb_[:].rearrange("r (a c) -> r a c", c=128))
            k2ld = [0, 0]
            k2st = [0, 0]
            for fc2 in range(FC):
                buf = fc2 % 2
                lane = "gb" if buf == 0 else "gc"
                E_.wait("gpsimd", "dd", k2st[buf])
                dl = E_.dma(lane, k2s[buf][:], chunk_ap(k2, fc2, D))
                E_.wait("sync", lane, dl)
                k2st[buf] = E_.dma("dd", chunk_ap(k2scr2, fc2, D), k2s[buf][:])
            E_.wait("sync", "dd", E_.c["dd"])
            eps_guard_v = ic
            eps_guard_t = 0
            for i, (sc, mult) in enumerate(((s0, 1.0), (s1, 4.0), (s2, 4.0))):
                E_.wait("sync", "v", eps_guard_v)
                E_.wait("sync", "t", eps_guard_t)
                dd = E_.dma("da", R.eps11[:], sc)
                E_.wait("tensor", "da", dd)
                tm = E_.mm(R.PM0[:, 0:1], R.ones1[0:1, :], R.eps11[:],
                           start=True, stop=True)
                E_.wait("vector", "t", tm)
                vv = E_.v("tensor_scalar", scl[:, i:i + 1], R.PM0[:, 0:1],
                          mult, None, OP.mult, OP.bypass)
                E_.wait("tensor", "v", vv)
                eps_guard_t = tm
            E_.wait("vector", "t", tm)
            vv = E_.v("memset", R.eps11[:], 1e-5)
            E_.wait("scalar", "v", vv)
        run_phase(nc, sems, cnt, prep)

        def main(E_):
            P4 = (R.PM0[:, 0:512], R.PM1[:, 0:512],
                  R.PSa[:, 0:512], R.PSb[:, 0:512])
            P8 = (R.PM0[:, 0:512], R.PM1[:, 0:512],
                  R.PSa[:, 0:512], R.PSb[:, 0:512],
                  R.PSa[:, 512:1024], R.PSb[:, 512:1024],
                  R.PU4[:, 0:512], R.PU5[:, 0:512])
            pm_v = [0, 0, 0, 0, 0, 0, 0, 0]
            h1_read_t = 0
            g2_read_t = 0
            k1_t = [0, 0]
            k1_d = [0, 0]
            k2_t = [0, 0]
            k2_d = [0, 0]
            ln_t = 0
            ln_v = 0
            for tt in range(NTT):
                E_.wait("sync", "v", ln_v)
                dd = E_.dma("da", xt[:], x_tile_ap(xin, T, tt))
                for eng in ("tensor", "vector", "scalar"):
                    E_.wait(eng, "da", dd)
                tb = R.ln_rows(E_, xt, R.PU4, R.PU5, ln_t, ln_v)
                ln_v = R.ln_apply(E_, xt, xhat, R.PU4, R.PU5, tb)
                E_.wait("tensor", "v", ln_v)
                for fc in range(FC):
                    bi = fc % 2
                    P = P4[bi]
                    E_.wait("tensor", "v", pm_v[bi])
                    E_.wait("tensor", "t", h1_read_t)
                    for dc in range(DC):
                        lhs = k0e[:, dc * F1 + fc * 128: dc * F1 + fc * 128 + 128]
                        tm = E_.mm(P, lhs, xhat[:, dc * TT:(dc + 1) * TT],
                                   start=(dc == 0), stop=(dc == DC - 1))
                    E_.wait("vector", "t", tm)
                    ve = E_.v("tensor_scalar", h1[:, fc * TT:(fc + 1) * TT],
                              P, scl[:, 0:1], 0.0, OP.mult, OP.max)
                    pm_v[bi] = ve
                    E_.wait("vector", "v", ve)
                h1_v = ve
                E_.wait("tensor", "v", h1_v)
                for fc in range(FC):
                    tm = E_.mm(R.PU4[0:1, 0:TT], R.onesc[:],
                               h1[:, fc * TT:(fc + 1) * TT],
                               start=(fc == 0), stop=(fc == FC - 1))
                E_.wait("vector", "t", tm)
                vr = E_.v("tensor_scalar", rrow[:], R.PU4[0:1, 0:TT], -0.5, None,
                          OP.mult, OP.bypass)
                E_.wait("tensor", "v", vr)
                rr_t = tm
                for fc2 in range(FC):
                    buf = fc2 % 2
                    lane = "db" if buf == 0 else "dc"
                    E_.wait("sync", "t", k1_t[buf])
                    k1ap = bass.AP(k1scr.tensor,
                                   k1scr.offset + fc2 * FC * 16384,
                                   [[FC * 128, 128], [1, FC * 128]])
                    k1_d[buf] = E_.dma(lane, k1s[buf][:], k1ap)
                    bi = 2 + fc2 % 2
                    P = P4[bi]
                    E_.wait("tensor", "v", pm_v[bi])
                    E_.wait("tensor", "t", g2_read_t)
                    tm = E_.mm(P, R.ones1[0:1, :], rrow[:],
                               start=True, stop=False)
                    E_.wait("tensor", lane, k1_d[buf])
                    for fc in range(FC):
                        tm = E_.mm(P, k1s[buf][:, fc * 128:(fc + 1) * 128],
                                   h1[:, fc * TT:(fc + 1) * TT],
                                   start=False, stop=(fc == FC - 1))
                    k1_t[buf] = tm
                    E_.wait("vector", "t", tm)
                    ve = E_.v("tensor_scalar", g2[:, fc2 * TT:(fc2 + 1) * TT],
                              P, scl[:, 1:2], 0.0, OP.mult, OP.max)
                    pm_v[bi] = ve
                    E_.wait("vector", "v", ve)
                g2_v = ve
                h1_read_t = tm
                E_.wait("tensor", "v", g2_v)
                E_.wait("tensor", "v", E_.c["v"])
                for fc in range(FC):
                    tm = E_.mm(R.PU4[0:1, 0:TT], R.onesc[:],
                               g2[:, fc * TT:(fc + 1) * TT],
                               start=(fc == 0), stop=(fc == FC - 1))
                E_.wait("vector", "t", tm)
                vr = E_.v("tensor_scalar", rrow[:], R.PU4[0:1, 0:TT], -0.5, None,
                          OP.mult, OP.bypass)
                E_.wait("tensor", "v", vr)
                ogt = [0] * 8
                for fc2 in range(FC):
                    buf = fc2 % 2
                    lane = "db" if buf == 0 else "dc"
                    E_.wait("sync", "t", k2_t[buf])
                    k2_d[buf] = E_.dma(lane, k2s[buf][:],
                                       chunk_ap(k2scr2, fc2, D))
                    E_.wait("tensor", lane, k2_d[buf])
                    for oc in range(DC):
                        P = P8[oc]
                        if fc2 == 0:
                            E_.wait("tensor", "v", pm_v[oc])
                            if oc == 6:
                                E_.wait("tensor", "v", vr)
                            E_.mm(P, R.ones1[0:1, :], rrow[:],
                                  start=True, stop=False)
                        tm = E_.mm(P, k2s[buf][:, oc * 128:(oc + 1) * 128],
                                   g2[:, fc2 * TT:(fc2 + 1) * TT],
                                   start=False, stop=(fc2 == FC - 1))
                        ogt[oc] = tm
                    k2_t[buf] = tm
                g2_read_t = tm
                for oc in range(DC):
                    E_.wait("vector", "t", ogt[oc])
                    ve = E_.v("tensor_scalar", R.y[:, oc * TT:(oc + 1) * TT],
                              P8[oc], scl[:, 2:3], 0.0, OP.mult, OP.bypass)
                    pm_v[oc] = ve
                    E_.wait("vector", "v", ve)
                E_.wait("scalar", "v", ve)
                E_.wait("tensor", "v", ve)
                vl, gd = R.finish_tile(E_, xout, T, tt, E_.c["s"], R.PU4, R.PU5,
                                       max(ogt[6], ogt[7]),
                                       max(pm_v[6], pm_v[7], ln_v))
                ln_v = vl
                ln_t = E_.c["t"]
            fin = E_.c["ga"]
            E_.wait("gpsimd", "ga", fin)
        run_phase(nc, sems, cnt, main)



PARTS = ("l0attn", "l0ffn", "l1attn", "l1ffn")


def build_program(parts, T):
    nc = bass.Bass("TRN2", target_bir_lowering=False)
    declared = {}

    def inp(name, shape, dt):
        if name not in declared:
            declared[name] = nc.dram_tensor(name, shape, dt,
                                            kind="ExternalInput").ap()
        return declared[name]

    xin = nc.dram_tensor("xT", [D, T], F32, kind="ExternalInput").ap()
    yout = nc.dram_tensor("yT", [D, T], F32, kind="ExternalOutput").ap()
    xs = [nc.dram_tensor("xs0", [D, T], F32).ap(),
          nc.dram_tensor("xs1", [D, T], F32).ap()]
    sems = {}
    cnt = {}
    with contextlib.ExitStack() as ctx:
        for s in tuple(LANES) + ("t", "v", "s"):
            sems[s] = ctx.enter_context(nc.semaphore("sem_" + s))
            cnt[s] = 0
        for i, part in enumerate(parts):
            src = xin if i == 0 else xs[(i - 1) % 2]
            dst = yout if i == len(parts) - 1 else xs[i % 2]
            if part == "l0attn":
                build_attn(nc, sems, cnt, inp, src, dst, T, 0)
            elif part == "l0ffn":
                build_l0ffn(nc, sems, cnt, inp, src, dst, T)
            elif part == "l1attn":
                build_attn(nc, sems, cnt, inp, src, dst, T, 1)
            elif part == "l1ffn":
                build_l1ffn(nc, sems, cnt, inp, src, dst, T)
            else:
                raise ValueError(part)
    return nc, set(declared.keys())


def host_consts():
    j = np.arange(512)
    r = np.arange(128)[:, None]
    c = np.arange(896)
    emask2 = (c[None, :] >= r + 384).astype(ml_dtypes.bfloat16)
    pmask = np.where(c[None, :] <= r + 384, 3.0e38, -3.0e38).astype(
        ml_dtypes.bfloat16)
    osel = np.zeros((16, NP * 128), np.float32)
    for p in range(NP):
        osel[2 * p, p * 128:p * 128 + 64] = 1.0
        osel[2 * p + 1, p * 128 + 64:p * 128 + 128] = 1.0
    return {"emask2": emask2, "pmask": pmask,
            "osel16": osel.astype(ml_dtypes.bfloat16)}


def host_weights(inputs):
    c = np.ascontiguousarray
    w = {
        "l0a_wq": c(np.asarray(inputs["L0_attn_Wq"], np.float32)),
        "l0a_wk": c(np.asarray(inputs["L0_attn_Wk"], np.float32)),
        "l0a_wv": c(np.asarray(inputs["L0_attn_Wv"], np.float32)),
        "l0a_wo": c(np.asarray(inputs["L0_attn_Wo"], np.float32)),
        "l0a_bq": c(np.asarray(inputs["L0_attn_bq"], np.float32)),
        "l0a_bk": c(np.asarray(inputs["L0_attn_bk"], np.float32)),
        "l0a_bv": c(np.asarray(inputs["L0_attn_bv"], np.float32)),
        "l0a_bo": c(np.asarray(inputs["L0_attn_bo"], np.float32)),
        "l0f_w1t": c(np.asarray(inputs["L0_ffn_W1"], np.float32).T),
        "l0f_w2t": c(np.asarray(inputs["L0_ffn_W2"], np.float32).T),
        "l0f_b1": c(np.asarray(inputs["L0_ffn_b1"], np.float32)),
        "l0f_b2": c(np.asarray(inputs["L0_ffn_b2"], np.float32)),
        "l1a_wqkvt": c(np.asarray(inputs["L1_attn_w_qkv"], np.float32).T),
        "l1a_wprojt": c(np.asarray(inputs["L1_attn_w_proj"], np.float32).T),
        "l1f_k0": c(np.asarray(inputs["L1_ffn_k0"])).view(np.uint8),
        "l1f_k1": c(np.asarray(inputs["L1_ffn_k1"])).view(np.uint8),
        "l1f_k2": c(np.asarray(inputs["L1_ffn_k2"])).view(np.uint8),
        "l1f_s0": np.asarray(inputs["L1_ffn_s0"], np.float32).reshape(1, 1),
        "l1f_s1": np.asarray(inputs["L1_ffn_s1"], np.float32).reshape(1, 1),
        "l1f_s2": np.asarray(inputs["L1_ffn_s2"], np.float32).reshape(1, 1),
    }
    w.update(host_consts())
    return w


def kernel(**inputs):
    T = 4096
    x = np.asarray(inputs["L0_attn_x"], np.float32)
    w = host_weights(inputs)
    nc, names = build_program(PARTS, T)
    in_maps = []
    for ci in range(NCORES):
        m = {k: v for k, v in w.items() if k in names}
        m["xT"] = np.ascontiguousarray(x[ci].T)
        in_maps.append(m)
    res = run_bass_kernel_spmd(nc, in_maps, core_ids=list(range(NCORES)))
    out = np.stack([np.asarray(res.results[ci]["yT"]).T for ci in range(NCORES)])
    return out.astype(np.float32)
